# version 14
# baseline (speedup 1.0000x reference)
import contextlib

import numpy as np
import ml_dtypes

import concourse.bass as bass
import concourse.mybir as mybir
from concourse.bass_utils import run_bass_kernel_spmd

F32 = mybir.dt.float32
BF16 = mybir.dt.bfloat16
F32R = mybir.dt.float32r
I32 = mybir.dt.int32
AF = mybir.ActivationFunctionType
ALU = mybir.AluOpType
AX = mybir.AxisListType

BF16NP = ml_dtypes.bfloat16

N_CORES = 8
EPS = 1e-5

SEM_NAMES = ("ld", "boot", "wa", "wb", "ra", "rb", "va", "vb", "mm", "ev", "dv", "stq", "cc",
             "gp")


class Emit:

    def __init__(self, env, eng, sems, base=None):
        self.env = env
        self.eng = eng
        self.sems = sems
        self.c = dict(base) if base else {k: 0 for k in sems}
        self.state = {}

    def on(self, name):
        return self.env == name

    def tag(self, ins, sem, n=1):
        v = self.c[sem] = self.c[sem] + n
        if ins is not None:
            ins.then_inc(self.sems[sem], n)
        return v

    def wait(self, env, sem, val):
        if self.env == env and val > 0:
            self.eng.wait_ge(self.sems[sem], val)

    def barrier(self):
        for s_ in self.sems:
            if self.c[s_] > 0:
                self.eng.wait_ge(self.sems[s_], self.c[s_])


def _ceil(a, b):
    return (a + b - 1) // b


_SECN = [0]


def _mk_alloc(nc, st):
    pfx = f"s{_SECN[0]}_"
    _SECN[0] += 1
    sb = lambda name, shape, dt: st.enter_context(nc.sbuf_tensor(pfx + name, shape, dt))
    psa = lambda name, shape, dt=F32: st.enter_context(nc.psum_tensor(pfx + name, shape, dt))
    return sb, psa


def emit_ln(p, h, xt, xln, nkc, mt, d_model, ps_stat, ps_bc, gb=None, xln2=None):
    inv_d = 1.0 / d_model
    for it in range(mt // 512):
        ts = slice(it * 512, (it + 1) * 512)
        p.wait("vector", "dv", p.state.get("ln_prev_dv", 0))
        dv = p.tag(p.eng.tensor_copy(h.cb[:], xt[:, :, ts]) if p.on("vector") else None, "dv")
        ev = p.tag(p.eng.activation(h.sqb[:], xt[:, :, ts], AF.Square)
                   if p.on("scalar") else None, "ev")
        p.wait("tensor", "dv", dv)
        p.wait("tensor", "ev", ev)
        mm = None
        for kc in range(nkc):
            if p.on("tensor"):
                p.eng.matmul(ps_stat[0:1, 0, :], h.ones_col[:, 0:1],
                             h.cb[:, kc, :], start=(kc == 0), stop=(kc == nkc - 1))
                mm = p.eng.matmul(ps_stat[0:1, 1, :], h.ones_col[:, 0:1],
                                  h.sqb[:, kc, :], start=(kc == 0), stop=(kc == nkc - 1))
        mmv = p.tag(mm, "mm")
        r_mu = h.row_f32[0:1, 0:512]
        r_var = h.row_f32[0:1, 512:1024]
        r_mu2 = h.row_f32[0:1, 1024:1536]
        r_var2 = h.row_f32[0:1, 1024:1536]
        r_std = h.row_f32[0:1, 512:1024]
        r_rstd = h.row_f32[0:1, 1536:2048]
        r_mr = h.row_f32[0:1, 0:512]
        p.wait("scalar", "mm", mmv)
        p.wait("scalar", "dv", p.state.get("ln_prev_dv", 0))
        if p.on("scalar"):
            p.eng.activation(r_mu, ps_stat[0:1, 0, :], AF.Copy, scale=inv_d)
        e1 = p.tag(p.eng.activation(r_var, ps_stat[0:1, 1, :], AF.Copy, scale=inv_d)
                   if p.on("scalar") else None, "ev")
        p.wait("scalar", "ev", e1)
        ev2 = p.tag(p.eng.activation(r_mu2, r_mu, AF.Square) if p.on("scalar") else None, "ev")
        p.wait("vector", "ev", ev2)
        dvv = p.tag(p.eng.tensor_sub(r_var2, r_var, r_mu2) if p.on("vector") else None, "dv")
        p.wait("scalar", "dv", dvv)
        ev3 = p.tag(p.eng.activation(r_std, r_var2, AF.Sqrt, bias=h.epsc[0:1, 0:1])
                    if p.on("scalar") else None, "ev")
        p.wait("vector", "ev", ev3)
        d2 = p.tag(p.eng.reciprocal(r_rstd, r_std) if p.on("vector") else None, "dv")
        p.wait("vector", "dv", d2)
        d3 = p.tag(p.eng.tensor_mul(r_mr, r_mu, r_rstd) if p.on("vector") else None, "dv")
        if p.on("vector"):
            p.eng.tensor_copy(h.row_bf[0:1, 0:512], r_rstd)
        p.wait("vector", "dv", d3)
        dv2 = p.tag(p.eng.tensor_copy(h.row_bf[0:1, 512:1024], r_mr)
                    if p.on("vector") else None, "dv")
        p.wait("tensor", "dv", dv2)
        mm2 = None
        if p.on("tensor"):
            p.eng.matmul(ps_bc[:, 0, :], h.ones_row[0:1, :], h.row_bf[0:1, 0:512],
                         start=True, stop=True)
            mm2 = p.eng.matmul(ps_bc[:, 1, :], h.ones_row[0:1, :], h.row_bf[0:1, 512:1024],
                               start=True, stop=True)
        mmv2 = p.tag(mm2, "mm")
        p.wait("vector", "mm", mmv2)
        d5 = p.tag(p.eng.tensor_mul(h.cb[:], xt[:, :, ts],
                                    ps_bc[:, 0:1, :].broadcast_to([128, nkc, 512]))
                   if p.on("vector") else None, "dv")
        p.wait("vector", "dv", d5)
        if gb is None:
            dv3 = p.tag(p.eng.tensor_sub(xln[:, :, ts], h.cb[:],
                                         ps_bc[:, 1:2, :].broadcast_to([128, nkc, 512]))
                        if p.on("vector") else None, "dv")
        else:
            d6 = p.tag(p.eng.tensor_sub(h.cb[:], h.cb[:],
                                        ps_bc[:, 1:2, :].broadcast_to([128, nkc, 512]))
                       if p.on("vector") else None, "dv")
            p.wait("vector", "dv", d6)
            dv3 = None
            for kc in range(nkc):
                dv3 = p.tag(p.eng.tensor_scalar(xln[:, kc, ts], h.cb[:, kc, :],
                                                gb[0][:, kc:kc + 1], gb[1][:, kc:kc + 1],
                                                ALU.mult, ALU.add)
                            if p.on("vector") else None, "dv")
        if xln2 is not None:
            p.wait("vector", "dv", dv3)
            dv3 = p.tag(p.eng.tensor_copy(xln2[:, :, ts], xln[:, :, ts])
                        if p.on("vector") else None, "dv")
        p.state["ln_prev_dv"] = dv3
        p.wait("tensor", "dv", dv3)


def build_ffn0(cfg, fctx=None):
    d_model, hid, n_e, tok, mt = cfg["D"], cfg["H"], cfg["E"], cfg["T"], cfg["MT"]
    nkc = d_model // 128
    nhc = hid // 128
    nh2 = hid // 512
    ntc = mt // 128
    nmt = tok // mt
    n512 = mt // 512
    ndh = _ceil(d_model, 512)
    dws = [min(512, d_model - i * 512) for i in range(ndh)]

    if fctx is None:
        nc = bass.Bass("TRN2", target_bir_lowering=False)
        x = nc.dram_tensor("x", [d_model, tok], F32, kind="ExternalInput").ap()
        rw1 = nc.dram_tensor("rw1", [d_model, hid], F32, kind="ExternalInput").ap()
        rb1 = nc.dram_tensor("rb1", [hid], F32, kind="ExternalInput").ap()
        rw2 = nc.dram_tensor("rw2", [hid, n_e], F32, kind="ExternalInput").ap()
        rb2 = nc.dram_tensor("rb2", [n_e], F32, kind="ExternalInput").ap()
        w1 = nc.dram_tensor("w1", [n_e, d_model, hid], F32, kind="ExternalInput").ap()
        b1 = nc.dram_tensor("b1", [n_e, hid], F32, kind="ExternalInput").ap()
        w2 = nc.dram_tensor("w2", [n_e, hid, d_model], F32, kind="ExternalInput").ap()
        b2 = nc.dram_tensor("b2", [n_e, d_model], F32, kind="ExternalInput").ap()
        ones_in = nc.dram_tensor("ones", [128, 128], F32, kind="ExternalInput").ap()
        idn_in = nc.dram_tensor("idn", [128, 128], F32, kind="ExternalInput").ap()
        y = nc.dram_tensor("y", [d_model, tok], F32, kind="ExternalOutput").ap()
        base = None
    else:
        nc, sems, base = fctx["nc"], fctx["sems"], fctx["base"]
        x = fctx["x"]; y = fctx["y"]; rw1 = fctx["rw1"]; rb1 = fctx["rb1"]
        rw2 = fctx["rw2"]; rb2 = fctx["rb2"]; w1 = fctx["w1"]; b1 = fctx["b1"]
        w2 = fctx["w2"]; b2 = fctx["b2"]; ones_in = fctx["ones"]; idn_in = fctx["idn"]

    st = contextlib.ExitStack()
    sb, psa = _mk_alloc(nc, st)

    class H:
        pass

    h = H()
    h.xt = sb("xt", [128, nkc, mt], F32)
    h.xln = sb("xln", [128, nkc, mt], BF16)
    h.cb = None
    h.sqb = None
    h.row_f32 = sb("row_f32", [1, 2048], F32)
    h.epsc = sb("epsc", [1, 1], F32)
    h.row_bf = sb("row_bf", [1, 1024], BF16)
    h.ones_col = sb("ones_col", [128, 1], BF16)
    h.ones_row = sb("ones_row", [1, 128], BF16)
    h.idn_f = sb("idn_f", [128, 128], F32)
    h.silu = sb("silu", [128, nhc, mt], BF16)
    _sf = h.silu[:].rearrange("p a b -> p (a b)")
    h.cb = _sf[:, 0:nkc * 512].rearrange("p (c w) -> p c w", w=512)
    h.sqb = _sf[:, nkc * 512:2 * nkc * 512].rearrange("p (c w) -> p c w", w=512)
    h.w1b = [sb(f"w1b{i}", [128, nkc, 512], BF16) for i in range(2)]
    h.w2r = [sb(f"w2r{i}", [128, 4, 512], BF16) for i in range(2)]
    h.rw2b = sb("rw2b", [128, nhc, n_e], BF16)
    h.g = [sb(f"g{i}", [128, mt], BF16) for i in range(2)]
    h.acc = sb("acc", [128, ntc, d_model], F32)
    h.we = sb("we", [128, ntc, n_e], F32)
    h.webf = sb("webf", [128, ntc, n_e], F32)
    h.weT = sb("weT", [n_e, mt], BF16)
    h.b1c = sb("b1c", [128, nhc], F32)
    h.rb1c = sb("rb1c", [128, nhc], F32)
    h.rb2c = sb("rb2c", [n_e, 1], F32)
    h.b2bf = sb("b2bf", [n_e, d_model], BF16)
    h.lg = sb("lg", [n_e, mt], F32)
    h.lgt = sb("lgt", [128, n_e], F32)
    h.lgm = sb("lgm", [128, 8 + 5 * n_e], F32)
    h.h2e = [sb(f"h2e{i}", [128, 512], BF16) for i in range(2)]

    PS = [psa(f"PS{i}", [128, 2, 512]) for i in range(3)]
    PSX = psa("PSX", [128, 2, 512])

    def psum_tc(tc):
        return (PS[tc // 2] if tc // 2 < 3 else PSX)[:, tc % 2, :]

    sem_cm = contextlib.ExitStack()
    if fctx is None:
        sems = {}
        for s_ in SEM_NAMES:
            sems[s_] = sem_cm.enter_context(nc.semaphore(s_))

    def program(p):
        ep0 = p.tag(p.eng.memset(h.epsc[:], float(EPS)) if p.on("vector") else None, "dv")
        p.wait("scalar", "dv", ep0)
        p.tag(p.eng.dma_start(out=h.ones_col[:], in_=ones_in[:, 0:1], allow_slow_non_contiguous=True) if p.on("gpsimd") else None,
              "boot", 16)
        p.tag(p.eng.dma_start(out=h.ones_row[:], in_=ones_in[0:1, :]) if p.on("gpsimd") else None,
              "boot", 16)
        p.tag(p.eng.dma_start(out=h.rw2b[:], in_=rw2.rearrange("(c p) e -> p c e", p=128),
                              allow_slow_non_contiguous=True)
              if p.on("gpsimd") else None, "boot", 16)
        bootv = p.tag(p.eng.dma_start(out=h.rw2b[:], in_=rw2.rearrange("(c p) e -> p c e", p=128))
              if False else None, "boot", 0)
        bootv = p.c["boot"]
        p.tag(p.eng.dma_start(out=h.idn_f[:], in_=idn_in[:]) if p.on("sync") else None, "ld", 16)
        p.tag(p.eng.dma_start(out=h.rb1c[:], in_=rb1.rearrange("(c p) -> p c", p=128),
                              allow_slow_non_contiguous=True)
              if p.on("sync") else None, "ld", 16)
        p.tag(p.eng.dma_start(out=h.rb2c[:, 0], in_=rb2[:], allow_slow_non_contiguous=True) if p.on("sync") else None, "ld", 16)
        ldv0 = p.tag(p.eng.dma_start(out=h.b2bf[:], in_=b2[:, :]) if p.on("gpsimd") else None,
                     "boot", 16)
        bootv = p.c["boot"]
        p.wait("scalar", "boot", bootv)
        p.wait("scalar", "ld", p.c["ld"])
        p.state["bootv"] = bootv
        evb2 = 0
        wait_w1_free = [0, 0]
        res_mm_prev = 0

        for mti in range(nmt):
            p.barrier()
            tsl = slice(mti * mt, (mti + 1) * mt)
            ldx = p.tag(p.eng.dma_start(out=h.xt[:],
                                        in_=x[:, tsl].rearrange("(c p) t -> p c t", p=128))
                        if p.on("sync") else None, "ld", 16)
            p.wait("vector", "ld", ldx)
            p.wait("scalar", "ld", ldx)
            p.wait("tensor", "boot", p.state["bootv"])
            p.wait("tensor", "ld", ldx)
            emit_ln(p, h, h.xt, h.xln, nkc, mt, d_model, PSX, PS[2])

            p.state[("w1ps", 0)] = p.state[("w1ps", 1)] = p.c["ev"]
            gfree = [0, 0]
            pend_g = []

            def flush_g():
                gbuf_, hidx_, evv_ = pend_g.pop(0)
                p.wait("tensor", "ev", evv_)
                mv = None
                for half in range(n512):
                    if p.on("tensor"):
                        mv = p.eng.matmul(PSX[0:n_e, half, :], h.rw2b[:, hidx_, :],
                                          h.g[gbuf_][:, half * 512:(half + 1) * 512],
                                          start=(hidx_ == 0), stop=(hidx_ == nhc - 1))
                gfree[gbuf_] = p.tag(mv, "mm")

            for hc2 in range(nh2):
                buf = hc2 % 2
                p.wait("gpsimd", "mm", wait_w1_free[buf])
                wsem = "wa" if buf == 0 else "wb"
                wv = p.tag(p.eng.dma_start(
                    out=h.w1b[buf][:],
                    in_=rw1.rearrange("(c p) h -> p c h", p=128)[:, :, hc2 * 512:(hc2 + 1) * 512])
                    if p.on("gpsimd") else None, wsem, 16)
                for hsub in range(4):
                    pa = PS[hsub % 2]
                    gbuf = hsub % 2
                    hidx = hc2 * 4 + hsub
                    p.wait("tensor", wsem, wv)
                    p.wait("tensor", "ev", p.state[("w1ps", hsub % 2)])
                    p.wait("tensor", "mm", gfree[gbuf])
                    mmv = None
                    for half in range(n512):
                        for kc in range(nkc):
                            if p.on("tensor"):
                                mmv = p.eng.matmul(
                                    pa[:, half, :],
                                    h.w1b[buf][:, kc, hsub * 128:(hsub + 1) * 128],
                                    h.xln[:, kc, half * 512:(half + 1) * 512],
                                    start=(kc == 0), stop=(kc == nkc - 1))
                    mmc = p.tag(mmv, "mm")
                    p.wait("scalar", "mm", mmc)
                    evv = p.tag(p.eng.activation(h.g[gbuf][:].rearrange("p (a b) -> p a b", a=n512),
                                                 pa[:, 0:n512, :], AF.Silu,
                                                 bias=h.rb1c[:, hidx:hidx + 1])
                                if p.on("scalar") else None, "ev")
                    p.state[("w1ps", hsub % 2)] = evv
                    pend_g.append((gbuf, hidx, evv))
                    if len(pend_g) > 1:
                        flush_g()
                wait_w1_free[buf] = p.c["mm"]
            while pend_g:
                flush_g()
            lgv = p.c["mm"]
            p.wait("scalar", "mm", lgv)
            evlg = p.tag(p.eng.activation(h.lg[:].rearrange("p (a b) -> p a b", a=n512),
                                          PSX[0:n_e, 0:n512, :], AF.Identity,
                                          bias=h.rb2c[:, 0:1]) if p.on("scalar") else None, "ev")
            for tcg in range(ntc):
                tgs = slice(tcg * 128, (tcg + 1) * 128)
                p.wait("tensor", "ev", evlg)
                p.wait("tensor", "dv", p.c["dv"])
                mt1 = p.tag(p.eng.transpose(PS[2][0:128, 0, 0:n_e], h.lg[:, tgs],
                                            h.idn_f[0:n_e, 0:n_e])
                            if p.on("tensor") else None, "mm")
                p.wait("scalar", "mm", mt1)
                evt = p.tag(p.eng.copy(h.lgt[:, :], PS[2][:, 0, 0:n_e])
                            if p.on("scalar") else None, "ev")
                p.wait("vector", "ev", evt)
                m1 = h.lgm[:, 0:1]
                m2 = h.lgm[:, 1:2]
                p1 = h.lgm[:, 2:3]
                dd = h.lgm[:, 3:4]
                p2 = h.lgm[:, 4:5]
                eq1 = h.lgm[:, 8:8 + n_e]
                eq2 = h.lgm[:, 8 + n_e:8 + 2 * n_e]
                eq3 = h.lgm[:, 8 + 2 * n_e:8 + 3 * n_e]
                wA = h.lgm[:, 8 + 3 * n_e:8 + 4 * n_e]
                wB = h.lgm[:, 8 + 4 * n_e:8 + 5 * n_e]
                da = p.tag(p.eng.reduce_max(m1, h.lgt[:], axis=AX.X)
                           if p.on("vector") else None, "dv")
                p.wait("vector", "dv", da)
                db = p.tag(p.eng.tensor_single_scalar(eq1, h.lgt[:], m1, ALU.is_equal)
                           if p.on("vector") else None, "dv")
                p.wait("vector", "dv", db)
                dc = p.tag(p.eng.scalar_tensor_tensor(eq2, eq1, -1e30, h.lgt[:], ALU.mult, ALU.add)
                           if p.on("vector") else None, "dv")
                p.wait("vector", "dv", dc)
                dm = p.tag(p.eng.reduce_max(m2, eq2, axis=AX.X)
                           if p.on("vector") else None, "dv")
                p.wait("vector", "dv", dm)
                if p.on("vector"):
                    p.eng.tensor_single_scalar(eq3, eq2, m2, ALU.is_equal)
                dv1 = p.tag(p.eng.tensor_sub(dd, m2, m1) if p.on("vector") else None, "dv")
                p.wait("scalar", "dv", dv1)
                if p.on("scalar"):
                    p.eng.activation(p1, dd, AF.Sigmoid, scale=-1.0)
                ev5 = p.tag(p.eng.activation(p2, dd, AF.Sigmoid) if p.on("scalar") else None, "ev")
                p.wait("vector", "ev", ev5)
                if p.on("vector"):
                    p.eng.tensor_scalar(wA, eq1, p1, 0.5, ALU.mult, ALU.mult)
                dww = p.tag(p.eng.tensor_scalar(wB, eq3, p2, 0.5, ALU.mult, ALU.mult)
                            if p.on("vector") else None, "dv")
                p.wait("vector", "dv", dww)
                dwe = p.tag(p.eng.tensor_add(h.we[:, tcg, :], wA, wB)
                            if p.on("vector") else None, "dv")
                p.wait("vector", "dv", dwe)
                dv2 = p.tag(p.eng.tensor_copy(h.webf[:, tcg, :], h.we[:, tcg, :])
                            if p.on("vector") else None, "dv")
                p.wait("tensor", "dv", dv2)
                mt2 = p.tag(p.eng.transpose(PS[2][0:n_e, 1, 0:128], h.webf[:, tcg, :],
                                            h.idn_f[:])
                            if p.on("tensor") else None, "mm")
                p.wait("scalar", "mm", mt2)
                p.tag(p.eng.activation(h.weT[:, tgs], PS[2][0:n_e, 1, 0:128], AF.Copy, scale=1.0)
                      if p.on("scalar") else None, "ev")
            prev_init = 0
            for tcg in range(ntc):
                pa = PS[tcg % 2]
                p.wait("tensor", "ev", p.c["ev"])
                p.wait("tensor", "mm", res_mm_prev)
                mv = None
                for half in range(ndh):
                    if p.on("tensor"):
                        mv = p.eng.matmul(pa[:, half, 0:dws[half]],
                                          h.weT[:, tcg * 128:(tcg + 1) * 128],
                                          h.b2bf[:, half * 512:half * 512 + dws[half]],
                                          start=True, stop=True)
                mmc = p.tag(mv, "mm")
                p.wait("scalar", "mm", mmc)
                p.wait("scalar", "ev", prev_init)
                prev_init = p.tag(
                    p.eng.copy(h.acc[:, tcg, :].rearrange("p (a b) -> p a b", a=ndh)
                               if ndh > 1 else h.acc[:, tcg, :],
                               pa[:, 0:ndh, 0:dws[0]] if ndh > 1 else pa[:, 0, 0:dws[0]])
                    if p.on("scalar") else None, "ev")
            p.state[("w1ps", 0)] = p.state[("w1ps", 1)] = p.c["ev"]

            h2efree = [0, 0]
            for e in range(n_e):
                p.wait("sync", "ev", p.c["ev"])
                ldb1 = p.tag(p.eng.dma_start(out=h.b1c[:],
                                             in_=b1[e, :].rearrange("(c p) -> p c", p=128),
                                             allow_slow_non_contiguous=True)
                             if p.on("sync") else None, "ld", 16)
                p.wait("sync", "ld", ldb1)
                p.wait("scalar", "ld", ldb1)
                for hc2 in range(nh2):
                    buf = hc2 % 2
                    p.wait("gpsimd", "mm", wait_w1_free[buf])
                    wsem = "wa" if buf == 0 else "wb"
                    wv = p.tag(p.eng.dma_start(
                        out=h.w1b[buf][:],
                        in_=w1[e].rearrange("(c p) h -> p c h", p=128)
                        [:, :, hc2 * 512:(hc2 + 1) * 512])
                        if p.on("gpsimd") else None, wsem, 16)
                    for hsub in range(4):
                        pa = PS[hsub % 2]
                        hidx = hc2 * 4 + hsub
                        p.wait("tensor", wsem, wv)
                        p.wait("tensor", "ev", p.state[("w1ps", hsub % 2)])
                        mmv = None
                        for half in range(n512):
                            for kc in range(nkc):
                                if p.on("tensor"):
                                    mmv = p.eng.matmul(
                                        pa[:, half, :],
                                        h.w1b[buf][:, kc, hsub * 128:(hsub + 1) * 128],
                                        h.xln[:, kc, half * 512:(half + 1) * 512],
                                        start=(kc == 0), stop=(kc == nkc - 1))
                        mmc = p.tag(mmv, "mm")
                        p.wait("scalar", "mm", mmc)
                        evsil = p.tag(p.eng.activation(h.silu[:, hidx, :].rearrange("p (a b) -> p a b", a=n512),
                                                       pa[:, 0:n512, :], AF.Silu,
                                                       bias=h.b1c[:, hidx:hidx + 1])
                                      if p.on("scalar") else None, "ev")
                        p.state[("w1ps", hsub % 2)] = evsil
                    wait_w1_free[buf] = p.c["mm"]
                silu_ready = p.c["ev"]
                psum_free_ev = [p.c["ev"]] * ntc
                for dh in range(ndh):
                    dsl = slice(dh * 512, dh * 512 + dws[dh])
                    for hg in range(nhc // 4):
                        slot = hg % 2
                        rsem = "ra" if slot == 0 else "rb"
                        p.wait("gpsimd", "mm", p.state.get(("ring", slot), 0))
                        wv = p.tag(p.eng.dma_start(
                            out=h.w2r[slot][:, :, 0:dws[dh]],
                            in_=w2[e].rearrange("(g p) d -> p g d", p=128)
                            [:, hg * 4:(hg + 1) * 4, dsl])
                            if p.on("gpsimd") else None, rsem, 16)
                        p.wait("tensor", rsem, wv)
                        if hg == 0:
                            p.wait("tensor", "ev", silu_ready)
                        mv = None
                        for tc in range(ntc):
                            pt = psum_tc(tc)
                            if hg == 0:
                                p.wait("tensor", "ev", psum_free_ev[tc])
                            for hl in range(4):
                                hidx = hg * 4 + hl
                                if p.on("tensor"):
                                    mv = p.eng.matmul(
                                        pt[:, 0:dws[dh]],
                                        h.silu[:, hidx, tc * 128:(tc + 1) * 128],
                                        h.w2r[slot][:, hl, 0:dws[dh]],
                                        start=(hidx == 0), stop=(hidx == nhc - 1))
                        p.state[("ring", slot)] = p.tag(mv, "mm")
                    h2done = p.c["mm"]
                    for tc in range(ntc):
                        pt = psum_tc(tc)
                        sbuf_i = tc % 2
                        p.wait("scalar", "mm", h2done)
                        p.wait("scalar", "dv", h2efree[sbuf_i])
                        evv = p.tag(p.eng.activation(h.h2e[sbuf_i][:, 0:dws[dh]], pt[:, 0:dws[dh]],
                                                     AF.Identity, scale=h.we[:, tc, e:e + 1])
                                    if p.on("scalar") else None, "ev")
                        psum_free_ev[tc] = evv
                        p.wait("vector", "ev", evv)
                        p.wait("vector", "dv", p.state.get(("acc", tc, dh), 0))
                        h2efree[sbuf_i] = p.tag(
                            p.eng.tensor_add(h.acc[:, tc, dsl], h.acc[:, tc, dsl],
                                             h.h2e[sbuf_i][:, 0:dws[dh]])
                            if p.on("vector") else None, "dv")
                        p.state[("acc", tc, dh)] = h2efree[sbuf_i]
                p.state[("w1ps", 0)] = max(p.state[("w1ps", 0)],
                                           psum_free_ev[0], psum_free_ev[1])
                p.state[("w1ps", 1)] = max(p.state[("w1ps", 1)],
                                           psum_free_ev[2], psum_free_ev[3])

            accdone = p.c["dv"]
            for tc in range(ntc):
                for kc in range(nkc):
                    pa = PS[(tc * nkc + kc) % 2]
                    p.wait("tensor", "dv", accdone if (tc, kc) == (0, 0) else p.c["dv"])
                    p.wait("tensor", "ev", p.c["ev"])
                    mt3 = p.tag(p.eng.transpose(pa[:, 0, 0:128],
                                                h.acc[:, tc, kc * 128:(kc + 1) * 128],
                                                h.idn_f[:])
                                if p.on("tensor") else None, "mm")
                    p.wait("vector", "mm", mt3)
                    p.tag(p.eng.tensor_add(h.xt[:, kc, tc * 128:(tc + 1) * 128],
                                           h.xt[:, kc, tc * 128:(tc + 1) * 128],
                                           pa[:, 0, 0:128])
                          if p.on("vector") else None, "dv")
            res_mm_prev = p.c["mm"]
            resdone = p.c["dv"]
            p.wait("sync", "dv", resdone)
            stv = p.tag(p.eng.dma_start(out=y[:, tsl].rearrange("(c p) t -> p c t", p=128),
                                        in_=h.xt[:])
                        if p.on("sync") else None, "stq", 16)
            p.wait("sync", "stq", stv)
        p.wait("sync", "stq", p.c["stq"])

    _mk_prog(nc, sems, st, base)(program)
    sem_cm.close()
    st.close()
    return nc





def _mk_prog(nc, sems, st, base=None):
    def run(program):
        final = {}

        def go(env, eng):
            p = Emit(env, eng, sems, base)
            if base is not None:
                p.barrier()
            program(p)
            final.clear()
            final.update(p.c)

        with nc.Block() as block:
            @block.sync
            def _(eng):
                go("sync", eng)

            @block.gpsimd
            def _(eng):
                go("gpsimd", eng)

            @block.tensor
            def _(eng):
                go("tensor", eng)

            @block.scalar
            def _(eng):
                go("scalar", eng)

            @block.vector
            def _(eng):
                go("vector", eng)
        if base is not None:
            base.clear()
            base.update(final)
    return run


def _alloc_common(nc, st, h, idn=True):
    sb, _psa_unused = _mk_alloc(nc, st)
    h.row_f32 = sb("row_f32", [1, 2048], F32)
    h.row_bf = sb("row_bf", [1, 1024], BF16)
    h.ones_col = sb("ones_col", [128, 1], BF16)
    h.ones_row = sb("ones_row", [1, 128], BF16)
    h.idn_f = sb("idn_f", [128, 128], F32) if idn else None
    h.epsc = sb("epsc", [1, 1], F32)
    return sb


def _common_boot(p, h, ones_in, idn_in):
    ep0 = p.tag(p.eng.memset(h.epsc[:], float(EPS)) if p.on("vector") else None, "dv")
    p.wait("scalar", "dv", ep0)
    p.tag(p.eng.dma_start(out=h.ones_col[:], in_=ones_in[:, 0:1],
                          allow_slow_non_contiguous=True) if p.on("gpsimd") else None, "boot", 16)
    p.tag(p.eng.dma_start(out=h.ones_row[:], in_=ones_in[0:1, :]) if p.on("gpsimd") else None,
          "boot", 16)
    if h.idn_f is not None:
        p.tag(p.eng.dma_start(out=h.idn_f[:], in_=idn_in[:]) if p.on("sync") else None,
              "ld", 16)


def build_kv(tok=2048, d_model=1024, fctx=None):
    nkc = d_model // 128
    if fctx is None:
        nc = bass.Bass("TRN2", target_bir_lowering=False)
        x = nc.dram_tensor("x", [d_model, tok], F32, kind="ExternalInput").ap()
        wqk = nc.dram_tensor("wqk", [d_model, 128], F32, kind="ExternalInput").ap()
        bqk = nc.dram_tensor("bqk", [128], F32, kind="ExternalInput").ap()
        wv = nc.dram_tensor("wv", [d_model, d_model], F32, kind="ExternalInput").ap()
        bv = nc.dram_tensor("bv", [d_model], F32, kind="ExternalInput").ap()
        ones_in = nc.dram_tensor("ones", [128, 128], F32, kind="ExternalInput").ap()
        idn_in = nc.dram_tensor("idn", [128, 128], F32, kind="ExternalInput").ap()
        qk_o = nc.dram_tensor("qk", [128, tok], F32, kind="ExternalOutput").ap()
        v_o = nc.dram_tensor("v", [tok, d_model], BF16, kind="ExternalOutput").ap()
        base = None
    else:
        nc, sems, base = fctx["nc"], fctx["sems"], fctx["base"]
        x = fctx["x"]; wqk = fctx["wqk"]; bqk = fctx["bqk"]; wv = fctx["wv"]
        bv = fctx["bv"]; ones_in = fctx["ones"]; idn_in = fctx["idn"]
        qk_o = fctx["qk_o"]; v_o = fctx["v_o"]

    st = contextlib.ExitStack()

    class H:
        pass

    h = H()
    sb = _alloc_common(nc, st, h)
    _, psa = _mk_alloc(nc, st)
    h.xt = sb("xt", [128, nkc, tok], F32)
    h.xln = sb("xln", [128, nkc, tok], BF16)
    h.cb = sb("cb", [128, nkc, 512], BF16)
    h.sqb = sb("sqb", [128, nkc, 512], BF16)
    h.wqkb = sb("wqkb", [128, nkc, 128], BF16)
    h.wvb = [sb(f"wvb{i}", [128, nkc, 128], BF16) for i in range(2)]
    h.bqkc = sb("bqkc", [128, 1], F32)
    h.bvc = sb("bvc", [128, nkc], F32)
    h.qk = sb("qk_s", [128, tok], F32)
    h.idn_bf = sb("idn_bf", [128, 128], BF16)
    h.vfm = sb("vfm", [128, nkc, tok], BF16)
    h.vst = sb("vst", [128, d_model], BF16)
    PS0 = psa("PS0", [128, 2, 512])
    PSX = psa("PSX", [128, 2, 512])
    PS2 = psa("PS2", [128, 2, 512])
    PBT = psa("PBT", [128, 1024], BF16)
    sem_cm = contextlib.ExitStack()
    if fctx is None:
        sems = {}
        for s_ in SEM_NAMES:
            sems[s_] = sem_cm.enter_context(nc.semaphore(s_))

    def program(p):
        _common_boot(p, h, ones_in, idn_in)
        p.tag(p.eng.dma_start(out=h.wqkb[:], in_=wqk.rearrange("(c p) e -> p c e", p=128))
              if p.on("gpsimd") else None, "boot", 16)
        p.tag(p.eng.dma_start(out=h.idn_bf[:], in_=idn_in[:]) if p.on("gpsimd") else None,
              "boot", 16)
        bootv = p.c["boot"]
        p.tag(p.eng.dma_start(out=h.bqkc[:, 0], in_=bqk[:]) if p.on("sync") else None, "ld", 16)
        p.tag(p.eng.dma_start(out=h.bvc[:], in_=bv.rearrange("(c p) -> p c", p=128),
                              allow_slow_non_contiguous=True)
              if p.on("sync") else None, "ld", 16)
        ldx = p.tag(p.eng.dma_start(out=h.xt[:], in_=x.rearrange("(c p) t -> p c t", p=128))
                    if p.on("sync") else None, "ld", 16)
        p.barrier()
        emit_ln(p, h, h.xt, h.xln, nkc, tok, d_model, PSX, PS2)
        nt = tok // 512
        for tq in range(nt):
            pa = PS0[:, tq % 2:tq % 2 + 1, :]
            p.wait("tensor", "ev", p.c["ev"])
            mv = None
            for kc in range(nkc):
                if p.on("tensor"):
                    mv = p.eng.matmul(pa[:, 0, :], h.wqkb[:, kc, :],
                                      h.xln[:, kc, tq * 512:(tq + 1) * 512],
                                      start=(kc == 0), stop=(kc == nkc - 1))
            mmc = p.tag(mv, "mm")
            p.wait("scalar", "mm", mmc)
            p.tag(p.eng.activation(h.qk[:, tq * 512:(tq + 1) * 512], pa[:, 0, :], AF.Silu,
                                   bias=h.bqkc[:, 0:1]) if p.on("scalar") else None, "ev")
        qkdone = p.c["ev"]
        p.wait("sync", "ev", qkdone)
        p.tag(p.eng.dma_start(out=qk_o[:], in_=h.qk[:]) if p.on("sync") else None, "stq", 16)
        for vc in range(nkc):
            buf = vc % 2
            p.wait("gpsimd", "mm", p.state.get(("wv", buf), 0))
            wsem = "wa" if buf == 0 else "wb"
            wv_ = p.tag(p.eng.dma_start(
                out=h.wvb[buf][:],
                in_=wv.rearrange("(c p) e -> p c e", p=128)[:, :, vc * 128:(vc + 1) * 128])
                if p.on("gpsimd") else None, wsem, 16)
            mv = None
            for tq in range(nt):
                pa = PS0[:, (vc * nt + tq) % 2:(vc * nt + tq) % 2 + 1, :]
                p.wait("tensor", wsem, wv_)
                p.wait("tensor", "ev", p.c["ev"])
                for kc in range(nkc):
                    if p.on("tensor"):
                        mv = p.eng.matmul(pa[:, 0, :], h.wvb[buf][:, kc, :],
                                          h.xln[:, kc, tq * 512:(tq + 1) * 512],
                                          start=(kc == 0), stop=(kc == nkc - 1))
                mmc = p.tag(mv, "mm")
                p.wait("scalar", "mm", mmc)
                p.tag(p.eng.activation(h.vfm[:, vc, tq * 512:(tq + 1) * 512], pa[:, 0, :],
                                       AF.Silu, bias=h.bvc[:, vc:vc + 1])
                      if p.on("scalar") else None, "ev")
            p.state[("wv", buf)] = p.c["mm"]
        vdone = p.c["ev"]
        for tc in range(tok // 128):
            for vc in range(nkc):
                pslot = PBT[:, (vc % 2) * 128:(vc % 2) * 128 + 128]
                p.wait("tensor", "ev", vdone if (tc, vc) == (0, 0) else p.c["ev"])
                mt1 = p.tag(p.eng.transpose(pslot, h.vfm[:, vc, tc * 128:(tc + 1) * 128],
                                            h.idn_bf[:])
                            if p.on("tensor") else None, "mm")
                p.wait("scalar", "mm", mt1)
                p.wait("scalar", "stq", p.state.get("vstq", 0))
                p.tag(p.eng.copy(h.vst[:, vc * 128:(vc + 1) * 128], pslot)
                      if p.on("scalar") else None, "ev")
            p.wait("sync", "ev", p.c["ev"])
            stv = p.tag(p.eng.dma_start(out=v_o[tc * 128:(tc + 1) * 128, :], in_=h.vst[:])
                        if p.on("sync") else None, "stq", 16)
            p.wait("sync", "stq", stv)
            p.state["vstq"] = stv
        p.wait("sync", "stq", p.c["stq"])

    _mk_prog(nc, sems, st, base)(program)
    sem_cm.close()
    st.close()
    return nc


ERF_A = 0.7978845608028654
ERF_B = -0.5641895835477563
SQ128 = 11.313708498984761
NEG_BIG = -30.0
ERF_FN = AF.Erf


def build_attn(tok=2048, d_model=1024, seq=8192, qt=256, fctx=None):
    nkc = d_model // 128
    nqi = tok // qt
    if fctx is None:
        nc = bass.Bass("TRN2", target_bir_lowering=False)
        x = nc.dram_tensor("x", [d_model, tok], F32, kind="ExternalInput").ap()
        qk_own = nc.dram_tensor("qk_own", [128, tok], F32, kind="ExternalInput").ap()
        qk_full = nc.dram_tensor("qk_full", [128, seq], F32, kind="ExternalInput").ap()
        v_full = nc.dram_tensor("v_full", [seq, d_model], BF16, kind="ExternalInput").ap()
        v_own = nc.dram_tensor("v_own", [tok, d_model], BF16, kind="ExternalInput").ap()
        gamma = nc.dram_tensor("gamma", [2, 128], F32, kind="ExternalInput").ap()
        beta = nc.dram_tensor("beta", [2, 128], F32, kind="ExternalInput").ap()
        rel_emb = nc.dram_tensor("rel_emb", [32, 1], F32, kind="ExternalInput").ap()
        sel1 = nc.dram_tensor("sel1", [1, 512], F32, kind="ExternalInput").ap()
        sel2 = nc.dram_tensor("sel2", [1, 512], F32, kind="ExternalInput").ap()
        sel128 = nc.dram_tensor("sel128", [1, 512], F32, kind="ExternalInput").ap()
        nbmask = nc.dram_tensor("nbmask", [3, 32, 128, 256], BF16, kind="ExternalInput").ap()
        mask_d0 = nc.dram_tensor("mask_d0", [128, 256], BF16, kind="ExternalInput").ap()
        mask_m128 = nc.dram_tensor("mask_m128", [128, 256], BF16, kind="ExternalInput").ap()
        ones_in = nc.dram_tensor("ones", [128, 128], F32, kind="ExternalInput").ap()
        y = nc.dram_tensor("y", [d_model, tok], F32, kind="ExternalOutput").ap()
        base = None
        qkg = vg = qk_cc = v_cc = None
    else:
        nc, sems, base = fctx["nc"], fctx["sems"], fctx["base"]
        x = fctx["x"]; qk_own = fctx["qk_own"]; v_own = fctx["v_own"]
        qk_full = None; qkg = fctx["qkg"]; vg = fctx["vg"]; v_full = fctx["vg"]
        qk_cc = fctx["qk_own"]; v_cc = fctx["v_own"]
        gamma = fctx["gamma"]; beta = fctx["beta"]; rel_emb = fctx["rel_emb"]
        sel1 = fctx["sel1"]; sel2 = fctx["sel2"]; sel128 = fctx["sel128"]
        nbmask = fctx["nbmask"]; mask_d0 = fctx["mask_d0"]; mask_m128 = fctx["mask_m128"]
        ones_in = fctx["ones"]; y = fctx["x"]

    st = contextlib.ExitStack()
    sb, psa = _mk_alloc(nc, st)

    class H:
        pass

    h = H()
    h.xt = sb("xt", [128, nkc, tok], F32)
    h.ones_row = sb("ones_row", [1, 128], BF16)
    h.q = sb("q", [128, tok], BF16)
    h.k = sb("k", [128, seq], BF16)
    h.kown = sb("kown", [128, tok], BF16)
    h.gb = sb("gb", [128, 4], F32)
    h.relrow = sb("relrow", [1, 32], F32)
    h.relbc = sb("relbc", [128, 32], F32)
    h.relbc2 = sb("relbc2", [128, 32], F32)
    h.selbf = sb("selbf", [1, 3 * 512], BF16)
    h.nbm = [sb(f"nbm{i}", [128, 256], BF16) for i in range(2)]
    h.nb = sb("nb", [128, 3 * 256 + 512], F32)
    h.bias_cols = sb("bias_cols", [128, 512], F32)
    h.s128_cols = sb("s128_cols", [128, 512], F32)
    h.masks = sb("masks", [128, 2 * 256], BF16)
    h.vb = [sb(f"vb{i}", [128, d_model], BF16) for i in range(5)]
    h.tts = [sb(f"tts{i}", [128, 256], F32) for i in range(4)]
    h.erf = [sb(f"erf{i}", [128, 256], F32) for i in range(4)]
    h.attn = [sb(f"attn{i}", [128, 256], BF16) for i in range(4)]
    h.attst = sb("attst", [128, nkc, 256], BF16)
    h.bcol = sb("bcol", [128, 1], F32)

    POUT = psa("POUT", [128, 8, 256])
    PSIM = psa("PSIM", [128, 4, 256])
    PB = psa("PB", [128, 2, 512])
    sem_cm = contextlib.ExitStack()
    if fctx is None:
        sems = {}
        for s_ in SEM_NAMES:
            sems[s_] = sem_cm.enter_context(nc.semaphore(s_))

    def program(p):
        b0 = p.tag(p.eng.memset(h.bcol[:], ERF_B) if p.on("vector") else None, "dv")
        p.tag(p.eng.dma_start(out=h.ones_row[:], in_=ones_in[0:1, :]) if p.on("gpsimd") else None,
              "boot", 16)
        p.tag(p.eng.dma_start(out=h.q[:], in_=qk_own[:]) if p.on("gpsimd") else None, "boot", 16)
        p.tag(p.eng.dma_start(out=h.kown[:], in_=qk_own[:]) if p.on("gpsimd") else None,
              "boot", 16)
        if qk_full is not None:
            p.tag(p.eng.dma_start(out=h.k[:], in_=qk_full[:]) if p.on("gpsimd") else None,
                  "boot", 16)
        else:
            ccv = None
            if p.on("gpsimd"):
                p.eng.collective_compute("AllGather", ALU.bypass,
                                         replica_groups=[[0, 1, 2, 3], [4, 5, 6, 7]],
                                         ins=[qk_cc.opt()], outs=[qkg.opt()]).then_inc(
                                             p.sems["cc"])
                ccv = p.eng.collective_compute(
                    "AllGather", ALU.bypass,
                    replica_groups=[[0, 1, 2, 3], [4, 5, 6, 7]],
                    ins=[v_cc.opt()], outs=[vg.opt()])
            p.c["cc"] += 1
            ccv2 = p.tag(ccv, "cc")
            p.wait("gpsimd", "cc", ccv2)
            for r in range(4):
                p.tag(p.eng.dma_start(out=h.k[:, r * tok:(r + 1) * tok],
                                      in_=qkg[r * 128:(r + 1) * 128, :])
                      if p.on("gpsimd") else None, "boot", 16)
        p.tag(p.eng.dma_start(out=h.selbf[0:1, 0:512], in_=sel1[:]) if p.on("gpsimd") else None,
              "boot", 16)
        p.tag(p.eng.dma_start(out=h.selbf[0:1, 512:1024], in_=sel2[:])
              if p.on("gpsimd") else None, "boot", 16)
        p.tag(p.eng.dma_start(out=h.selbf[0:1, 1024:1536], in_=sel128[:])
              if p.on("gpsimd") else None, "boot", 16)
        p.tag(p.eng.dma_start(out=h.masks[:, 0:256], in_=mask_d0[:]) if p.on("gpsimd") else None,
              "boot", 16)
        p.tag(p.eng.dma_start(out=h.masks[:, 256:512], in_=mask_m128[:])
              if p.on("gpsimd") else None, "boot", 16)
        p.tag(p.eng.dma_start(out=h.gb[:, 0:2], in_=gamma.rearrange("g p -> p g"),
                              allow_slow_non_contiguous=True)
              if p.on("sync") else None, "ld", 16)
        p.tag(p.eng.dma_start(out=h.gb[:, 2:4], in_=beta.rearrange("g p -> p g"),
                              allow_slow_non_contiguous=True)
              if p.on("sync") else None, "ld", 16)
        p.tag(p.eng.dma_start(out=h.relrow[:], in_=rel_emb.rearrange("a b -> b a"),
                              allow_slow_non_contiguous=True)
              if p.on("sync") else None, "ld", 16)
        ldx = p.tag(p.eng.dma_start(out=h.xt[:], in_=x.rearrange("(c p) t -> p c t", p=128))
                    if p.on("sync") else None, "ld", 16)
        p.barrier()
        d1 = p.tag(p.eng.tensor_scalar(h.q[:], h.q[:], h.gb[:, 0:1], h.gb[:, 2:3],
                                       ALU.mult, ALU.add) if p.on("vector") else None, "dv")
        d2 = p.tag(p.eng.tensor_scalar(h.kown[:], h.kown[:], h.gb[:, 1:2], h.gb[:, 3:4],
                                       ALU.mult, ALU.add) if p.on("vector") else None, "dv")
        d3 = p.tag(p.eng.tensor_scalar(h.k[:], h.k[:], h.gb[:, 1:2], h.gb[:, 3:4],
                                       ALU.mult, ALU.add) if p.on("vector") else None, "dv")
        dcast = p.tag(p.eng.tensor_copy(h.nb[0:1, 3 * 256:3 * 256 + 32].bitcast(BF16)[0:1, 0:32],
                                        h.relrow[:]) if p.on("vector") else None, "dv")
        p.wait("tensor", "dv", dcast)
        mmr = p.tag(p.eng.matmul(PB[:, 0, 0:32], h.ones_row[0:1, :],
                                 h.nb[0:1, 3 * 256:3 * 256 + 32].bitcast(BF16)[0:1, 0:32],
                                 start=True, stop=True) if p.on("tensor") else None, "mm")
        p.wait("scalar", "mm", mmr)
        e1 = p.tag(p.eng.activation(h.relbc[:], PB[:, 0, 0:32], AF.Copy, scale=SQ128 * float(seq))
                   if p.on("scalar") else None, "ev")
        e2 = p.tag(p.eng.activation(h.relbc2[:], PB[:, 0, 0:32], AF.Copy, scale=ERF_A * SQ128)
                   if p.on("scalar") else None, "ev")
        p.wait("vector", "ev", e1)
        for ti in range(3):
            dst = h.nb[:, ti * 256:(ti + 1) * 256]
            z0 = p.tag(p.eng.memset(dst, 0.0) if p.on("vector") else None, "dv")
            p.wait("vector", "dv", z0)
            for bkt in range(32):
                slot = bkt % 2
                p.wait("gpsimd", "dv", p.state.get(("nbm", slot), 0))
                lv = p.tag(p.eng.dma_start(out=h.nbm[slot][:], in_=nbmask[ti, bkt])
                           if p.on("gpsimd") else None, "wa" if slot == 0 else "wb", 16)
                p.wait("vector", "wa" if slot == 0 else "wb", lv)
                dacc = p.tag(p.eng.scalar_tensor_tensor(dst, h.nbm[slot][:],
                                                        h.relbc[:, bkt:bkt + 1], dst,
                                                        ALU.mult, ALU.add)
                             if p.on("vector") else None, "dv")
                p.state[("nbm", slot)] = dacc
                p.wait("vector", "dv", dacc)
        selv = h.selbf
        p.wait("tensor", "ev", e2)
        mmb = None
        if p.on("tensor"):
            p.eng.matmul(PB[:, 0, :], h.ones_row[0:1, :], selv[0:1, 0:512],
                         start=True, stop=True)
            mmb = p.eng.matmul(PB[:, 1, :], h.ones_row[0:1, :], selv[0:1, 512:1024],
                               start=True, stop=True)
        mmbv = p.tag(mmb, "mm")
        p.wait("vector", "ev", e2)
        dcol = p.tag(p.eng.tensor_scalar(h.nb[:, 3 * 256 + 64:3 * 256 + 65],
                                         h.relbc2[:, 31:32], 1.0, ERF_B, ALU.mult, ALU.add)
                     if p.on("vector") else None, "dv")
        p.wait("vector", "mm", mmbv)
        p.wait("vector", "dv", dcol)
        dbc = p.tag(p.eng.tensor_scalar(h.bias_cols[:], PB[:, 0, :],
                                        h.nb[:, 3 * 256 + 64:3 * 256 + 65], None, ALU.mult)
                    if p.on("vector") else None, "dv")
        p.wait("vector", "dv", dbc)
        dbc2 = p.tag(p.eng.scalar_tensor_tensor(h.bias_cols[:], PB[:, 1, :], NEG_BIG,
                                                h.bias_cols[:], ALU.mult, ALU.add)
                     if p.on("vector") else None, "dv")
        p.wait("tensor", "dv", dbc2)
        mm128 = p.tag(p.eng.matmul(PB[:, 0, :], h.ones_row[0:1, :], selv[0:1, 1024:1536],
                                   start=True, stop=True) if p.on("tensor") else None, "mm")
        p.wait("vector", "mm", mm128)
        d128 = p.tag(p.eng.tensor_copy(h.s128_cols[:], PB[:, 0, :])
                     if p.on("vector") else None, "dv")
        p.wait("vector", "dv", d128)
        dbc3 = p.tag(p.eng.scalar_tensor_tensor(h.bias_cols[:], h.s128_cols[:], ERF_B,
                                                h.bias_cols[:], ALU.mult, ALU.add)
                     if p.on("vector") else None, "dv")
        p.barrier()

        def sim_part(qi, kt_glob, src_k, src_v, kti, nbti, with_mask, sel128_col):
            qs = slice(qi * qt, (qi + 1) * qt)
            sidx = p.state.get("sidx", 0)
            p.state["sidx"] = (sidx + 1) % 4
            sslot = (0, 2, 1, 3)[sidx]
            vslot = p.state.get("vslot", 0)
            p.state["vslot"] = (vslot + 1) % 5
            aslot = p.state.get("aslot", 0)
            p.state["aslot"] = (aslot + 1) % 4
            vsem = ("va", "vb", "ra", "rb", "wa")[vslot]
            qeng = "sync" if vslot % 2 == 0 else "gpsimd"
            p.wait(qeng, "mm", p.state.get(("vfree", vslot), 0))
            lv = p.tag(p.eng.dma_start(out=h.vb[vslot][:], in_=src_v[kti * 128:(kti + 1) * 128, :])
                       if p.on(qeng) else None, vsem, 16)
            for s_ in (sslot, sslot ^ 1):
                skind, sval = p.state.get(("simfree", s_), ("ev", 0))
                p.wait("tensor", skind, sval)
            ms = p.tag(p.eng.matmul(PSIM[:, sslot, :], src_k[:, kti * 128:(kti + 1) * 128],
                                    h.q[:, qs], start=True, stop=True)
                       if p.on("tensor") else None, "mm")
            if nbti is not None or sel128_col is not None:
                p.wait("vector", "mm", ms)
                p.wait("vector", "ev", p.state.get(("tts_ev", aslot), 0))
                tsum = h.tts[aslot]
                if nbti is not None:
                    dn = p.tag(p.eng.tensor_add(tsum[:], PSIM[:, sslot, :],
                                                h.nb[:, nbti * 256:(nbti + 1) * 256])
                               if p.on("vector") else None, "dv")
                else:
                    dn = p.tag(p.eng.scalar_tensor_tensor(tsum[:], h.nb[:, 0:256], sel128_col,
                                                          PSIM[:, sslot, :], ALU.mult, ALU.add)
                               if p.on("vector") else None, "dv")
                p.state[("simfree", sslot)] = ("dv", dn)
                p.wait("scalar", "dv", dn)
                erf_in = tsum[:]
            else:
                p.wait("scalar", "mm", ms)
                erf_in = PSIM[:, sslot, :]
            p.wait("scalar", "dv", p.state.get(("erf_dv", aslot), 0))
            bias_ap = h.bcol[:, 0:1] if (nbti is not None) else \
                h.bias_cols[:, qi * 64 + kt_glob:qi * 64 + kt_glob + 1]
            ee = p.tag(p.eng.activation(h.erf[aslot][:], erf_in, ERF_FN,
                                        scale=ERF_A / float(seq), bias=bias_ap)
                       if p.on("scalar") else None, "ev")
            if nbti is None and sel128_col is None:
                p.state[("simfree", sslot)] = ("ev", ee)
            p.state[("tts_ev", aslot)] = ee
            p.wait("vector", "ev", ee)
            p.wait("vector", "mm", p.state.get(("attnfree", aslot), 0))
            if with_mask is not None:
                da = p.tag(p.eng.scalar_tensor_tensor(h.attn[aslot][:], h.erf[aslot][:], 1.0,
                                                      with_mask, ALU.add, ALU.mult)
                           if p.on("vector") else None, "dv")
            else:
                da = p.tag(p.eng.tensor_scalar(h.attn[aslot][:], h.erf[aslot][:], 1.0, None,
                                               ALU.add)
                           if p.on("vector") else None, "dv")
            p.state[("erf_dv", aslot)] = da
            return dict(vslot=vslot, vsem=vsem, lv=lv, aslot=aslot, da=da)

        def av_part(t, first, last):
            p.wait("tensor", "dv", t["da"])
            p.wait("tensor", t["vsem"], t["lv"])
            if first:
                p.wait("tensor", "ev", p.state.get("poutfree", 0))
            mv = None
            for vc in range(8):
                if p.on("tensor"):
                    mv = p.eng.matmul(POUT[:, vc, :],
                                      h.vb[t["vslot"]][:, vc * 128:(vc + 1) * 128],
                                      h.attn[t["aslot"]][:],
                                      start=(first and vc % 2 == 0),
                                      stop=(last and vc % 2 == 1))
            mmv = p.tag(mv, "mm")
            p.state[("vfree", t["vslot"])] = mmv
            p.state[("attnfree", t["aslot"])] = mmv
            return mmv

        LAG = 2
        for qi in range(nqi):
            KT = 2 * (24 + qi) + 2
            n_diag = 2 if qi == 0 else 3
            total_tiles = KT + n_diag
            tiles = []
            for kt in range(KT):
                s128col = h.s128_cols[:, qi * 64 + kt:qi * 64 + kt + 1] if qi == 0 else None
                tiles.append((qi, kt, h.k, v_full, kt, None, None, s128col))
            diag = []
            if qi > 0:
                diag.append((2 * qi - 1, 0, None))
            diag.append((2 * qi, 1, h.masks[:, 0:256]))
            diag.append((2 * qi + 1, 2, h.masks[:, 256:512]))
            for (ktl, nbti, msk) in diag:
                tiles.append((qi, 63, h.kown, v_own, ktl, nbti, msk, None))
            pipe = []
            done = 0
            for targs in tiles:
                pipe.append(sim_part(*targs))
                if len(pipe) > LAG:
                    av_part(pipe.pop(0), first=(done == 0), last=(done == total_tiles - 1))
                    done += 1
            while pipe:
                av_part(pipe.pop(0), first=(done == 0), last=(done == total_tiles - 1))
                done += 1
            outdone = p.c["mm"]
            for vc in range(8):
                p.wait("scalar", "mm", outdone)
                p.wait("scalar", "dv", p.state.get(("attadd", vc), 0))
                evv = p.tag(p.eng.activation(h.attst[:, vc, :], POUT[:, vc, :],
                                             AF.Copy, scale=0.5)
                            if p.on("scalar") else None, "ev")
                p.wait("vector", "ev", evv)
                p.state[("attadd", vc)] = p.tag(
                    p.eng.tensor_add(h.xt[:, vc, qi * qt:(qi + 1) * qt],
                                     h.xt[:, vc, qi * qt:(qi + 1) * qt], h.attst[:, vc, :])
                    if p.on("vector") else None, "dv")
            p.state["poutfree"] = p.c["ev"]
        p.wait("sync", "dv", p.c["dv"])
        stv = p.tag(p.eng.dma_start(out=y.rearrange("(c p) t -> p c t", p=128), in_=h.xt[:])
                    if p.on("sync") else None, "stq", 16)
        p.wait("sync", "stq", stv)

    _mk_prog(nc, sems, st, base)(program)
    sem_cm.close()
    st.close()
    return nc


def emit_mx_quant(p, scr, src, out_bf, width):
    iscr, f1, f2 = scr["iscr"], scr["f1"], scr["f2"]
    b1_, b2_, b3_ = scr["b1"], scr["b2"], scr["b3"]
    mxs, mxi = scr["mxs"], scr["mxi"]
    nb = width // 32
    v3 = src.rearrange("p (a b) -> p a b", b=32)
    dmx = p.tag(p.eng.reduce_max(mxs[:, 0:nb], v3, axis=AX.X, apply_absolute_value=True)
                if p.on("vector") else None, "dv")
    p.wait("vector", "dv", dmx)
    ds = p.tag(p.eng.tensor_scalar(mxs[:, 0:nb], mxs[:, 0:nb], 1.0 / 7.5, 1e-30,
                                   ALU.mult, ALU.max) if p.on("vector") else None, "dv")
    p.wait("vector", "dv", ds)
    di = p.tag(p.eng.reciprocal(mxi[:, 0:nb], mxs[:, 0:nb])
               if p.on("vector") else None, "dv")
    p.wait("vector", "dv", di)
    dy = p.tag(p.eng.tensor_mul(f1.rearrange("p (a b) -> p a b", b=32), v3,
                                mxi[:, 0:nb].unsqueeze(2).broadcast_to([128, nb, 32]))
               if p.on("vector") else None, "dv")
    p.wait("scalar", "dv", dy)
    ea = p.tag(p.eng.activation(b1_, f1, AF.Abs) if p.on("scalar") else None, "ev")
    p.wait("vector", "ev", ea)
    dc2 = p.tag(p.eng.tensor_scalar(b2_, b1_, 4.0, None, ALU.is_lt)
                if p.on("vector") else None, "dv")
    dc1 = p.tag(p.eng.tensor_scalar(b3_, b1_, 2.0, None, ALU.is_lt)
                if p.on("vector") else None, "dv")
    p.wait("vector", "dv", dc1)
    du = p.tag(p.eng.tensor_scalar(b1_, b2_, 2.0, 2.0, ALU.mult, ALU.add)
               if p.on("vector") else None, "dv")
    p.wait("vector", "dv", du)
    dis = p.tag(p.eng.scalar_tensor_tensor(b1_, b3_, 4.0, b1_, ALU.mult, ALU.add)
                if p.on("vector") else None, "dv")
    p.wait("vector", "dv", dis)
    dz = p.tag(p.eng.tensor_mul(f2, f1, b1_) if p.on("vector") else None, "dv")
    p.wait("scalar", "dv", dz)
    ei = p.tag(p.eng.activation(iscr, f2, AF.Copy, bias=64.5) if p.on("scalar") else None, "ev")
    p.wait("scalar", "ev", ei)
    ef = p.tag(p.eng.activation(f2, iscr, AF.Copy, bias=-64.0) if p.on("scalar") else None, "ev")
    dv1 = p.tag(p.eng.tensor_scalar(b2_, b2_, -0.25, 0.5, ALU.mult, ALU.add)
                if p.on("vector") else None, "dv")
    p.wait("vector", "dv", dv1)
    dv2 = p.tag(p.eng.scalar_tensor_tensor(b2_, b3_, -0.125, b2_, ALU.mult, ALU.add)
                if p.on("vector") else None, "dv")
    p.wait("vector", "ev", ef)
    p.wait("vector", "dv", dv2)
    dq = p.tag(p.eng.tensor_mul(f2, f2, b2_) if p.on("vector") else None, "dv")
    p.wait("vector", "dv", dq)
    dd = p.tag(p.eng.tensor_mul(out_bf.rearrange("p (a b) -> p a b", b=32),
                                f2.rearrange("p (a b) -> p a b", b=32),
                                mxs[:, 0:nb].unsqueeze(2).broadcast_to([128, nb, 32]))
               if p.on("vector") else None, "dv")
    return dd


def build_ffn1(tok=2048, d_model=1024, fctx=None, wmode=None):
    nkc = d_model // 128
    if fctx is None:
        nc = bass.Bass("TRN2", target_bir_lowering=False)
        x = nc.dram_tensor("x", [d_model, tok], F32, kind="ExternalInput").ap()
        w = nc.dram_tensor("w", [d_model, d_model], F32, kind="ExternalInput").ap()
        bias = nc.dram_tensor("bias", [d_model], F32, kind="ExternalInput").ap()
        ones_in = nc.dram_tensor("ones", [128, 128], F32, kind="ExternalInput").ap()
        idn_in = nc.dram_tensor("idn", [128, 128], F32, kind="ExternalInput").ap()
        y = nc.dram_tensor("y", [d_model, tok], F32, kind="ExternalOutput").ap()
        base = None
        wdT_dram = None
        wmode = None
    else:
        nc, sems, base = fctx["nc"], fctx["sems"], fctx["base"]
        x = fctx["x"]; w = fctx["w"]; bias = fctx["bias"]
        ones_in = fctx["ones"]; idn_in = fctx["idn"]; y = fctx["y"]
        wdT_dram = fctx.get("wdT_dram")

    st = contextlib.ExitStack()

    class H:
        pass

    h = H()
    sb = _alloc_common(nc, st, h)
    _, psa = _mk_alloc(nc, st)
    h.xt = sb("xt", [128, nkc, tok], F32)
    h.xln = sb("xln", [128, nkc, tok], BF16)
    h.idn_bf = sb("idn_bf", [128, 128], BF16)
    h.wtc = [sb(f"wtc{i}", [128, d_model], F32) for i in range(2)]
    h.wqc = [sb(f"wqc{i}", [128, d_model], BF16) for i in range(2)]
    h.wdT = sb("wdT", [128, nkc, d_model], BF16)
    h.xd = sb("xd", [128, nkc, tok], BF16)
    _xf = h.xd[:].rearrange("p a b -> p (a b)")
    h.cb = _xf[:, 0:nkc * 512].rearrange("p (c w) -> p c w", w=512)
    h.sqb = _xf[:, nkc * 512:2 * nkc * 512].rearrange("p (c w) -> p c w", w=512)
    h.xtm = [sb(f"xtm{i}", [128, d_model], BF16) for i in range(2)]
    h.xdtm = [sb(f"xdtm{i}", [128, d_model], BF16) for i in range(2)]
    h.scr = []
    for i in range(2):
        _f1 = sb(f"qf1_{i}", [128, d_model], F32)
        h.scr.append({
            "f1": _f1[:], "f2": sb(f"qf2_{i}", [128, d_model], F32)[:],
            "iscr": _f1[:].bitcast(I32),
            "b1": sb(f"qb1_{i}", [128, d_model], BF16)[:],
            "b2": sb(f"qb2_{i}", [128, d_model], BF16)[:],
            "b3": sb(f"qb3_{i}", [128, d_model], BF16)[:],
            "mxs": sb(f"mxs_{i}", [128, d_model // 32], F32)[:],
            "mxi": sb(f"mxi_{i}", [128, d_model // 32], F32)[:],
        })
    h.bih = sb("bih", [128, nkc], F32)
    h.bic = sb("bic", [128, nkc], F32)
    h.h2e = [sb(f"h2e{i}", [128, 512], BF16) for i in range(2)]
    PS0 = psa("PS0", [128, 2, 512])
    PS1 = psa("PS1", [128, 2, 512])
    PSX = psa("PSX", [128, 2, 512])
    PS2 = psa("PS2", [128, 2, 512])
    sem_cm = contextlib.ExitStack()
    if fctx is None:
        sems = {}
        for s_ in SEM_NAMES:
            sems[s_] = sem_cm.enter_context(nc.semaphore(s_))

    def program(p):
        _common_boot(p, h, ones_in, idn_in)
        p.tag(p.eng.dma_start(out=h.idn_bf[:], in_=idn_in[:]) if p.on("gpsimd") else None,
              "boot", 16)
        p.tag(p.eng.dma_start(out=h.bic[:], in_=bias.rearrange("(c p) -> p c", p=128),
                              allow_slow_non_contiguous=True)
              if p.on("sync") else None, "ld", 16)
        p.tag(p.eng.dma_start(out=h.xt[:], in_=x.rearrange("(c p) t -> p c t", p=128))
              if p.on("sync") else None, "ld", 16)
        p.barrier()
        p.tag(p.eng.activation(h.bih[:], h.bic[:], AF.Copy, scale=0.5)
              if p.on("scalar") else None, "ev")
        emit_ln(p, h, h.xt, h.xln, nkc, tok, d_model, PSX, PS2)
        ln_done = p.c["dv"]

        def tp_block(src_ap, dst_ap, gate_mm=None, gate_dv=None):
            tsl = p.state.get("tpsl", 0)
            p.state["tpsl"] = tsl ^ 1
            pslot = PS2[:, tsl, 0:128].bitcast(BF16)[:, 0:128]
            p.wait("tensor", "ev", p.state.get(("tps", tsl), 0))
            if gate_dv is not None:
                p.wait("tensor", "dv", gate_dv)
            mt1 = p.tag(p.eng.transpose(pslot, src_ap, h.idn_bf[:])
                        if p.on("tensor") else None, "mm")
            p.wait("scalar", "mm", mt1)
            if gate_mm is not None:
                p.wait("scalar", "dv", gate_mm)
            cst = p.tag(p.eng.copy(dst_ap, pslot) if p.on("scalar") else None, "ev")
            p.state[("tps", tsl)] = cst
            return cst

        if wmode == "load":
            lwc = p.tag(p.eng.dma_start(out=h.wdT[:].rearrange("p a b -> p (a b)"),
                                        in_=wdT_dram[:, :])
                        if p.on("gpsimd") else None, "boot", 16)
            p.wait("tensor", "boot", lwc)
        else:
            for c in range(nkc):
                sl = c % 2
                wsem = "va" if sl == 0 else "vb"
                p.wait("sync", "dv", p.state.get(("wtc", sl), 0))
                lw = p.tag(p.eng.dma_start(out=h.wtc[sl][:],
                                           in_=w.rearrange("(c p) e -> p c e", p=128)[:, c, :])
                           if p.on("sync") else None, wsem, 16)
                p.wait("vector", wsem, lw)
                p.wait("scalar", wsem, lw)
                p.wait("vector", "dv", p.state.get(("scr_free", sl), 0))
                p.wait("vector", "mm", p.state.get(("wqc_free", sl), 0))
                dd = emit_mx_quant(p, h.scr[sl], h.wtc[sl][:], h.wqc[sl][:], d_model)
                p.state[("scr_free", sl)] = dd
                p.state[("wtc", sl)] = dd
                first = True
                for icc in range(nkc):
                    tp_block(h.wqc[sl][:, icc * 128:(icc + 1) * 128],
                             h.wdT[:, icc, c * 128:(c + 1) * 128],
                             gate_dv=dd if first else None)
                    first = False
                p.state[("wqc_free", sl)] = p.c["mm"]
            if wmode == "save":
                p.wait("gpsimd", "ev", p.c["ev"])
                swc = p.tag(p.eng.dma_start(out=wdT_dram[:, :],
                                            in_=h.wdT[:].rearrange("p a b -> p (a b)"))
                            if p.on("gpsimd") else None, "boot", 16)
                p.wait("gpsimd", "boot", swc)
        for tc in range(tok // 128):
            sl = tc % 2
            first = True
            for kc in range(nkc):
                tp_block(h.xln[:, kc, tc * 128:(tc + 1) * 128],
                         h.xtm[sl][:, kc * 128:(kc + 1) * 128],
                         gate_mm=p.state.get(("xtm_free", sl), 0),
                         gate_dv=ln_done if first else None)
                first = False
            tin_done = p.c["ev"]
            p.wait("vector", "ev", tin_done)
            p.wait("vector", "dv", p.state.get(("scr_free", sl), 0))
            p.wait("vector", "mm", p.state.get(("xdtm_free", sl), 0))
            dd = emit_mx_quant(p, h.scr[sl], h.xtm[sl][:], h.xdtm[sl][:], d_model)
            p.state[("scr_free", sl)] = dd
            p.state[("xtm_free", sl)] = dd
            first = True
            for kc in range(nkc):
                tp_block(h.xdtm[sl][:, kc * 128:(kc + 1) * 128],
                         h.xd[:, kc, tc * 128:(tc + 1) * 128],
                         gate_dv=dd if first else None)
                first = False
            p.state[("xdtm_free", sl)] = p.c["mm"]
        p.barrier()
        for oc in range(nkc):
            for tq in range(tok // 512):
                bk = (oc * (tok // 512) + tq) % 2
                pa = PS0 if bk == 0 else PS1
                p.wait("tensor", "ev", p.state.get(("ffps", bk), 0))
                mv = None
                for icc in range(nkc):
                    if p.on("tensor"):
                        mv = p.eng.matmul(pa[:, 0, :], h.wdT[:, icc, oc * 128:(oc + 1) * 128],
                                          h.xd[:, icc, tq * 512:(tq + 1) * 512],
                                          start=(icc == 0), stop=(icc == nkc - 1))
                mmc = p.tag(mv, "mm")
                sl = tq % 2
                p.wait("scalar", "mm", mmc)
                p.wait("scalar", "dv", p.state.get(("h2", sl), 0))
                evv = p.tag(p.eng.activation(h.h2e[sl][:], pa[:, 0, :], AF.Identity,
                                             scale=0.5, bias=h.bih[:, oc:oc + 1])
                            if p.on("scalar") else None, "ev")
                p.state[("ffps", bk)] = evv
                p.wait("vector", "ev", evv)
                p.state[("h2", sl)] = p.tag(
                    p.eng.tensor_add(h.xt[:, oc, tq * 512:(tq + 1) * 512],
                                     h.xt[:, oc, tq * 512:(tq + 1) * 512], h.h2e[sl][:])
                    if p.on("vector") else None, "dv")
        p.wait("sync", "dv", p.c["dv"])
        stv = p.tag(p.eng.dma_start(out=y.rearrange("(c p) t -> p c t", p=128), in_=h.xt[:])
                    if p.on("sync") else None, "stq", 16)
        p.wait("sync", "stq", stv)

    _mk_prog(nc, sems, st, base)(program)
    sem_cm.close()
    st.close()
    return nc


def build_scanprep(tok=2048, d_model=1024, R=64, N=16, fctx=None):
    nkc = d_model // 128
    if fctx is None:
        nc = bass.Bass("TRN2", target_bir_lowering=False)
        x = nc.dram_tensor("x", [d_model, tok], F32, kind="ExternalInput").ap()
        wdtwT = nc.dram_tensor("wdtwT", [d_model, R], F32, kind="ExternalInput").ap()
        wdtT = nc.dram_tensor("wdtT", [R, d_model], F32, kind="ExternalInput").ap()
        bdt = nc.dram_tensor("bdt", [d_model], F32, kind="ExternalInput").ap()
        wBT = nc.dram_tensor("wBT", [d_model, N], F32, kind="ExternalInput").ap()
        wCT = nc.dram_tensor("wCT", [d_model, N], F32, kind="ExternalInput").ap()
        ones_in = nc.dram_tensor("ones", [128, 128], F32, kind="ExternalInput").ap()
        idn_in = nc.dram_tensor("idn", [128, 128], F32, kind="ExternalInput").ap()
        delta_o = nc.dram_tensor("delta", [d_model, tok], F32, kind="ExternalOutput").ap()
        dx_o = nc.dram_tensor("dx", [d_model, tok], F32, kind="ExternalOutput").ap()
        B_o = nc.dram_tensor("B", [N, tok], F32, kind="ExternalOutput").ap()
        C_o = nc.dram_tensor("C", [N, tok], F32, kind="ExternalOutput").ap()
        base = None
        a2a = False
    else:
        nc, sems, base = fctx["nc"], fctx["sems"], fctx["base"]
        x = fctx["x"]; wdtwT = fctx["wdtwT"]; wdtT = fctx["wdtT"]; bdt = fctx["bdt"]
        wBT = fctx["wBT"]; wCT = fctx["wCT"]; ones_in = fctx["ones"]; idn_in = fctx["idn"]
        delta_o = fctx["delta_o"]; dx_o = fctx["dx_o"]; B_o = fctx["B_o"]; C_o = fctx["C_o"]
        a2a = fctx.get("a2a", False)

    st = contextlib.ExitStack()

    class H:
        pass

    h = H()
    sb = _alloc_common(nc, st, h)
    _, psa = _mk_alloc(nc, st)
    h.xt = sb("xt", [128, nkc, tok], F32)
    h.xlnf = sb("xlnf", [128, nkc, tok], F32)
    h.cb = sb("cb", [128, nkc, 512], BF16)
    h.sqb = sb("sqb", [128, nkc, 512], BF16)
    h.wdtwb = sb("wdtwb", [128, nkc, R], BF16)
    h.wdtb = sb("wdtb", [R, nkc, 128], BF16)
    h.wBb = sb("wBb", [128, nkc, N], BF16)
    h.wCb = sb("wCb", [128, nkc, N], BF16)
    h.bdtc = sb("bdtc", [128, nkc], F32)
    h.dt = sb("dt", [R, tok], BF16)
    h.zf = sb("zf", [128, 512], F32)
    h.ab = sb("ab", [128, 512], F32)
    h.e1 = sb("e1", [128, 512], F32)
    h.r1 = sb("r1", [128, 512], F32)
    h.dst = [sb(f"dst{i}", [128, 512], F32) for i in range(2)]
    h.dxs = [sb(f"dxs{i}", [128, 512], F32) for i in range(2)]
    h.onec = sb("onec", [128, 1], F32)
    PS0 = psa("PS0", [128, 2, 512])
    PS1 = psa("PS1", [128, 2, 512])
    PSX = psa("PSX", [128, 2, 512])
    PS2 = psa("PS2", [128, 2, 512])
    sem_cm = contextlib.ExitStack()
    if fctx is None:
        sems = {}
        for s_ in SEM_NAMES:
            sems[s_] = sem_cm.enter_context(nc.semaphore(s_))

    def program(p):
        _common_boot(p, h, ones_in, idn_in)
        oc0 = p.tag(p.eng.memset(h.onec[:], 1.0) if p.on("vector") else None, "dv")
        p.tag(p.eng.dma_start(out=h.wdtwb[:], in_=wdtwT.rearrange("(c p) r -> p c r", p=128))
              if p.on("gpsimd") else None, "boot", 16)
        p.tag(p.eng.dma_start(out=h.wdtb[:], in_=wdtT.rearrange("r (c p) -> r c p", p=128),
                              allow_slow_non_contiguous=True)
              if p.on("gpsimd") else None, "boot", 16)
        p.tag(p.eng.dma_start(out=h.wBb[:], in_=wBT.rearrange("(c p) n -> p c n", p=128))
              if p.on("gpsimd") else None, "boot", 16)
        p.tag(p.eng.dma_start(out=h.wCb[:], in_=wCT.rearrange("(c p) n -> p c n", p=128))
              if p.on("gpsimd") else None, "boot", 16)
        p.tag(p.eng.dma_start(out=h.bdtc[:], in_=bdt.rearrange("(c p) -> p c", p=128),
                              allow_slow_non_contiguous=True)
              if p.on("sync") else None, "ld", 16)
        p.tag(p.eng.dma_start(out=h.xt[:], in_=x.rearrange("(c p) t -> p c t", p=128))
              if p.on("sync") else None, "ld", 16)
        p.barrier()
        emit_ln(p, h, h.xt, h.xlnf, nkc, tok, d_model, PSX, PS2)
        nt = tok // 512
        for tq in range(nt):
            p.wait("vector", "dv", p.c["dv"])
            p.wait("vector", "mm", p.c["mm"])
            dcb = p.tag(p.eng.tensor_copy(h.cb[:], h.xlnf[:, :, tq * 512:(tq + 1) * 512])
                        if p.on("vector") else None, "dv")
            p.wait("tensor", "dv", dcb)
            mv = None
            for kc in range(nkc):
                if p.on("tensor"):
                    mv = p.eng.matmul(PS0[0:R, 0, :], h.wdtwb[:, kc, :], h.cb[:, kc, :],
                                      start=(kc == 0), stop=(kc == nkc - 1))
            m1 = p.tag(mv, "mm")
            p.wait("scalar", "mm", m1)
            ev1 = p.tag(p.eng.copy(h.dt[:, tq * 512:(tq + 1) * 512], PS0[0:R, 0, :])
                        if p.on("scalar") else None, "ev")
            mv = None
            for kc in range(nkc):
                if p.on("tensor"):
                    p.eng.matmul(PS1[0:N, 0, :], h.wBb[:, kc, :], h.cb[:, kc, :],
                                 start=(kc == 0), stop=(kc == nkc - 1))
                    mv = p.eng.matmul(PS1[0:N, 1, :], h.wCb[:, kc, :], h.cb[:, kc, :],
                                      start=(kc == 0), stop=(kc == nkc - 1))
            m2 = p.tag(mv, "mm")
            p.wait("scalar", "mm", m2)
            p.wait("scalar", "va", p.state.get(("bcq", 0), 0))
            p.wait("scalar", "vb", p.state.get(("bcq", 1), 0))
            ev2 = p.tag(p.eng.copy(h.zf[0:N, :], PS1[0:N, 0, :]) if p.on("scalar") else None, "ev")
            ev3 = p.tag(p.eng.copy(h.ab[0:N, :], PS1[0:N, 1, :]) if p.on("scalar") else None, "ev")
            p.wait("sync", "ev", ev3)
            bsem = "va" if tq % 2 == 0 else "vb"
            p.tag(p.eng.dma_start(out=B_o[:, tq * 512:(tq + 1) * 512], in_=h.zf[0:N, :])
                  if p.on("sync") else None, bsem, 16)
            bq = p.tag(p.eng.dma_start(out=C_o[:, tq * 512:(tq + 1) * 512], in_=h.ab[0:N, :])
                       if p.on("sync") else None, bsem, 16)
            p.state[("bcq", tq % 2)] = bq
            p.wait("tensor", "ev", ev3)
        dtdone = p.c["ev"]
        for kc in range(nkc):
            for tq in range(nt):
                pa = PS0 if (kc * nt + tq) % 2 == 0 else PS1
                p.wait("tensor", "ev", p.c["ev"])
                mv = None
                if p.on("tensor"):
                    mv = p.eng.matmul(pa[:, 0, :], h.wdtb[:, kc, :],
                                      h.dt[:, tq * 512:(tq + 1) * 512], start=True, stop=True)
                m3 = p.tag(mv, "mm")
                p.wait("scalar", "mm", max(m3, 0))
                p.wait("scalar", "ev", dtdone)
                sl = tq % 2
                p.wait("scalar", "va", p.state.get(("bcq", 0), 0))
                p.wait("scalar", "vb", p.state.get(("bcq", 1), 0))
                p.wait("scalar", "dv", p.state.get("sp_d1", 0))
                e1 = p.tag(p.eng.activation(h.zf[:], pa[:, 0, :], AF.Identity,
                                            bias=h.bdtc[:, kc:kc + 1])
                           if p.on("scalar") else None, "ev")
                p.wait("scalar", "ev", e1)
                e2 = p.tag(p.eng.activation(h.ab[:], h.zf[:], AF.Abs)
                           if p.on("scalar") else None, "ev")
                p.wait("scalar", "ev", e2)
                e3 = p.tag(p.eng.activation(h.e1[:], h.ab[:], AF.Exp, scale=-1.0)
                           if p.on("scalar") else None, "ev")
                p.wait("scalar", "ev", e3)
                e4 = p.tag(p.eng.activation(h.e1[:], h.e1[:], AF.Ln, bias=h.onec[:, 0:1])
                           if p.on("scalar") else None, "ev")
                p.wait("scalar", "ev", e4)
                e5 = p.tag(p.eng.activation(h.r1[:], h.zf[:], AF.Relu)
                           if p.on("scalar") else None, "ev")
                p.wait("vector", "ev", e5)
                p.wait("vector", "va" if sl == 0 else "vb", p.state.get(("dq", sl), 0))
                d1 = p.tag(p.eng.tensor_add(h.dst[sl][:], h.r1[:], h.e1[:])
                           if p.on("vector") else None, "dv")
                p.state["sp_d1"] = d1
                p.wait("vector", "dv", d1)
                d2 = p.tag(p.eng.tensor_mul(h.dxs[sl][:], h.dst[sl][:],
                                            h.xlnf[:, kc, tq * 512:(tq + 1) * 512])
                           if p.on("vector") else None, "dv")
                deng = "gpsimd" if a2a else "sync"
                p.wait(deng, "dv", d2)
                dsem = "va" if sl == 0 else "vb"
                if a2a:
                    rd = 512 * (kc // 2) + 128 * (kc % 2)
                    ddst = delta_o[rd:rd + 128, tq * 512:(tq + 1) * 512]
                    xdst = dx_o[rd + 256:rd + 384, tq * 512:(tq + 1) * 512]
                else:
                    ddst = delta_o.rearrange("(c p) t -> p c t", p=128)[
                        :, kc, tq * 512:(tq + 1) * 512]
                    xdst = dx_o.rearrange("(c p) t -> p c t", p=128)[
                        :, kc, tq * 512:(tq + 1) * 512]
                p.tag(p.eng.dma_start(out=ddst, in_=h.dst[sl][:])
                      if p.on(deng) else None, dsem, 16)
                dq = p.tag(p.eng.dma_start(out=xdst, in_=h.dxs[sl][:])
                           if p.on(deng) else None, dsem, 16)
                p.state[("dq", sl)] = dq
        for s_ in ("va", "vb"):
            p.wait("sync", s_, p.c[s_])

    _mk_prog(nc, sems, st, base)(program)
    sem_cm.close()
    st.close()
    return nc


def build_scan(seq=8192, n_b=2, N=16, fctx=None):
    if fctx is None:
        nc = bass.Bass("TRN2", target_bir_lowering=False)
        delta = nc.dram_tensor("delta", [n_b, 128, seq], F32, kind="ExternalInput").ap()
        dx = nc.dram_tensor("dx", [n_b, 128, seq], F32, kind="ExternalInput").ap()
        Bf = nc.dram_tensor("Bf", [n_b, N, seq], F32, kind="ExternalInput").ap()
        Cf = nc.dram_tensor("Cf", [n_b, N, seq], F32, kind="ExternalInput").ap()
        A_pairs = nc.dram_tensor("A_pairs", [128, 16], F32, kind="ExternalInput").ap()
        seld = nc.dram_tensor("seld", [128, 16, 128], F32, kind="ExternalInput").ap()
        seln = nc.dram_tensor("seln", [N, 128], F32, kind="ExternalInput").ap()
        sel8 = nc.dram_tensor("sel8", [128, 8], F32, kind="ExternalInput").ap()
        y_o = nc.dram_tensor("y", [n_b, 128, seq], F32, kind="ExternalOutput").ap()
        base = None
        fused = False
    else:
        nc, sems, base = fctx["nc"], fctx["sems"], fctx["base"]
        A_pairs = fctx["A_pairs"]; seld = fctx["seld"]; seln = fctx["seln"]
        sel8 = fctx["sel8"]
        fused = True
        a2ain = fctx["a2ain"]; ag2 = fctx["ag2"]
        bcin = fctx["bcin"]; bcout = fctx["bcout"]
        gidx = fctx["gidx2"]
        ya = fctx["ya"]

    st = contextlib.ExitStack()
    sb, psa = _mk_alloc(nc, st)

    class H:
        pass

    h = H()
    h.delt = sb("delt", [128, seq], BF16)
    h.dxt = sb("dxt", [128, seq], BF16)
    h.Bbc = sb("Bbc", [128, seq], F32)
    h.Cbc = sb("Cbc", [128, seq], F32)
    h.Bin = sb("Bin", [N, seq], F32R)
    h.selds = sb("selds", [128, 16, 128], BF16)
    h.selns = sb("selns", [N, 128], F32R)
    h.sel8s = sb("sel8s", [128, 8], BF16)
    h.Aps = sb("Aps", [128, 32 if fused else 16], F32)
    h.Acol = sb("Acol", [128, 32 if fused else 16], F32)
    h.a_sb = [sb(f"a_sb{i}", [128, 1024], F32) for i in range(2)]
    h.u_sb = [sb(f"u_sb{i}", [128, 1024], F32) for i in range(2)]
    h.h_sb = [sb(f"h_sb{i}", [128, 1024], F32) for i in range(2)]
    h.hc_sb = [sb(f"hc_sb{i}", [128, 1024], BF16) for i in range(2)]
    h.carry = sb("carry", [128, 16], F32)
    h.ystg = sb("ystg", [128, 1024], F32)
    h.gidxs = sb("gidxs", [128, 16], I32)
    PA = psa("PA", [128, 2, 512])
    PB = psa("PB", [128, 2, 512])
    PY = psa("PY", [128, 2, 512])
    PC = PY
    sem_cm = contextlib.ExitStack()
    if fctx is None:
        sems = {}
        for s_ in SEM_NAMES:
            sems[s_] = sem_cm.enter_context(nc.semaphore(s_))
    TQ = 1024
    nch = seq // TQ

    def program(p):
        if fused:
            c1 = None
            if p.on("gpsimd"):
                c1 = p.eng.collective_compute(
                    "AllGather", ALU.bypass,
                    replica_groups=[[0, 1, 2, 3], [4, 5, 6, 7]],
                    ins=[a2ain.opt()], outs=[ag2.opt()])
            p.tag(c1, "cc")
            c2 = None
            if p.on("gpsimd"):
                c2 = p.eng.collective_compute(
                    "AllGather", ALU.bypass,
                    replica_groups=[[0, 1, 2, 3], [4, 5, 6, 7]],
                    ins=[bcin.opt()], outs=[bcout.opt()])
            cct = p.tag(c2, "cc")
            lgi = p.tag(p.eng.dma_start(out=h.gidxs[:], in_=gidx[:])
                        if p.on("sync") else None, "ld", 16)
            p.wait("gpsimd", "cc", cct)
            p.wait("gpsimd", "ld", lgi)
        p.tag(p.eng.dma_start(out=h.selds[:], in_=seld[:]) if p.on("gpsimd") else None,
              "boot", 16)
        p.tag(p.eng.dma_start(out=h.selns[:], in_=seln[:]) if p.on("gpsimd") else None,
              "boot", 16)
        p.tag(p.eng.dma_start(out=h.sel8s[:], in_=sel8[:]) if p.on("gpsimd") else None,
              "boot", 16)
        p.tag(p.eng.dma_start(out=h.Aps[:], in_=A_pairs[:]) if p.on("sync") else None, "ld", 16)
        pz = p.tag(p.eng.memset(PY[:], 0.0) if p.on("vector") else None, "dv")
        p.wait("tensor", "dv", pz)
        p.barrier()
        ea = p.tag(p.eng.activation(h.Acol[:], h.Aps[:], AF.Exp) if p.on("scalar") else None,
                   "ev")
        p.wait("vector", "ev", ea)
        dA = p.tag(p.eng.tensor_scalar(h.Acol[:], h.Acol[:], -1.0, None, ALU.mult)
                   if p.on("vector") else None, "dv")
        p.wait("scalar", "dv", dA)

        ypend = []

        def do_y(ent):
            sl_, k2_, b_, ch_ = ent["sl"], ent["k2"], ent["b"], ent["ch"]
            p.wait("tensor", "gp", ent["d4"])
            if k2_ % 4 == 0:
                p.wait("tensor", "ev", p.state.get("py_free", 0))
            strip = (k2_ % 4) * 32
            mv = None
            for half in range(2):
                if p.on("tensor"):
                    mv = p.eng.matmul(
                        PY[strip:strip + 8, half, :], h.sel8s[:],
                        h.hc_sb[sl_][:, half * 512:(half + 1) * 512],
                        start=True, stop=True,
                        tile_position=(0, strip))
            my = p.tag(mv, "mm")
            p.state[("hc_free", sl_)] = my
            if k2_ % 4 == 3:
                g = k2_ // 4
                p.wait("scalar", "mm", my)
                p.wait("scalar", "va", p.state.get(("ystq", 0), 0))
                p.wait("scalar", "vb", p.state.get(("ystq", 1), 0))
                ey = p.tag(p.eng.activation(h.ystg[:].rearrange("p (x y) -> p x y", x=2),
                                            PY[:, 0:2, :], AF.Copy)
                           if p.on("scalar") else None, "ev")
                p.state["py_free"] = ey
                p.wait("sync", "ev", ey)
                ysem = "va" if g % 2 == 0 else "vb"
                yq = None
                if fused:
                    q_ = (ch_ * TQ) // 2048
                    cw = (ch_ * TQ) % 2048
                    for j in range(4):
                        rr = q_ * 256 + b_ * 128 + 32 * g + 8 * j
                        yq = p.tag(p.eng.dma_start(
                            out=ya[rr:rr + 8, cw:cw + TQ],
                            in_=h.ystg[32 * j:32 * j + 8, :])
                            if p.on("sync") else None, ysem, 16)
                else:
                    for j in range(4):
                        yq = p.tag(p.eng.dma_start(
                            out=y_o[b_, 32 * g + 8 * j:32 * g + 8 * j + 8, ent["csl"]],
                            in_=h.ystg[32 * j:32 * j + 8, :])
                            if p.on("sync") else None, ysem, 16)
                p.state[("ystq", g % 2)] = yq

        for b in range(n_b):
            p.barrier()
            if fused:
                lds = None
                for q_ in range(4):
                    r = b * 4 + q_
                    lds = p.tag(p.eng.indirect_dma_start(
                        h.delt[:, q_ * 2048:(q_ + 1) * 2048], None, ag2[:, :],
                        bass.IndirectOffsetOnAxis(h.gidxs[:, r:r + 1], 0))
                        if p.on("gpsimd") else None, "wa", 16)
                    lds = p.tag(p.eng.indirect_dma_start(
                        h.dxt[:, q_ * 2048:(q_ + 1) * 2048], None, ag2[:, :],
                        bass.IndirectOffsetOnAxis(h.gidxs[:, 8 + r:8 + r + 1], 0))
                        if p.on("gpsimd") else None, "wa", 16)
                p.wait("tensor", "wa", lds)
                lb = None
                for q_ in range(4):
                    lb = p.tag(p.eng.dma_start(out=h.Bin[:, q_ * 2048:(q_ + 1) * 2048],
                                               in_=bcout[32 * q_:32 * q_ + N, :])
                               if p.on("gpsimd") else None, "wa", 16)
                p.wait("tensor", "wa", lb)
            else:
                p.tag(p.eng.dma_start(out=h.delt[:], in_=delta[b]) if p.on("gpsimd") else None,
                      "wa", 16)
                p.tag(p.eng.dma_start(out=h.dxt[:], in_=dx[b]) if p.on("gpsimd") else None,
                      "wa", 16)
                lb = p.tag(p.eng.dma_start(out=h.Bin[:], in_=Bf[b]) if p.on("gpsimd") else None,
                           "wa", 16)
                p.wait("tensor", "wa", lb)
            for cc in range(seq // 512):
                mv = None
                if p.on("tensor"):
                    mv = p.eng.matmul(PC[:, cc % 2, :], h.selns[:],
                                      h.Bin[:, cc * 512:(cc + 1) * 512], start=True, stop=True)
                m1 = p.tag(mv, "mm")
                p.wait("scalar", "mm", m1)
                p.tag(p.eng.copy(h.Bbc[:, cc * 512:(cc + 1) * 512], PC[:, cc % 2, :])
                      if p.on("scalar") else None, "ev")
                p.wait("tensor", "ev", p.c["ev"])
            p.wait("gpsimd", "mm", p.c["mm"])
            lc = None
            if fused:
                for q_ in range(4):
                    lc = p.tag(p.eng.dma_start(
                        out=h.Bin[:, q_ * 2048:(q_ + 1) * 2048],
                        in_=bcout[32 * q_ + N:32 * q_ + 2 * N, :])
                        if p.on("gpsimd") else None, "wb", 16)
            else:
                lc = p.tag(p.eng.dma_start(out=h.Bin[:], in_=Cf[b]) if p.on("gpsimd") else None,
                           "wb", 16)
            p.wait("tensor", "wb", lc)
            for cc in range(seq // 512):
                mv = None
                if p.on("tensor"):
                    mv = p.eng.matmul(PC[:, cc % 2, :], h.selns[:],
                                      h.Bin[:, cc * 512:(cc + 1) * 512], start=True, stop=True)
                m1 = p.tag(mv, "mm")
                p.wait("scalar", "mm", m1)
                p.tag(p.eng.copy(h.Cbc[:, cc * 512:(cc + 1) * 512], PC[:, cc % 2, :])
                      if p.on("scalar") else None, "ev")
                p.wait("tensor", "ev", p.c["ev"])
            p.barrier()
            for ch in range(nch):
                csl = slice(ch * TQ, (ch + 1) * TQ)
                for k2 in range(16):
                    it = p.state.get("scan_it", 0)
                    p.state["scan_it"] = it + 1
                    sl = it % 2
                    a_sb, u_sb = h.a_sb[sl], h.u_sb[sl]
                    h_sb, hc_sb = h.h_sb[sl], h.hc_sb[sl]
                    mh = [0, 0]
                    for half in range(2):
                        p.wait("tensor", "ev", p.state.get(("paf", half), 0))
                        p.wait("tensor", "dv", p.state.get(("pbf", half), 0))
                        mv = None
                        if p.on("tensor"):
                            p.eng.matmul(PA[:, half, :], h.selds[:, k2, :],
                                         h.delt[:, ch * TQ + half * 512:ch * TQ + (half + 1) * 512],
                                         start=True, stop=True)
                            mv = p.eng.matmul(PB[:, half, :], h.selds[:, k2, :],
                                              h.dxt[:, ch * TQ + half * 512:ch * TQ + (half + 1) * 512],
                                              start=True, stop=True)
                        mh[half] = p.tag(mv, "mm")
                    e1 = 0
                    for half in range(2):
                        p.wait("scalar", "mm", mh[half])
                        if half == 0:
                            p.wait("scalar", "dv", p.state.get(("a_free", sl), 0))
                        ak = b * 16 + k2 if fused else k2
                        e1 = p.tag(p.eng.activation(a_sb[:, half * 512:(half + 1) * 512],
                                                    PA[:, half, :], AF.Exp,
                                                    scale=h.Acol[:, ak:ak + 1])
                                   if p.on("scalar") else None, "ev")
                        p.state[("paf", half)] = e1
                    d1 = 0
                    for half in range(2):
                        p.wait("vector", "mm", mh[half])
                        if half == 0:
                            p.wait("vector", "dv", p.state.get(("u_free", sl), 0))
                        d1 = p.tag(p.eng.tensor_mul(u_sb[:, half * 512:(half + 1) * 512],
                                                    PB[:, half, :],
                                                    h.Bbc[:, ch * TQ + half * 512:ch * TQ + (half + 1) * 512])
                                   if p.on("vector") else None, "dv")
                        p.state[("pbf", half)] = d1
                    p.wait("vector", "ev", e1)
                    p.wait("vector", "dv", d1)
                    p.wait("vector", "gp", p.state.get(("h_free", sl), 0))
                    init = h.carry[:, k2:k2 + 1] if ch > 0 else 0.0
                    if ch > 0:
                        p.wait("vector", "gp", p.state.get(("carry_gp", k2), 0))
                    d2 = p.tag(p.eng.tensor_tensor_scan(h_sb[:], a_sb[:], u_sb[:],
                                                        init, ALU.mult, ALU.add)
                               if p.on("vector") else None, "dv")
                    p.state[("a_free", sl)] = d2
                    p.state[("u_free", sl)] = d2
                    p.wait("gpsimd", "dv", d2)
                    d3 = p.tag(p.eng.tensor_copy(h.carry[:, k2:k2 + 1], h_sb[:, TQ - 1:TQ])
                               if p.on("gpsimd") else None, "gp")
                    p.state[("carry_gp", k2)] = d3
                    p.wait("gpsimd", "mm", p.state.get(("hc_free", sl), 0))
                    d4 = p.tag(p.eng.tensor_mul(hc_sb[:], h_sb[:], h.Cbc[:, csl])
                               if p.on("gpsimd") else None, "gp")
                    p.state[("h_free", sl)] = d4
                    ypend.append(dict(sl=sl, k2=k2, b=b, ch=ch, csl=csl, d4=d4))
                    if len(ypend) > 1:
                        do_y(ypend.pop(0))
            while ypend:
                do_y(ypend.pop(0))
        for s_ in ("va", "vb"):
            p.wait("sync", s_, p.c[s_])

    _mk_prog(nc, sems, st, base)(program)
    sem_cm.close()
    st.close()
    return nc


def build_l1post(tok=2048, d_model=1024, fctx=None):
    nkc = d_model // 128
    if fctx is None:
        nc = bass.Bass("TRN2", target_bir_lowering=False)
        x = nc.dram_tensor("x", [d_model, tok], F32, kind="ExternalInput").ap()
        yin = nc.dram_tensor("yin", [d_model, tok], F32, kind="ExternalInput").ap()
        Dp = nc.dram_tensor("Dp", [d_model], F32, kind="ExternalInput").ap()
        lng = nc.dram_tensor("lng", [d_model], F32, kind="ExternalInput").ap()
        lnb = nc.dram_tensor("lnb", [d_model], F32, kind="ExternalInput").ap()
        woutT = nc.dram_tensor("woutT", [d_model, d_model], F32, kind="ExternalInput").ap()
        ones_in = nc.dram_tensor("ones", [128, 128], F32, kind="ExternalInput").ap()
        idn_in = nc.dram_tensor("idn", [128, 128], F32, kind="ExternalInput").ap()
        y = nc.dram_tensor("y", [d_model, tok], F32, kind="ExternalOutput").ap()
        base = None
        fused = False
    else:
        nc, sems, base = fctx["nc"], fctx["sems"], fctx["base"]
        x = fctx["x"]; y = fctx["x"]; yin = None
        Dp = fctx["Dp"]; lng = fctx["lng"]; lnb = fctx["lnb"]; woutT = fctx["woutT"]
        ones_in = fctx["ones"]; idn_in = fctx["idn"]
        ya_cc = fctx["ya"]; agy = fctx["agy"]; gidx = fctx["gidx"]
        fused = True

    st = contextlib.ExitStack()

    class H:
        pass

    h = H()
    sb = _alloc_common(nc, st, h, idn=False)
    _, psa = _mk_alloc(nc, st)
    h.xt = sb("xt", [128, nkc, tok], F32)
    h.yt = sb("yt", [128, nkc, tok], F32)
    h.xlnb = sb("xlnb", [128, nkc, tok], BF16)
    h.y2ln = sb("y2ln", [128, nkc, tok], BF16)
    h.wob = [sb(f"wob{i}", [128, nkc, 128], BF16) for i in range(2)]
    h.dpc = sb("dpc", [128, nkc], F32)
    h.gidxs = sb("gidxs", [128, 8], I32)
    h.gc = sb("gc", [128, nkc], F32)
    h.bc = sb("bc", [128, nkc], F32)
    PS0 = psa("PS0", [128, 2, 512])
    PS1 = psa("PS1", [128, 2, 512])
    PSX = psa("PSX", [128, 2, 512])
    PS2 = psa("PS2", [128, 2, 512])
    sem_cm = contextlib.ExitStack()
    if fctx is None:
        sems = {}
        for s_ in SEM_NAMES:
            sems[s_] = sem_cm.enter_context(nc.semaphore(s_))

    def program(p):
        _common_boot(p, h, ones_in, idn_in)
        p.tag(p.eng.dma_start(out=h.dpc[:], in_=Dp.rearrange("(c p) -> p c", p=128),
                              allow_slow_non_contiguous=True)
              if p.on("sync") else None, "ld", 16)
        p.tag(p.eng.dma_start(out=h.gc[:], in_=lng.rearrange("(c p) -> p c", p=128),
                              allow_slow_non_contiguous=True)
              if p.on("sync") else None, "ld", 16)
        p.tag(p.eng.dma_start(out=h.bc[:], in_=lnb.rearrange("(c p) -> p c", p=128),
                              allow_slow_non_contiguous=True)
              if p.on("sync") else None, "ld", 16)
        p.tag(p.eng.dma_start(out=h.xt[:], in_=x.rearrange("(c p) t -> p c t", p=128))
              if p.on("sync") else None, "ld", 16)
        if not fused:
            p.tag(p.eng.dma_start(out=h.yt[:], in_=yin.rearrange("(c p) t -> p c t", p=128))
                  if p.on("sync") else None, "ld", 16)
        else:
            ccl = None
            if p.on("gpsimd"):
                ccl = p.eng.collective_compute(
                    "AllGather", ALU.bypass,
                    replica_groups=[[0, 1, 2, 3], [4, 5, 6, 7]],
                    ins=[ya_cc.opt()], outs=[agy.opt()])
            cct = p.tag(ccl, "cc")
            lgi = p.tag(p.eng.dma_start(out=h.gidxs[:], in_=gidx[:])
                        if p.on("sync") else None, "ld", 16)
            p.wait("sync", "ld", lgi)
            p.wait("gpsimd", "cc", cct)
            p.wait("gpsimd", "ld", lgi)
            gv = None
            for j in range(8):
                gv = p.tag(p.eng.indirect_dma_start(
                    h.yt[:, j, :], None, agy[:, :],
                    bass.IndirectOffsetOnAxis(h.gidxs[:, j:j + 1], 0))
                    if p.on("gpsimd") else None, "wa", 16)
        p.barrier()
        _yf = h.y2ln[:].rearrange("p a b -> p (a b)")
        h.cb = _yf[:, 0:nkc * 512].rearrange("p (c w) -> p c w", w=512)
        h.sqb = _yf[:, nkc * 512:2 * nkc * 512].rearrange("p (c w) -> p c w", w=512)
        emit_ln(p, h, h.xt, h.xlnb, nkc, tok, d_model, PSX, PS2)
        for kc in range(nkc):
            p.wait("vector", "dv", p.c["dv"])
            dd = p.tag(p.eng.scalar_tensor_tensor(h.yt[:, kc, :], h.xlnb[:, kc, :],
                                                  h.dpc[:, kc:kc + 1], h.yt[:, kc, :],
                                                  ALU.mult, ALU.add)
                       if p.on("vector") else None, "dv")
        p.barrier()
        _xf2 = h.xlnb[:].rearrange("p a b -> p (a b)")
        h.cb = _xf2[:, 0:nkc * 512].rearrange("p (c w) -> p c w", w=512)
        h.sqb = _xf2[:, nkc * 512:2 * nkc * 512].rearrange("p (c w) -> p c w", w=512)
        emit_ln(p, h, h.yt, h.y2ln, nkc, tok, d_model, PSX, PS2, gb=(h.gc, h.bc))
        nt = tok // 512
        for oc in range(nkc):
            buf = oc % 2
            p.wait("gpsimd", "mm", p.state.get(("wo", buf), 0))
            wsem = "wa" if buf == 0 else "wb"
            wv_ = p.tag(p.eng.dma_start(
                out=h.wob[buf][:],
                in_=woutT.rearrange("(c p) e -> p c e", p=128)[:, :, oc * 128:(oc + 1) * 128])
                if p.on("gpsimd") else None, wsem, 16)
            for tq in range(nt):
                bk = (oc * nt + tq) % 2
                pa = PS0 if bk == 0 else PS1
                p.wait("tensor", wsem, wv_)
                if oc == 0 and tq == 0:
                    p.wait("tensor", "dv", p.c["dv"])
                else:
                    p.wait("tensor", "dv", p.state.get(("ffps", bk), 0))
                mv = None
                for kc in range(nkc):
                    if p.on("tensor"):
                        mv = p.eng.matmul(pa[:, 0, :], h.wob[buf][:, kc, :],
                                          h.y2ln[:, kc, tq * 512:(tq + 1) * 512],
                                          start=(kc == 0), stop=(kc == nkc - 1))
                mmc = p.tag(mv, "mm")
                p.wait("vector", "mm", mmc)
                dvt = p.tag(p.eng.tensor_add(h.xt[:, oc, tq * 512:(tq + 1) * 512],
                                             h.xt[:, oc, tq * 512:(tq + 1) * 512],
                                             pa[:, 0, :])
                            if p.on("vector") else None, "dv")
                p.state[("ffps", bk)] = dvt
            p.state[("wo", buf)] = p.c["mm"]
        p.wait("sync", "dv", p.c["dv"])
        stv = p.tag(p.eng.dma_start(out=y.rearrange("(c p) t -> p c t", p=128), in_=h.xt[:])
                    if p.on("sync") else None, "stq", 16)
        p.wait("sync", "stq", stv)

    _mk_prog(nc, sems, st, base)(program)
    sem_cm.close()
    st.close()
    return nc


def _bucket_host(n):
    import math as _m
    if n < 16:
        return int(n)
    v = 16 + int(np.float32(np.log(np.float32(max(n, 1)) / 16) / _m.log(128 / 16) * 16))
    return min(v, 31)


def _attn_consts():
    f32 = np.float32
    mask_d0 = np.zeros((128, 256), BF16NP)
    mask_m128 = np.zeros((128, 256), BF16NP)
    for p_ in range(128):
        mask_d0[p_, p_:] = 1
        if p_ + 128 < 256:
            mask_m128[p_, p_ + 128:] = 1
    nbm = np.zeros((3, 32, 128, 256), BF16NP)
    for ti, d in enumerate((128, 0, -128)):
        for p_ in range(128):
            for f_ in range(256):
                dist = d + f_ - p_
                if dist >= 0:
                    nbm[ti, _bucket_host(dist), p_, f_] = 1
    return mask_d0, mask_m128, nbm


def _attn_sels(q):
    sel1 = np.zeros((1, 512), np.float32)
    sel2 = np.zeros((1, 512), np.float32)
    sel128 = np.zeros((1, 512), np.float32)
    for qi in range(8):
        KT = 2 * (24 + qi) + 2
        kd128 = 16 * q + 2 * qi - 1
        kd0 = 16 * q + 2 * qi
        kdm = 16 * q + 2 * qi + 1
        for kt in range(64):
            j = qi * 64 + kt
            if kt >= KT:
                sel2[0, j] = 1.0
                continue
            skip = (kt > kdm) or (kt in (kd0, kdm)) or (qi >= 1 and kt == kd128)
            if skip:
                sel2[0, j] = 1.0
            else:
                if qi == 0 and q > 0 and kt == 16 * q - 1:
                    sel128[0, j] = 1.0
                else:
                    sel1[0, j] = 1.0
    return sel1, sel2, sel128


_PROG_CACHE = {}


def _prog(name, builder, *args):
    key = (name,) + args
    if key not in _PROG_CACHE:
        _PROG_CACHE[key] = builder(*args)
    return _PROG_CACHE[key]


def kernel_multi(**inputs):
    f32 = np.float32
    g = {k: np.ascontiguousarray(np.asarray(v, f32)) for k, v in inputs.items()}
    x0 = g["L0_attn_x"]
    ones = np.ones((128, 128), f32)
    idn = np.eye(128, dtype=f32)
    cores = list(range(N_CORES))
    TOK = 2048

    def fm(c, arr):
        b, q = c // 4, c % 4
        return np.ascontiguousarray(arr[b, q * TOK:(q + 1) * TOK, :].T)

    x_fm = [fm(c, x0) for c in cores]

    cfg = {"D": 1024, "H": 4096, "E": 8, "T": TOK, "MT": 1024}
    nc_ffn0 = _prog("ffn0", build_ffn0, tuple(sorted(cfg.items()))) if False else None
    if "ffn0" not in _PROG_CACHE:
        _PROG_CACHE["ffn0"] = build_ffn0(cfg)
    nc_ffn0 = _PROG_CACHE["ffn0"]

    def run_ffn0(xs):
        base = {"rw1": g["L0_ffn_rw1"], "rb1": g["L0_ffn_rb1"], "rw2": g["L0_ffn_rw2"],
                "rb2": g["L0_ffn_rb2"], "w1": g["L0_ffn_w1"], "b1": g["L0_ffn_b1"],
                "w2": g["L0_ffn_w2"], "b2": g["L0_ffn_b2"], "ones": ones, "idn": idn}
        res = run_bass_kernel_spmd(nc_ffn0, [dict(base, x=xs[c]) for c in cores],
                                   core_ids=cores)
        return [res.results[c]["y"] for c in cores]

    x1 = run_ffn0(x_fm)

    nc_kv = _PROG_CACHE.setdefault("kv", build_kv())
    base = {"wqk": g["L0_attn_Wqk"], "bqk": g["L0_attn_bqk"], "wv": g["L0_attn_Wv"],
            "bv": g["L0_attn_bv"], "ones": ones, "idn": idn}
    res = run_bass_kernel_spmd(nc_kv, [dict(base, x=x1[c]) for c in cores], core_ids=cores)
    qk = [res.results[c]["qk"] for c in cores]
    vtm = [res.results[c]["v"] for c in cores]

    nc_attn = _PROG_CACHE.setdefault("attn", build_attn())
    mask_d0, mask_m128, nbm = _attn_consts()
    qk_full = [np.ascontiguousarray(np.concatenate([qk[b * 4 + q] for q in range(4)], 1))
               for b in range(2)]
    v_full = [np.ascontiguousarray(np.concatenate([vtm[b * 4 + q] for q in range(4)], 0))
              for b in range(2)]
    in_maps = []
    for c in cores:
        b, q = c // 4, c % 4
        s1, s2, s128 = _attn_sels(q)
        in_maps.append({"x": x1[c], "qk_own": qk[c], "qk_full": qk_full[b],
                        "v_full": v_full[b], "v_own": vtm[c],
                        "gamma": g["L0_attn_gamma"], "beta": g["L0_attn_beta"],
                        "rel_emb": g["L0_attn_rel_emb"], "sel1": s1, "sel2": s2,
                        "sel128": s128, "nbmask": nbm, "mask_d0": mask_d0,
                        "mask_m128": mask_m128, "ones": ones})
    res = run_bass_kernel_spmd(nc_attn, in_maps, core_ids=cores)
    x2 = [res.results[c]["y"] for c in cores]

    x3 = run_ffn0(x2)

    nc_ffn1 = _PROG_CACHE.setdefault("ffn1", build_ffn1())

    def run_ffn1(xs):
        base = {"w": g["L1_ffn_weight"], "bias": g["L1_ffn_bias"], "ones": ones, "idn": idn}
        res = run_bass_kernel_spmd(nc_ffn1, [dict(base, x=xs[c]) for c in cores],
                                   core_ids=cores)
        return [res.results[c]["y"] for c in cores]

    x4 = run_ffn1(x3)

    nc_sp = _PROG_CACHE.setdefault("scanprep", build_scanprep())
    base = {"wdtwT": np.ascontiguousarray(g["L1_attn_W_dtw"].T),
            "wdtT": np.ascontiguousarray(g["L1_attn_W_dt"].T),
            "bdt": g["L1_attn_b_dt"],
            "wBT": np.ascontiguousarray(g["L1_attn_W_B"].T),
            "wCT": np.ascontiguousarray(g["L1_attn_W_C"].T),
            "ones": ones, "idn": idn}
    res = run_bass_kernel_spmd(nc_sp, [dict(base, x=x4[c]) for c in cores], core_ids=cores)
    delta = [res.results[c]["delta"] for c in cores]
    dxs = [res.results[c]["dx"] for c in cores]
    Bs = [res.results[c]["B"] for c in cores]
    Cs = [res.results[c]["C"] for c in cores]

    nc_scan = _PROG_CACHE.setdefault("scan", build_scan())
    seld = np.zeros((128, 16, 128), f32)
    for dd in range(128):
        for p_ in range(128):
            if dd == (p_ // 16) + 0:
                pass
    for k2 in range(16):
        for p_ in range(128):
            seld[k2 * 8 + p_ // 16, k2, p_] = 1.0
    seln = np.zeros((16, 128), f32)
    for p_ in range(128):
        seln[p_ % 16, p_] = 1.0
    sel8 = np.zeros((128, 8), f32)
    for p_ in range(128):
        sel8[p_, p_ // 16] = 1.0
    Bfull = np.stack([np.concatenate([Bs[b * 4 + q] for q in range(4)], 1) for b in range(2)])
    Cfull = np.stack([np.concatenate([Cs[b * 4 + q] for q in range(4)], 1) for b in range(2)])
    A_log = g["L1_attn_A_log"]
    in_maps = []
    for j in cores:
        dj = np.stack([np.concatenate([delta[b * 4 + q][128 * j:128 * (j + 1)]
                                       for q in range(4)], 1) for b in range(2)])
        dxj = np.stack([np.concatenate([dxs[b * 4 + q][128 * j:128 * (j + 1)]
                                        for q in range(4)], 1) for b in range(2)])
        Ap = np.empty((128, 16), f32)
        for k2 in range(16):
            for p_ in range(128):
                Ap[p_, k2] = A_log[128 * j + 8 * k2 + p_ // 16, p_ % 16]
        in_maps.append({"delta": np.ascontiguousarray(dj), "dx": np.ascontiguousarray(dxj),
                        "Bf": Bfull, "Cf": Cfull, "A_pairs": Ap, "seld": seld,
                        "seln": seln, "sel8": sel8})
    res = run_bass_kernel_spmd(nc_scan, in_maps, core_ids=cores)
    yj = [res.results[j]["y"] for j in cores]
    y_fm = []
    for c in cores:
        b, q = c // 4, c % 4
        yc = np.concatenate([yj[j][b, :, q * TOK:(q + 1) * TOK] for j in cores], 0)
        y_fm.append(np.ascontiguousarray(yc))

    nc_post = _PROG_CACHE.setdefault("l1post", build_l1post())
    base = {"Dp": g["L1_attn_Dp"], "lng": g["L1_attn_ln_g"], "lnb": g["L1_attn_ln_b"],
            "woutT": np.ascontiguousarray(g["L1_attn_W_out"].T), "ones": ones, "idn": idn}
    res = run_bass_kernel_spmd(nc_post, [dict(base, x=x4[c], yin=y_fm[c]) for c in cores],
                               core_ids=cores)
    x5 = [res.results[c]["y"] for c in cores]

    x6 = run_ffn1(x5)

    out = np.empty((2, 8192, 1024), f32)
    for c in cores:
        b, q = c // 4, c % 4
        out[b, q * TOK:(q + 1) * TOK, :] = x6[c].T
    return out


def build_fused(cfg):
    d_model, tok = cfg["D"], cfg["T"]
    n_e, hid = cfg["E"], cfg["H"]
    nc = bass.Bass("TRN2", target_bir_lowering=False)
    EI = lambda n, sh, dt=F32: nc.dram_tensor(n, sh, dt, kind="ExternalInput").ap()
    IN = lambda n, sh, dt=F32: nc.dram_tensor(n, sh, dt, kind="Internal").ap()
    x_in = EI("x", [d_model, tok])
    out = nc.dram_tensor("out", [d_model, tok], F32, kind="ExternalOutput").ap()
    ones = EI("ones", [128, 128])
    idn = EI("idn", [128, 128])
    io = {
        "rw1": EI("rw1", [d_model, hid]), "rb1": EI("rb1", [hid]),
        "rw2": EI("rw2", [hid, n_e]), "rb2": EI("rb2", [n_e]),
        "w1": EI("w1", [n_e, d_model, hid]), "b1": EI("b1", [n_e, hid]),
        "w2": EI("w2", [n_e, hid, d_model]), "b2": EI("b2", [n_e, d_model]),
        "wqk": EI("wqk", [d_model, 128]), "bqk": EI("bqk", [128]),
        "wv": EI("wv", [d_model, d_model]), "bv": EI("bv", [d_model]),
        "gamma": EI("gamma", [2, 128]), "beta": EI("beta", [2, 128]),
        "rel_emb": EI("rel_emb", [32, 1]),
        "sel1": EI("sel1", [1, 512]), "sel2": EI("sel2", [1, 512]),
        "sel128": EI("sel128", [1, 512]),
        "nbmask": EI("nbmask", [3, 32, 128, 256], BF16),
        "mask_d0": EI("mask_d0", [128, 256], BF16),
        "mask_m128": EI("mask_m128", [128, 256], BF16),
        "w": EI("w", [d_model, d_model]), "bias": EI("bias", [d_model]),
        "wdtwT": EI("wdtwT", [d_model, 64]), "wdtT": EI("wdtT", [64, d_model]),
        "bdt": EI("bdt", [d_model]),
        "wBT": EI("wBT", [d_model, 16]), "wCT": EI("wCT", [d_model, 16]),
        "A_pairs": EI("A_pairs", [128, 32]),
        "seld": EI("seld", [128, 16, 128]), "seln": EI("seln", [16, 128]),
        "sel8": EI("sel8", [128, 8]), "gidx": EI("gidx", [128, 8], I32),
        "gidx2": EI("gidx2", [128, 16], I32),
        "Dp": EI("Dp", [d_model]), "lng": EI("lng", [d_model]),
        "lnb": EI("lnb", [d_model]), "woutT": EI("woutT", [d_model, d_model]),
        "ones": ones, "idn": idn,
    }
    xs = IN("xs", [d_model, tok])
    qk = IN("qk_i", [128, tok])
    v = IN("v_i", [tok, d_model], BF16)
    qkg = IN("qkg", [4 * 128, tok])
    vg = IN("vg", [4 * tok, d_model], BF16)
    a2ain = IN("a2ain", [2048, tok], BF16)
    ag2 = IN("ag2", [4 * 2048, tok], BF16)
    bcin = IN("bcin", [32, tok])
    bcout = IN("bcout", [128, tok])
    ya = IN("ya", [8 * 128, tok])
    agy = IN("agy", [4 * 1024, tok])
    wdTs = IN("wdTs", [128, 8 * 1024], BF16)

    sem_cm = contextlib.ExitStack()
    sems = {s_: sem_cm.enter_context(nc.semaphore(s_)) for s_ in SEM_NAMES}
    base = {s_: 0 for s_ in SEM_NAMES}
    fc = dict(io)
    fc.update(nc=nc, sems=sems, base=base)

    build_ffn0(cfg, fctx=dict(fc, x=x_in, y=xs))
    build_kv(tok, d_model, fctx=dict(fc, x=xs, qk_o=qk, v_o=v))
    build_attn(tok, d_model, fctx=dict(fc, x=xs, qk_own=qk, v_own=v, qkg=qkg, vg=vg))
    build_ffn0(cfg, fctx=dict(fc, x=xs, y=xs))
    build_ffn1(tok, d_model, fctx=dict(fc, x=xs, y=xs, wdT_dram=wdTs), wmode="save")
    build_scanprep(tok, d_model, fctx=dict(fc, x=xs, delta_o=a2ain, dx_o=a2ain,
                                           B_o=bcin[0:16, :], C_o=bcin[16:32, :],
                                           a2a=True))
    build_scan(fctx=dict(fc, a2ain=a2ain, ag2=ag2, bcin=bcin,
                         bcout=bcout, ya=ya))
    build_l1post(tok, d_model, fctx=dict(fc, x=xs, ya=ya, agy=agy))
    build_ffn1(tok, d_model, fctx=dict(fc, x=xs, y=out, wdT_dram=wdTs), wmode="load")
    sem_cm.close()
    return nc


def kernel(**inputs):
    f32 = np.float32
    g = {k: np.ascontiguousarray(np.asarray(v, f32)) for k, v in inputs.items()}
    x0 = g["L0_attn_x"]
    ones = np.ones((128, 128), f32)
    idn = np.eye(128, dtype=f32)
    cores = list(range(N_CORES))
    TOK = 2048
    cfg = {"D": 1024, "H": 4096, "E": 8, "T": TOK, "MT": 1024}
    if "fused" not in _PROG_CACHE:
        _PROG_CACHE["fused"] = build_fused(cfg)
    nc = _PROG_CACHE["fused"]
    mask_d0, mask_m128, nbm = _attn_consts()
    seld = np.zeros((128, 16, 128), f32)
    for k2 in range(16):
        for p_ in range(128):
            seld[k2 * 8 + p_ // 16, k2, p_] = 1.0
    seln = np.zeros((16, 128), f32)
    sel8 = np.zeros((128, 8), f32)
    for p_ in range(128):
        seln[p_ % 16, p_] = 1.0
        sel8[p_, p_ // 16] = 1.0
    A_log = g["L1_attn_A_log"]
    base = {
        "rw1": g["L0_ffn_rw1"], "rb1": g["L0_ffn_rb1"], "rw2": g["L0_ffn_rw2"],
        "rb2": g["L0_ffn_rb2"], "w1": g["L0_ffn_w1"], "b1": g["L0_ffn_b1"],
        "w2": g["L0_ffn_w2"], "b2": g["L0_ffn_b2"],
        "wqk": g["L0_attn_Wqk"], "bqk": g["L0_attn_bqk"], "wv": g["L0_attn_Wv"],
        "bv": g["L0_attn_bv"], "gamma": g["L0_attn_gamma"], "beta": g["L0_attn_beta"],
        "rel_emb": g["L0_attn_rel_emb"],
        "nbmask": nbm, "mask_d0": mask_d0, "mask_m128": mask_m128,
        "w": g["L1_ffn_weight"], "bias": g["L1_ffn_bias"],
        "wdtwT": np.ascontiguousarray(g["L1_attn_W_dtw"].T),
        "wdtT": np.ascontiguousarray(g["L1_attn_W_dt"].T),
        "bdt": g["L1_attn_b_dt"],
        "wBT": np.ascontiguousarray(g["L1_attn_W_B"].T),
        "wCT": np.ascontiguousarray(g["L1_attn_W_C"].T),
        "seld": seld, "seln": seln, "sel8": sel8,
        "Dp": g["L1_attn_Dp"], "lng": g["L1_attn_ln_g"], "lnb": g["L1_attn_ln_b"],
        "woutT": np.ascontiguousarray(g["L1_attn_W_out"].T),
        "ones": ones, "idn": idn,
    }
    in_maps = []
    for c in cores:
        b, q = c // 4, c % 4
        s1, s2, s128 = _attn_sels(q)
        Ap = np.empty((128, 32), f32)
        for dblk in range(2):
            dwin = 256 * q + 128 * dblk
            for k2 in range(16):
                for p_ in range(128):
                    Ap[p_, dblk * 16 + k2] = A_log[dwin + 8 * k2 + p_ // 16, p_ % 16]
        dd_ = np.arange(1024, dtype=np.int32)
        rows = (dd_ // 256) * 1024 + q * 256 + (dd_ % 256)
        gidx = rows.reshape(8, 128).T.astype(np.int32)
        g2a = np.empty((128, 8), np.int32)
        for dblk in range(2):
            for q_ in range(4):
                g2a[:, dblk * 4 + q_] = (2048 * q_ + 512 * q + 128 * dblk
                                         + np.arange(128))
        gidx2 = np.concatenate([g2a, g2a + 256], axis=1).astype(np.int32)
        in_maps.append(dict(base, x=np.ascontiguousarray(x0[b, q * TOK:(q + 1) * TOK, :].T),
                            sel1=s1, sel2=s2, sel128=s128, A_pairs=Ap, gidx=gidx,
                            gidx2=gidx2))
    res = run_bass_kernel_spmd(nc, in_maps, core_ids=cores)
    outp = np.empty((2, 8192, 1024), f32)
    for c in cores:
        b, q = c // 4, c % 4
        outp[b, q * TOK:(q + 1) * TOK, :] = res.results[c]["out"].T
    return outp


# version 15
# speedup vs baseline: 1.0802x; 1.0802x over previous
import contextlib

import numpy as np
import ml_dtypes

import concourse.bass as bass
import concourse.mybir as mybir
from concourse.bass_utils import run_bass_kernel_spmd

F32 = mybir.dt.float32
BF16 = mybir.dt.bfloat16
F32R = mybir.dt.float32r
I32 = mybir.dt.int32
AF = mybir.ActivationFunctionType
ALU = mybir.AluOpType
AX = mybir.AxisListType

BF16NP = ml_dtypes.bfloat16

N_CORES = 8
EPS = 1e-5

SEM_NAMES = ("ld", "boot", "wa", "wb", "ra", "rb", "va", "vb", "mm", "ev", "dv", "stq", "cc",
             "gp")


class Emit:

    def __init__(self, env, eng, sems, base=None):
        self.env = env
        self.eng = eng
        self.sems = sems
        self.c = dict(base) if base else {k: 0 for k in sems}
        self.state = {}

    def on(self, name):
        return self.env == name

    def tag(self, ins, sem, n=1):
        v = self.c[sem] = self.c[sem] + n
        if ins is not None:
            ins.then_inc(self.sems[sem], n)
        return v

    def wait(self, env, sem, val):
        if self.env == env and val > 0:
            self.eng.wait_ge(self.sems[sem], val)

    def barrier(self):
        for s_ in self.sems:
            if self.c[s_] > 0:
                self.eng.wait_ge(self.sems[s_], self.c[s_])


def _ceil(a, b):
    return (a + b - 1) // b


_SECN = [0]


def _mk_alloc(nc, st):
    pfx = f"s{_SECN[0]}_"
    _SECN[0] += 1
    sb = lambda name, shape, dt: st.enter_context(nc.sbuf_tensor(pfx + name, shape, dt))
    psa = lambda name, shape, dt=F32: st.enter_context(nc.psum_tensor(pfx + name, shape, dt))
    return sb, psa


def emit_ln(p, h, xt, xln, nkc, mt, d_model, ps_stat, ps_bc, gb=None, xln2=None):
    inv_d = 1.0 / d_model
    for it in range(mt // 512):
        ts = slice(it * 512, (it + 1) * 512)
        p.wait("vector", "dv", p.state.get("ln_prev_dv", 0))
        dv = p.tag(p.eng.tensor_copy(h.cb[:], xt[:, :, ts]) if p.on("vector") else None, "dv")
        ev = p.tag(p.eng.activation(h.sqb[:], xt[:, :, ts], AF.Square)
                   if p.on("scalar") else None, "ev")
        p.wait("tensor", "dv", dv)
        p.wait("tensor", "ev", ev)
        mm = None
        for kc in range(nkc):
            if p.on("tensor"):
                p.eng.matmul(ps_stat[0:1, 0, :], h.ones_col[:, 0:1],
                             h.cb[:, kc, :], start=(kc == 0), stop=(kc == nkc - 1))
                mm = p.eng.matmul(ps_stat[0:1, 1, :], h.ones_col[:, 0:1],
                                  h.sqb[:, kc, :], start=(kc == 0), stop=(kc == nkc - 1))
        mmv = p.tag(mm, "mm")
        r_mu = h.row_f32[0:1, 0:512]
        r_var = h.row_f32[0:1, 512:1024]
        r_mu2 = h.row_f32[0:1, 1024:1536]
        r_var2 = h.row_f32[0:1, 1024:1536]
        r_std = h.row_f32[0:1, 512:1024]
        r_rstd = h.row_f32[0:1, 1536:2048]
        r_mr = h.row_f32[0:1, 0:512]
        p.wait("scalar", "mm", mmv)
        p.wait("scalar", "dv", p.state.get("ln_prev_dv", 0))
        if p.on("scalar"):
            p.eng.activation(r_mu, ps_stat[0:1, 0, :], AF.Copy, scale=inv_d)
        e1 = p.tag(p.eng.activation(r_var, ps_stat[0:1, 1, :], AF.Copy, scale=inv_d)
                   if p.on("scalar") else None, "ev")
        p.wait("scalar", "ev", e1)
        ev2 = p.tag(p.eng.activation(r_mu2, r_mu, AF.Square) if p.on("scalar") else None, "ev")
        p.wait("vector", "ev", ev2)
        dvv = p.tag(p.eng.tensor_sub(r_var2, r_var, r_mu2) if p.on("vector") else None, "dv")
        p.wait("scalar", "dv", dvv)
        ev3 = p.tag(p.eng.activation(r_std, r_var2, AF.Sqrt, bias=h.epsc[0:1, 0:1])
                    if p.on("scalar") else None, "ev")
        p.wait("vector", "ev", ev3)
        d2 = p.tag(p.eng.reciprocal(r_rstd, r_std) if p.on("vector") else None, "dv")
        p.wait("vector", "dv", d2)
        d3 = p.tag(p.eng.tensor_mul(r_mr, r_mu, r_rstd) if p.on("vector") else None, "dv")
        if p.on("vector"):
            p.eng.tensor_copy(h.row_bf[0:1, 0:512], r_rstd)
        p.wait("vector", "dv", d3)
        dv2 = p.tag(p.eng.tensor_copy(h.row_bf[0:1, 512:1024], r_mr)
                    if p.on("vector") else None, "dv")
        p.wait("tensor", "dv", dv2)
        mm2 = None
        if p.on("tensor"):
            p.eng.matmul(ps_bc[:, 0, :], h.ones_row[0:1, :], h.row_bf[0:1, 0:512],
                         start=True, stop=True)
            mm2 = p.eng.matmul(ps_bc[:, 1, :], h.ones_row[0:1, :], h.row_bf[0:1, 512:1024],
                               start=True, stop=True)
        mmv2 = p.tag(mm2, "mm")
        p.wait("vector", "mm", mmv2)
        d5 = p.tag(p.eng.tensor_mul(h.cb[:], xt[:, :, ts],
                                    ps_bc[:, 0:1, :].broadcast_to([128, nkc, 512]))
                   if p.on("vector") else None, "dv")
        p.wait("vector", "dv", d5)
        if gb is None:
            dv3 = p.tag(p.eng.tensor_sub(xln[:, :, ts], h.cb[:],
                                         ps_bc[:, 1:2, :].broadcast_to([128, nkc, 512]))
                        if p.on("vector") else None, "dv")
        else:
            d6 = p.tag(p.eng.tensor_sub(h.cb[:], h.cb[:],
                                        ps_bc[:, 1:2, :].broadcast_to([128, nkc, 512]))
                       if p.on("vector") else None, "dv")
            p.wait("vector", "dv", d6)
            dv3 = None
            for kc in range(nkc):
                dv3 = p.tag(p.eng.tensor_scalar(xln[:, kc, ts], h.cb[:, kc, :],
                                                gb[0][:, kc:kc + 1], gb[1][:, kc:kc + 1],
                                                ALU.mult, ALU.add)
                            if p.on("vector") else None, "dv")
        if xln2 is not None:
            p.wait("vector", "dv", dv3)
            dv3 = p.tag(p.eng.tensor_copy(xln2[:, :, ts], xln[:, :, ts])
                        if p.on("vector") else None, "dv")
        p.state["ln_prev_dv"] = dv3
        p.wait("tensor", "dv", dv3)


def build_ffn0(cfg, fctx=None):
    d_model, hid, n_e, tok, mt = cfg["D"], cfg["H"], cfg["E"], cfg["T"], cfg["MT"]
    nkc = d_model // 128
    nhc = hid // 128
    nh2 = hid // 512
    ntc = mt // 128
    nmt = tok // mt
    n512 = mt // 512
    ndh = _ceil(d_model, 512)
    dws = [min(512, d_model - i * 512) for i in range(ndh)]

    if fctx is None:
        nc = bass.Bass("TRN2", target_bir_lowering=False)
        x = nc.dram_tensor("x", [d_model, tok], F32, kind="ExternalInput").ap()
        rw1 = nc.dram_tensor("rw1", [d_model, hid], F32, kind="ExternalInput").ap()
        rb1 = nc.dram_tensor("rb1", [hid], F32, kind="ExternalInput").ap()
        rw2 = nc.dram_tensor("rw2", [hid, n_e], F32, kind="ExternalInput").ap()
        rb2 = nc.dram_tensor("rb2", [n_e], F32, kind="ExternalInput").ap()
        w1 = nc.dram_tensor("w1", [n_e, d_model, hid], F32, kind="ExternalInput").ap()
        b1 = nc.dram_tensor("b1", [n_e, hid], F32, kind="ExternalInput").ap()
        w2 = nc.dram_tensor("w2", [n_e, hid, d_model], F32, kind="ExternalInput").ap()
        b2 = nc.dram_tensor("b2", [n_e, d_model], F32, kind="ExternalInput").ap()
        ones_in = nc.dram_tensor("ones", [128, 128], F32, kind="ExternalInput").ap()
        idn_in = nc.dram_tensor("idn", [128, 128], F32, kind="ExternalInput").ap()
        y = nc.dram_tensor("y", [d_model, tok], F32, kind="ExternalOutput").ap()
        base = None
    else:
        nc, sems, base = fctx["nc"], fctx["sems"], fctx["base"]
        x = fctx["x"]; y = fctx["y"]; rw1 = fctx["rw1"]; rb1 = fctx["rb1"]
        rw2 = fctx["rw2"]; rb2 = fctx["rb2"]; w1 = fctx["w1"]; b1 = fctx["b1"]
        w2 = fctx["w2"]; b2 = fctx["b2"]; ones_in = fctx["ones"]; idn_in = fctx["idn"]

    st = contextlib.ExitStack()
    sb, psa = _mk_alloc(nc, st)

    class H:
        pass

    h = H()
    h.xt = sb("xt", [128, nkc, mt], F32)
    h.xln = sb("xln", [128, nkc, mt], BF16)
    h.cb = None
    h.sqb = None
    h.row_f32 = sb("row_f32", [1, 2048], F32)
    h.epsc = sb("epsc", [1, 1], F32)
    h.row_bf = sb("row_bf", [1, 1024], BF16)
    h.ones_col = sb("ones_col", [128, 1], BF16)
    h.ones_row = sb("ones_row", [1, 128], BF16)
    h.idn_f = sb("idn_f", [128, 128], F32)
    h.silu = sb("silu", [128, nhc, mt], BF16)
    _sf = h.silu[:].rearrange("p a b -> p (a b)")
    h.cb = _sf[:, 0:nkc * 512].rearrange("p (c w) -> p c w", w=512)
    h.sqb = _sf[:, nkc * 512:2 * nkc * 512].rearrange("p (c w) -> p c w", w=512)
    h.w1b = [sb(f"w1b{i}", [128, nkc, 512], BF16) for i in range(2)]
    h.w2r = [sb(f"w2r{i}", [128, 4, 512], BF16) for i in range(2)]
    h.rw2b = sb("rw2b", [128, nhc, n_e], BF16)
    h.g = [sb(f"g{i}", [128, mt], BF16) for i in range(2)]
    h.acc = sb("acc", [128, ntc, d_model], F32)
    h.we = sb("we", [128, ntc, n_e], F32)
    h.webf = sb("webf", [128, ntc, n_e], F32)
    h.weT = sb("weT", [n_e, mt], BF16)
    h.b1c = sb("b1c", [128, nhc], F32)
    h.rb1c = sb("rb1c", [128, nhc], F32)
    h.rb2c = sb("rb2c", [n_e, 1], F32)
    h.b2bf = sb("b2bf", [n_e, d_model], BF16)
    h.lg = sb("lg", [n_e, mt], F32)
    h.lgt = sb("lgt", [128, n_e], F32)
    h.lgm = sb("lgm", [128, 8 + 5 * n_e], F32)
    h.h2e = [sb(f"h2e{i}", [128, 512], BF16) for i in range(2)]

    PS = [psa(f"PS{i}", [128, 2, 512]) for i in range(3)]
    PSX = psa("PSX", [128, 2, 512])

    def psum_tc(tc):
        return (PS[tc // 2] if tc // 2 < 3 else PSX)[:, tc % 2, :]

    sem_cm = contextlib.ExitStack()
    if fctx is None:
        sems = {}
        for s_ in SEM_NAMES:
            sems[s_] = sem_cm.enter_context(nc.semaphore(s_))

    def program(p):
        ep0 = p.tag(p.eng.memset(h.epsc[:], float(EPS)) if p.on("vector") else None, "dv")
        p.wait("scalar", "dv", ep0)
        p.tag(p.eng.dma_start(out=h.ones_col[:], in_=ones_in[:, 0:1], allow_slow_non_contiguous=True) if p.on("gpsimd") else None,
              "boot", 16)
        p.tag(p.eng.dma_start(out=h.ones_row[:], in_=ones_in[0:1, :]) if p.on("gpsimd") else None,
              "boot", 16)
        p.tag(p.eng.dma_start(out=h.rw2b[:], in_=rw2.rearrange("(c p) e -> p c e", p=128),
                              allow_slow_non_contiguous=True)
              if p.on("gpsimd") else None, "boot", 16)
        bootv = p.tag(p.eng.dma_start(out=h.rw2b[:], in_=rw2.rearrange("(c p) e -> p c e", p=128))
              if False else None, "boot", 0)
        bootv = p.c["boot"]
        p.tag(p.eng.dma_start(out=h.idn_f[:], in_=idn_in[:]) if p.on("sync") else None, "ld", 16)
        p.tag(p.eng.dma_start(out=h.rb1c[:], in_=rb1.rearrange("(c p) -> p c", p=128),
                              allow_slow_non_contiguous=True)
              if p.on("sync") else None, "ld", 16)
        p.tag(p.eng.dma_start(out=h.rb2c[:, 0], in_=rb2[:], allow_slow_non_contiguous=True) if p.on("sync") else None, "ld", 16)
        ldv0 = p.tag(p.eng.dma_start(out=h.b2bf[:], in_=b2[:, :]) if p.on("gpsimd") else None,
                     "boot", 16)
        bootv = p.c["boot"]
        p.wait("scalar", "boot", bootv)
        p.wait("scalar", "ld", p.c["ld"])
        p.state["bootv"] = bootv
        evb2 = 0
        wait_w1_free = [0, 0]
        res_mm_prev = 0

        for mti in range(nmt):
            p.barrier()
            tsl = slice(mti * mt, (mti + 1) * mt)
            ldx = p.tag(p.eng.dma_start(out=h.xt[:],
                                        in_=x[:, tsl].rearrange("(c p) t -> p c t", p=128))
                        if p.on("sync") else None, "ld", 16)
            p.wait("vector", "ld", ldx)
            p.wait("scalar", "ld", ldx)
            p.wait("tensor", "boot", p.state["bootv"])
            p.wait("tensor", "ld", ldx)
            emit_ln(p, h, h.xt, h.xln, nkc, mt, d_model, PSX, PS[2])

            p.state[("w1ps", 0)] = p.state[("w1ps", 1)] = p.c["ev"]
            gfree = [0, 0]
            pend_g = []

            def flush_g():
                gbuf_, hidx_, evv_ = pend_g.pop(0)
                p.wait("tensor", "ev", evv_)
                mv = None
                for half in range(n512):
                    if p.on("tensor"):
                        mv = p.eng.matmul(PSX[0:n_e, half, :], h.rw2b[:, hidx_, :],
                                          h.g[gbuf_][:, half * 512:(half + 1) * 512],
                                          start=(hidx_ == 0), stop=(hidx_ == nhc - 1))
                gfree[gbuf_] = p.tag(mv, "mm")

            for hc2 in range(nh2):
                buf = hc2 % 2
                p.wait("gpsimd", "mm", wait_w1_free[buf])
                wsem = "wa" if buf == 0 else "wb"
                wv = p.tag(p.eng.dma_start(
                    out=h.w1b[buf][:],
                    in_=rw1.rearrange("(c p) h -> p c h", p=128)[:, :, hc2 * 512:(hc2 + 1) * 512])
                    if p.on("gpsimd") else None, wsem, 16)
                for hsub in range(4):
                    pa = PS[hsub % 2]
                    gbuf = hsub % 2
                    hidx = hc2 * 4 + hsub
                    p.wait("tensor", wsem, wv)
                    p.wait("tensor", "ev", p.state[("w1ps", hsub % 2)])
                    p.wait("tensor", "mm", gfree[gbuf])
                    mmv = None
                    for half in range(n512):
                        for kc in range(nkc):
                            if p.on("tensor"):
                                mmv = p.eng.matmul(
                                    pa[:, half, :],
                                    h.w1b[buf][:, kc, hsub * 128:(hsub + 1) * 128],
                                    h.xln[:, kc, half * 512:(half + 1) * 512],
                                    start=(kc == 0), stop=(kc == nkc - 1))
                    mmc = p.tag(mmv, "mm")
                    p.wait("scalar", "mm", mmc)
                    evv = p.tag(p.eng.activation(h.g[gbuf][:].rearrange("p (a b) -> p a b", a=n512),
                                                 pa[:, 0:n512, :], AF.Silu,
                                                 bias=h.rb1c[:, hidx:hidx + 1])
                                if p.on("scalar") else None, "ev")
                    p.state[("w1ps", hsub % 2)] = evv
                    pend_g.append((gbuf, hidx, evv))
                    if len(pend_g) > 1:
                        flush_g()
                wait_w1_free[buf] = p.c["mm"]
            while pend_g:
                flush_g()
            lgv = p.c["mm"]
            p.wait("scalar", "mm", lgv)
            evlg = p.tag(p.eng.activation(h.lg[:].rearrange("p (a b) -> p a b", a=n512),
                                          PSX[0:n_e, 0:n512, :], AF.Identity,
                                          bias=h.rb2c[:, 0:1]) if p.on("scalar") else None, "ev")
            for tcg in range(ntc):
                tgs = slice(tcg * 128, (tcg + 1) * 128)
                p.wait("tensor", "ev", evlg)
                p.wait("tensor", "dv", p.c["dv"])
                mt1 = p.tag(p.eng.transpose(PS[2][0:128, 0, 0:n_e], h.lg[:, tgs],
                                            h.idn_f[0:n_e, 0:n_e])
                            if p.on("tensor") else None, "mm")
                p.wait("scalar", "mm", mt1)
                evt = p.tag(p.eng.copy(h.lgt[:, :], PS[2][:, 0, 0:n_e])
                            if p.on("scalar") else None, "ev")
                p.wait("vector", "ev", evt)
                m1 = h.lgm[:, 0:1]
                m2 = h.lgm[:, 1:2]
                p1 = h.lgm[:, 2:3]
                dd = h.lgm[:, 3:4]
                p2 = h.lgm[:, 4:5]
                eq1 = h.lgm[:, 8:8 + n_e]
                eq2 = h.lgm[:, 8 + n_e:8 + 2 * n_e]
                eq3 = h.lgm[:, 8 + 2 * n_e:8 + 3 * n_e]
                wA = h.lgm[:, 8 + 3 * n_e:8 + 4 * n_e]
                wB = h.lgm[:, 8 + 4 * n_e:8 + 5 * n_e]
                da = p.tag(p.eng.reduce_max(m1, h.lgt[:], axis=AX.X)
                           if p.on("vector") else None, "dv")
                p.wait("vector", "dv", da)
                db = p.tag(p.eng.tensor_single_scalar(eq1, h.lgt[:], m1, ALU.is_equal)
                           if p.on("vector") else None, "dv")
                p.wait("vector", "dv", db)
                dc = p.tag(p.eng.scalar_tensor_tensor(eq2, eq1, -1e30, h.lgt[:], ALU.mult, ALU.add)
                           if p.on("vector") else None, "dv")
                p.wait("vector", "dv", dc)
                dm = p.tag(p.eng.reduce_max(m2, eq2, axis=AX.X)
                           if p.on("vector") else None, "dv")
                p.wait("vector", "dv", dm)
                if p.on("vector"):
                    p.eng.tensor_single_scalar(eq3, eq2, m2, ALU.is_equal)
                dv1 = p.tag(p.eng.tensor_sub(dd, m2, m1) if p.on("vector") else None, "dv")
                p.wait("scalar", "dv", dv1)
                if p.on("scalar"):
                    p.eng.activation(p1, dd, AF.Sigmoid, scale=-1.0)
                ev5 = p.tag(p.eng.activation(p2, dd, AF.Sigmoid) if p.on("scalar") else None, "ev")
                p.wait("vector", "ev", ev5)
                if p.on("vector"):
                    p.eng.tensor_scalar(wA, eq1, p1, 0.5, ALU.mult, ALU.mult)
                dww = p.tag(p.eng.tensor_scalar(wB, eq3, p2, 0.5, ALU.mult, ALU.mult)
                            if p.on("vector") else None, "dv")
                p.wait("vector", "dv", dww)
                dwe = p.tag(p.eng.tensor_add(h.we[:, tcg, :], wA, wB)
                            if p.on("vector") else None, "dv")
                p.wait("vector", "dv", dwe)
                dv2 = p.tag(p.eng.tensor_copy(h.webf[:, tcg, :], h.we[:, tcg, :])
                            if p.on("vector") else None, "dv")
                p.wait("tensor", "dv", dv2)
                mt2 = p.tag(p.eng.transpose(PS[2][0:n_e, 1, 0:128], h.webf[:, tcg, :],
                                            h.idn_f[:])
                            if p.on("tensor") else None, "mm")
                p.wait("scalar", "mm", mt2)
                p.tag(p.eng.activation(h.weT[:, tgs], PS[2][0:n_e, 1, 0:128], AF.Copy, scale=1.0)
                      if p.on("scalar") else None, "ev")
            prev_init = 0
            for tcg in range(ntc):
                pa = PS[tcg % 2]
                p.wait("tensor", "ev", p.c["ev"])
                p.wait("tensor", "mm", res_mm_prev)
                mv = None
                for half in range(ndh):
                    if p.on("tensor"):
                        mv = p.eng.matmul(pa[:, half, 0:dws[half]],
                                          h.weT[:, tcg * 128:(tcg + 1) * 128],
                                          h.b2bf[:, half * 512:half * 512 + dws[half]],
                                          start=True, stop=True)
                mmc = p.tag(mv, "mm")
                p.wait("scalar", "mm", mmc)
                p.wait("scalar", "ev", prev_init)
                prev_init = p.tag(
                    p.eng.copy(h.acc[:, tcg, :].rearrange("p (a b) -> p a b", a=ndh)
                               if ndh > 1 else h.acc[:, tcg, :],
                               pa[:, 0:ndh, 0:dws[0]] if ndh > 1 else pa[:, 0, 0:dws[0]])
                    if p.on("scalar") else None, "ev")
            p.state[("w1ps", 0)] = p.state[("w1ps", 1)] = p.c["ev"]

            h2efree = [0, 0]
            for e in range(n_e):
                p.wait("sync", "ev", p.c["ev"])
                ldb1 = p.tag(p.eng.dma_start(out=h.b1c[:],
                                             in_=b1[e, :].rearrange("(c p) -> p c", p=128),
                                             allow_slow_non_contiguous=True)
                             if p.on("sync") else None, "ld", 16)
                p.wait("sync", "ld", ldb1)
                p.wait("scalar", "ld", ldb1)
                for hc2 in range(nh2):
                    buf = hc2 % 2
                    p.wait("gpsimd", "mm", wait_w1_free[buf])
                    wsem = "wa" if buf == 0 else "wb"
                    wv = p.tag(p.eng.dma_start(
                        out=h.w1b[buf][:],
                        in_=w1[e].rearrange("(c p) h -> p c h", p=128)
                        [:, :, hc2 * 512:(hc2 + 1) * 512])
                        if p.on("gpsimd") else None, wsem, 16)
                    for hsub in range(4):
                        pa = PS[hsub % 2]
                        hidx = hc2 * 4 + hsub
                        p.wait("tensor", wsem, wv)
                        p.wait("tensor", "ev", p.state[("w1ps", hsub % 2)])
                        mmv = None
                        for half in range(n512):
                            for kc in range(nkc):
                                if p.on("tensor"):
                                    mmv = p.eng.matmul(
                                        pa[:, half, :],
                                        h.w1b[buf][:, kc, hsub * 128:(hsub + 1) * 128],
                                        h.xln[:, kc, half * 512:(half + 1) * 512],
                                        start=(kc == 0), stop=(kc == nkc - 1))
                        mmc = p.tag(mmv, "mm")
                        p.wait("scalar", "mm", mmc)
                        evsil = p.tag(p.eng.activation(h.silu[:, hidx, :].rearrange("p (a b) -> p a b", a=n512),
                                                       pa[:, 0:n512, :], AF.Silu,
                                                       bias=h.b1c[:, hidx:hidx + 1])
                                      if p.on("scalar") else None, "ev")
                        p.state[("w1ps", hsub % 2)] = evsil
                    wait_w1_free[buf] = p.c["mm"]
                silu_ready = p.c["ev"]
                psum_free_ev = [p.c["ev"]] * ntc
                for dh in range(ndh):
                    dsl = slice(dh * 512, dh * 512 + dws[dh])
                    for hg in range(nhc // 4):
                        slot = hg % 2
                        rsem = "ra" if slot == 0 else "rb"
                        p.wait("gpsimd", "mm", p.state.get(("ring", slot), 0))
                        wv = p.tag(p.eng.dma_start(
                            out=h.w2r[slot][:, :, 0:dws[dh]],
                            in_=w2[e].rearrange("(g p) d -> p g d", p=128)
                            [:, hg * 4:(hg + 1) * 4, dsl])
                            if p.on("gpsimd") else None, rsem, 16)
                        p.wait("tensor", rsem, wv)
                        if hg == 0:
                            p.wait("tensor", "ev", silu_ready)
                        mv = None
                        for tc in range(ntc):
                            pt = psum_tc(tc)
                            if hg == 0:
                                p.wait("tensor", "ev", psum_free_ev[tc])
                            for hl in range(4):
                                hidx = hg * 4 + hl
                                if p.on("tensor"):
                                    mv = p.eng.matmul(
                                        pt[:, 0:dws[dh]],
                                        h.silu[:, hidx, tc * 128:(tc + 1) * 128],
                                        h.w2r[slot][:, hl, 0:dws[dh]],
                                        start=(hidx == 0), stop=(hidx == nhc - 1))
                        p.state[("ring", slot)] = p.tag(mv, "mm")
                    h2done = p.c["mm"]
                    for tc in range(ntc):
                        pt = psum_tc(tc)
                        sbuf_i = tc % 2
                        p.wait("scalar", "mm", h2done)
                        p.wait("scalar", "dv", h2efree[sbuf_i])
                        evv = p.tag(p.eng.activation(h.h2e[sbuf_i][:, 0:dws[dh]], pt[:, 0:dws[dh]],
                                                     AF.Identity, scale=h.we[:, tc, e:e + 1])
                                    if p.on("scalar") else None, "ev")
                        psum_free_ev[tc] = evv
                        p.wait("vector", "ev", evv)
                        p.wait("vector", "dv", p.state.get(("acc", tc, dh), 0))
                        h2efree[sbuf_i] = p.tag(
                            p.eng.tensor_add(h.acc[:, tc, dsl], h.acc[:, tc, dsl],
                                             h.h2e[sbuf_i][:, 0:dws[dh]])
                            if p.on("vector") else None, "dv")
                        p.state[("acc", tc, dh)] = h2efree[sbuf_i]
                p.state[("w1ps", 0)] = max(p.state[("w1ps", 0)],
                                           psum_free_ev[0], psum_free_ev[1])
                p.state[("w1ps", 1)] = max(p.state[("w1ps", 1)],
                                           psum_free_ev[2], psum_free_ev[3])

            accdone = p.c["dv"]
            for tc in range(ntc):
                for kc in range(nkc):
                    pa = PS[(tc * nkc + kc) % 2]
                    p.wait("tensor", "dv", accdone if (tc, kc) == (0, 0) else p.c["dv"])
                    p.wait("tensor", "ev", p.c["ev"])
                    mt3 = p.tag(p.eng.transpose(pa[:, 0, 0:128],
                                                h.acc[:, tc, kc * 128:(kc + 1) * 128],
                                                h.idn_f[:])
                                if p.on("tensor") else None, "mm")
                    p.wait("vector", "mm", mt3)
                    p.tag(p.eng.tensor_add(h.xt[:, kc, tc * 128:(tc + 1) * 128],
                                           h.xt[:, kc, tc * 128:(tc + 1) * 128],
                                           pa[:, 0, 0:128])
                          if p.on("vector") else None, "dv")
            res_mm_prev = p.c["mm"]
            resdone = p.c["dv"]
            p.wait("sync", "dv", resdone)
            stv = p.tag(p.eng.dma_start(out=y[:, tsl].rearrange("(c p) t -> p c t", p=128),
                                        in_=h.xt[:])
                        if p.on("sync") else None, "stq", 16)
            p.wait("sync", "stq", stv)
        p.wait("sync", "stq", p.c["stq"])

    _mk_prog(nc, sems, st, base)(program)
    sem_cm.close()
    st.close()
    return nc





def _mk_prog(nc, sems, st, base=None):
    def run(program):
        final = {}

        def go(env, eng):
            p = Emit(env, eng, sems, base)
            if base is not None:
                p.barrier()
            program(p)
            final.clear()
            final.update(p.c)

        with nc.Block() as block:
            @block.sync
            def _(eng):
                go("sync", eng)

            @block.gpsimd
            def _(eng):
                go("gpsimd", eng)

            @block.tensor
            def _(eng):
                go("tensor", eng)

            @block.scalar
            def _(eng):
                go("scalar", eng)

            @block.vector
            def _(eng):
                go("vector", eng)
        if base is not None:
            base.clear()
            base.update(final)
    return run


def _alloc_common(nc, st, h, idn=True):
    sb, _psa_unused = _mk_alloc(nc, st)
    h.row_f32 = sb("row_f32", [1, 2048], F32)
    h.row_bf = sb("row_bf", [1, 1024], BF16)
    h.ones_col = sb("ones_col", [128, 1], BF16)
    h.ones_row = sb("ones_row", [1, 128], BF16)
    h.idn_f = sb("idn_f", [128, 128], F32) if idn else None
    h.epsc = sb("epsc", [1, 1], F32)
    return sb


def _common_boot(p, h, ones_in, idn_in):
    ep0 = p.tag(p.eng.memset(h.epsc[:], float(EPS)) if p.on("vector") else None, "dv")
    p.wait("scalar", "dv", ep0)
    p.tag(p.eng.dma_start(out=h.ones_col[:], in_=ones_in[:, 0:1],
                          allow_slow_non_contiguous=True) if p.on("gpsimd") else None, "boot", 16)
    p.tag(p.eng.dma_start(out=h.ones_row[:], in_=ones_in[0:1, :]) if p.on("gpsimd") else None,
          "boot", 16)
    if h.idn_f is not None:
        p.tag(p.eng.dma_start(out=h.idn_f[:], in_=idn_in[:]) if p.on("sync") else None,
              "ld", 16)


def build_kv(tok=2048, d_model=1024, fctx=None):
    nkc = d_model // 128
    if fctx is None:
        nc = bass.Bass("TRN2", target_bir_lowering=False)
        x = nc.dram_tensor("x", [d_model, tok], F32, kind="ExternalInput").ap()
        wqk = nc.dram_tensor("wqk", [d_model, 128], F32, kind="ExternalInput").ap()
        bqk = nc.dram_tensor("bqk", [128], F32, kind="ExternalInput").ap()
        wv = nc.dram_tensor("wv", [d_model, d_model], F32, kind="ExternalInput").ap()
        bv = nc.dram_tensor("bv", [d_model], F32, kind="ExternalInput").ap()
        ones_in = nc.dram_tensor("ones", [128, 128], F32, kind="ExternalInput").ap()
        idn_in = nc.dram_tensor("idn", [128, 128], F32, kind="ExternalInput").ap()
        qk_o = nc.dram_tensor("qk", [128, tok], F32, kind="ExternalOutput").ap()
        v_o = nc.dram_tensor("v", [tok, d_model], BF16, kind="ExternalOutput").ap()
        base = None
    else:
        nc, sems, base = fctx["nc"], fctx["sems"], fctx["base"]
        x = fctx["x"]; wqk = fctx["wqk"]; bqk = fctx["bqk"]; wv = fctx["wv"]
        bv = fctx["bv"]; ones_in = fctx["ones"]; idn_in = fctx["idn"]
        qk_o = fctx["qk_o"]; v_o = fctx["v_o"]

    st = contextlib.ExitStack()

    class H:
        pass

    h = H()
    sb = _alloc_common(nc, st, h)
    _, psa = _mk_alloc(nc, st)
    h.xt = sb("xt", [128, nkc, tok], F32)
    h.xln = sb("xln", [128, nkc, tok], BF16)
    h.cb = sb("cb", [128, nkc, 512], BF16)
    h.sqb = sb("sqb", [128, nkc, 512], BF16)
    h.wqkb = sb("wqkb", [128, nkc, 128], BF16)
    h.wvb = [sb(f"wvb{i}", [128, nkc, 128], BF16) for i in range(2)]
    h.bqkc = sb("bqkc", [128, 1], F32)
    h.bvc = sb("bvc", [128, nkc], F32)
    h.qk = sb("qk_s", [128, tok], F32)
    h.idn_bf = sb("idn_bf", [128, 128], BF16)
    h.vfm = sb("vfm", [128, nkc, tok], BF16)
    h.vst = sb("vst", [128, d_model], BF16)
    PS0 = psa("PS0", [128, 2, 512])
    PSX = psa("PSX", [128, 2, 512])
    PS2 = psa("PS2", [128, 2, 512])
    PBT = psa("PBT", [128, 1024], BF16)
    sem_cm = contextlib.ExitStack()
    if fctx is None:
        sems = {}
        for s_ in SEM_NAMES:
            sems[s_] = sem_cm.enter_context(nc.semaphore(s_))

    def program(p):
        _common_boot(p, h, ones_in, idn_in)
        p.tag(p.eng.dma_start(out=h.wqkb[:], in_=wqk.rearrange("(c p) e -> p c e", p=128))
              if p.on("gpsimd") else None, "boot", 16)
        p.tag(p.eng.dma_start(out=h.idn_bf[:], in_=idn_in[:]) if p.on("gpsimd") else None,
              "boot", 16)
        bootv = p.c["boot"]
        p.tag(p.eng.dma_start(out=h.bqkc[:, 0], in_=bqk[:]) if p.on("sync") else None, "ld", 16)
        p.tag(p.eng.dma_start(out=h.bvc[:], in_=bv.rearrange("(c p) -> p c", p=128),
                              allow_slow_non_contiguous=True)
              if p.on("sync") else None, "ld", 16)
        ldx = p.tag(p.eng.dma_start(out=h.xt[:], in_=x.rearrange("(c p) t -> p c t", p=128))
                    if p.on("sync") else None, "ld", 16)
        p.barrier()
        emit_ln(p, h, h.xt, h.xln, nkc, tok, d_model, PSX, PS2)
        nt = tok // 512
        for tq in range(nt):
            pa = PS0[:, tq % 2:tq % 2 + 1, :]
            p.wait("tensor", "ev", p.c["ev"])
            mv = None
            for kc in range(nkc):
                if p.on("tensor"):
                    mv = p.eng.matmul(pa[:, 0, :], h.wqkb[:, kc, :],
                                      h.xln[:, kc, tq * 512:(tq + 1) * 512],
                                      start=(kc == 0), stop=(kc == nkc - 1))
            mmc = p.tag(mv, "mm")
            p.wait("scalar", "mm", mmc)
            p.tag(p.eng.activation(h.qk[:, tq * 512:(tq + 1) * 512], pa[:, 0, :], AF.Silu,
                                   bias=h.bqkc[:, 0:1]) if p.on("scalar") else None, "ev")
        qkdone = p.c["ev"]
        p.wait("sync", "ev", qkdone)
        p.tag(p.eng.dma_start(out=qk_o[:], in_=h.qk[:]) if p.on("sync") else None, "stq", 16)
        for vc in range(nkc):
            buf = vc % 2
            p.wait("gpsimd", "mm", p.state.get(("wv", buf), 0))
            wsem = "wa" if buf == 0 else "wb"
            wv_ = p.tag(p.eng.dma_start(
                out=h.wvb[buf][:],
                in_=wv.rearrange("(c p) e -> p c e", p=128)[:, :, vc * 128:(vc + 1) * 128])
                if p.on("gpsimd") else None, wsem, 16)
            mv = None
            for tq in range(nt):
                pa = PS0[:, (vc * nt + tq) % 2:(vc * nt + tq) % 2 + 1, :]
                p.wait("tensor", wsem, wv_)
                p.wait("tensor", "ev", p.c["ev"])
                for kc in range(nkc):
                    if p.on("tensor"):
                        mv = p.eng.matmul(pa[:, 0, :], h.wvb[buf][:, kc, :],
                                          h.xln[:, kc, tq * 512:(tq + 1) * 512],
                                          start=(kc == 0), stop=(kc == nkc - 1))
                mmc = p.tag(mv, "mm")
                p.wait("scalar", "mm", mmc)
                p.tag(p.eng.activation(h.vfm[:, vc, tq * 512:(tq + 1) * 512], pa[:, 0, :],
                                       AF.Silu, bias=h.bvc[:, vc:vc + 1])
                      if p.on("scalar") else None, "ev")
            p.state[("wv", buf)] = p.c["mm"]
        vdone = p.c["ev"]
        for tc in range(tok // 128):
            for vc in range(nkc):
                pslot = PBT[:, (vc % 2) * 128:(vc % 2) * 128 + 128]
                p.wait("tensor", "ev", vdone if (tc, vc) == (0, 0) else p.c["ev"])
                mt1 = p.tag(p.eng.transpose(pslot, h.vfm[:, vc, tc * 128:(tc + 1) * 128],
                                            h.idn_bf[:])
                            if p.on("tensor") else None, "mm")
                p.wait("scalar", "mm", mt1)
                p.wait("scalar", "stq", p.state.get("vstq", 0))
                p.tag(p.eng.copy(h.vst[:, vc * 128:(vc + 1) * 128], pslot)
                      if p.on("scalar") else None, "ev")
            p.wait("sync", "ev", p.c["ev"])
            stv = p.tag(p.eng.dma_start(out=v_o[tc * 128:(tc + 1) * 128, :], in_=h.vst[:])
                        if p.on("sync") else None, "stq", 16)
            p.wait("sync", "stq", stv)
            p.state["vstq"] = stv
        p.wait("sync", "stq", p.c["stq"])

    _mk_prog(nc, sems, st, base)(program)
    sem_cm.close()
    st.close()
    return nc


ERF_A = 0.7978845608028654
ERF_B = -0.5641895835477563
SQ128 = 11.313708498984761
NEG_BIG = -30.0
ERF_FN = AF.Erf


def build_attn(tok=2048, d_model=1024, seq=8192, qt=256, fctx=None):
    nkc = d_model // 128
    nqi = tok // qt
    if fctx is None:
        nc = bass.Bass("TRN2", target_bir_lowering=False)
        x = nc.dram_tensor("x", [d_model, tok], F32, kind="ExternalInput").ap()
        qk_own = nc.dram_tensor("qk_own", [128, tok], F32, kind="ExternalInput").ap()
        qk_full = nc.dram_tensor("qk_full", [128, seq], F32, kind="ExternalInput").ap()
        v_full = nc.dram_tensor("v_full", [seq, d_model], BF16, kind="ExternalInput").ap()
        v_own = nc.dram_tensor("v_own", [tok, d_model], BF16, kind="ExternalInput").ap()
        gamma = nc.dram_tensor("gamma", [2, 128], F32, kind="ExternalInput").ap()
        beta = nc.dram_tensor("beta", [2, 128], F32, kind="ExternalInput").ap()
        rel_emb = nc.dram_tensor("rel_emb", [32, 1], F32, kind="ExternalInput").ap()
        sel1 = nc.dram_tensor("sel1", [1, 512], F32, kind="ExternalInput").ap()
        sel2 = nc.dram_tensor("sel2", [1, 512], F32, kind="ExternalInput").ap()
        sel128 = nc.dram_tensor("sel128", [1, 512], F32, kind="ExternalInput").ap()
        nbmask = nc.dram_tensor("nbmask", [3, 32, 128, 256], BF16, kind="ExternalInput").ap()
        mask_d0 = nc.dram_tensor("mask_d0", [128, 256], BF16, kind="ExternalInput").ap()
        mask_m128 = nc.dram_tensor("mask_m128", [128, 256], BF16, kind="ExternalInput").ap()
        ones_in = nc.dram_tensor("ones", [128, 128], F32, kind="ExternalInput").ap()
        y = nc.dram_tensor("y", [d_model, tok], F32, kind="ExternalOutput").ap()
        base = None
        qkg = vg = qk_cc = v_cc = None
    else:
        nc, sems, base = fctx["nc"], fctx["sems"], fctx["base"]
        x = fctx["x"]; qk_own = fctx["qk_own"]; v_own = fctx["v_own"]
        qk_full = None; qkg = fctx["qkg"]; vg = fctx["vg"]; v_full = fctx["vg"]
        qk_cc = fctx["qk_own"]; v_cc = fctx["v_own"]
        gamma = fctx["gamma"]; beta = fctx["beta"]; rel_emb = fctx["rel_emb"]
        sel1 = fctx["sel1"]; sel2 = fctx["sel2"]; sel128 = fctx["sel128"]
        nbmask = fctx["nbmask"]; mask_d0 = fctx["mask_d0"]; mask_m128 = fctx["mask_m128"]
        ones_in = fctx["ones"]; y = fctx["x"]

    st = contextlib.ExitStack()
    sb, psa = _mk_alloc(nc, st)

    class H:
        pass

    h = H()
    h.xt = sb("xt", [128, nkc, tok], F32)
    h.ones_row = sb("ones_row", [1, 128], BF16)
    h.q = sb("q", [128, tok], BF16)
    h.k = sb("k", [128, seq], BF16)
    h.kown = sb("kown", [128, tok], BF16)
    h.gb = sb("gb", [128, 4], F32)
    h.relrow = sb("relrow", [1, 32], F32)
    h.relbc = sb("relbc", [128, 32], F32)
    h.relbc2 = sb("relbc2", [128, 32], F32)
    h.selbf = sb("selbf", [1, 3 * 512], BF16)
    h.nbm = [sb(f"nbm{i}", [128, 256], BF16) for i in range(2)]
    h.nb = sb("nb", [128, 3 * 256 + 512], F32)
    h.bias_cols = sb("bias_cols", [128, 512], F32)
    h.s128_cols = sb("s128_cols", [128, 512], F32)
    h.masks = sb("masks", [128, 2 * 256], BF16)
    h.vb = [sb(f"vb{i}", [128, d_model], BF16) for i in range(5)]
    h.tts = [sb(f"tts{i}", [128, 256], F32) for i in range(4)]
    h.erf = [sb(f"erf{i}", [128, 256], F32) for i in range(4)]
    h.attn = [sb(f"attn{i}", [128, 256], BF16) for i in range(4)]
    h.attst = sb("attst", [128, nkc, 256], BF16)
    h.bcol = sb("bcol", [128, 1], F32)

    POUT = psa("POUT", [128, 8, 256])
    PSIM = psa("PSIM", [128, 4, 256])
    PB = psa("PB", [128, 2, 512])
    sem_cm = contextlib.ExitStack()
    if fctx is None:
        sems = {}
        for s_ in SEM_NAMES:
            sems[s_] = sem_cm.enter_context(nc.semaphore(s_))

    def program(p):
        b0 = p.tag(p.eng.memset(h.bcol[:], ERF_B) if p.on("vector") else None, "dv")
        p.tag(p.eng.dma_start(out=h.ones_row[:], in_=ones_in[0:1, :]) if p.on("gpsimd") else None,
              "boot", 16)
        p.tag(p.eng.dma_start(out=h.q[:], in_=qk_own[:]) if p.on("gpsimd") else None, "boot", 16)
        p.tag(p.eng.dma_start(out=h.kown[:], in_=qk_own[:]) if p.on("gpsimd") else None,
              "boot", 16)
        if qk_full is not None:
            p.tag(p.eng.dma_start(out=h.k[:], in_=qk_full[:]) if p.on("gpsimd") else None,
                  "boot", 16)
        else:
            ccv = None
            if p.on("gpsimd"):
                p.eng.collective_compute("AllGather", ALU.bypass,
                                         replica_groups=[[0, 1, 2, 3], [4, 5, 6, 7]],
                                         ins=[qk_cc.opt()], outs=[qkg.opt()]).then_inc(
                                             p.sems["cc"])
                ccv = p.eng.collective_compute(
                    "AllGather", ALU.bypass,
                    replica_groups=[[0, 1, 2, 3], [4, 5, 6, 7]],
                    ins=[v_cc.opt()], outs=[vg.opt()])
            p.c["cc"] += 1
            ccv2 = p.tag(ccv, "cc")
            p.wait("gpsimd", "cc", ccv2)
            for r in range(4):
                p.tag(p.eng.dma_start(out=h.k[:, r * tok:(r + 1) * tok],
                                      in_=qkg[r * 128:(r + 1) * 128, :])
                      if p.on("gpsimd") else None, "boot", 16)
        p.tag(p.eng.dma_start(out=h.selbf[0:1, 0:512], in_=sel1[:]) if p.on("gpsimd") else None,
              "boot", 16)
        p.tag(p.eng.dma_start(out=h.selbf[0:1, 512:1024], in_=sel2[:])
              if p.on("gpsimd") else None, "boot", 16)
        p.tag(p.eng.dma_start(out=h.selbf[0:1, 1024:1536], in_=sel128[:])
              if p.on("gpsimd") else None, "boot", 16)
        p.tag(p.eng.dma_start(out=h.masks[:, 0:256], in_=mask_d0[:]) if p.on("gpsimd") else None,
              "boot", 16)
        p.tag(p.eng.dma_start(out=h.masks[:, 256:512], in_=mask_m128[:])
              if p.on("gpsimd") else None, "boot", 16)
        p.tag(p.eng.dma_start(out=h.gb[:, 0:2], in_=gamma.rearrange("g p -> p g"),
                              allow_slow_non_contiguous=True)
              if p.on("sync") else None, "ld", 16)
        p.tag(p.eng.dma_start(out=h.gb[:, 2:4], in_=beta.rearrange("g p -> p g"),
                              allow_slow_non_contiguous=True)
              if p.on("sync") else None, "ld", 16)
        p.tag(p.eng.dma_start(out=h.relrow[:], in_=rel_emb.rearrange("a b -> b a"),
                              allow_slow_non_contiguous=True)
              if p.on("sync") else None, "ld", 16)
        ldx = p.tag(p.eng.dma_start(out=h.xt[:], in_=x.rearrange("(c p) t -> p c t", p=128))
                    if p.on("sync") else None, "ld", 16)
        p.barrier()
        d1 = p.tag(p.eng.tensor_scalar(h.q[:], h.q[:], h.gb[:, 0:1], h.gb[:, 2:3],
                                       ALU.mult, ALU.add) if p.on("vector") else None, "dv")
        d2 = p.tag(p.eng.tensor_scalar(h.kown[:], h.kown[:], h.gb[:, 1:2], h.gb[:, 3:4],
                                       ALU.mult, ALU.add) if p.on("vector") else None, "dv")
        d3 = p.tag(p.eng.tensor_scalar(h.k[:], h.k[:], h.gb[:, 1:2], h.gb[:, 3:4],
                                       ALU.mult, ALU.add) if p.on("vector") else None, "dv")
        dcast = p.tag(p.eng.tensor_copy(h.nb[0:1, 3 * 256:3 * 256 + 32].bitcast(BF16)[0:1, 0:32],
                                        h.relrow[:]) if p.on("vector") else None, "dv")
        p.wait("tensor", "dv", dcast)
        mmr = p.tag(p.eng.matmul(PB[:, 0, 0:32], h.ones_row[0:1, :],
                                 h.nb[0:1, 3 * 256:3 * 256 + 32].bitcast(BF16)[0:1, 0:32],
                                 start=True, stop=True) if p.on("tensor") else None, "mm")
        p.wait("scalar", "mm", mmr)
        e1 = p.tag(p.eng.activation(h.relbc[:], PB[:, 0, 0:32], AF.Copy, scale=SQ128 * float(seq))
                   if p.on("scalar") else None, "ev")
        e2 = p.tag(p.eng.activation(h.relbc2[:], PB[:, 0, 0:32], AF.Copy, scale=ERF_A * SQ128)
                   if p.on("scalar") else None, "ev")
        p.wait("vector", "ev", e1)
        for ti in range(3):
            dst = h.nb[:, ti * 256:(ti + 1) * 256]
            z0 = p.tag(p.eng.memset(dst, 0.0) if p.on("vector") else None, "dv")
            p.wait("vector", "dv", z0)
            for bkt in range(32):
                slot = bkt % 2
                p.wait("gpsimd", "dv", p.state.get(("nbm", slot), 0))
                lv = p.tag(p.eng.dma_start(out=h.nbm[slot][:], in_=nbmask[ti, bkt])
                           if p.on("gpsimd") else None, "wa" if slot == 0 else "wb", 16)
                p.wait("vector", "wa" if slot == 0 else "wb", lv)
                dacc = p.tag(p.eng.scalar_tensor_tensor(dst, h.nbm[slot][:],
                                                        h.relbc[:, bkt:bkt + 1], dst,
                                                        ALU.mult, ALU.add)
                             if p.on("vector") else None, "dv")
                p.state[("nbm", slot)] = dacc
                p.wait("vector", "dv", dacc)
        selv = h.selbf
        p.wait("tensor", "ev", e2)
        mmb = None
        if p.on("tensor"):
            p.eng.matmul(PB[:, 0, :], h.ones_row[0:1, :], selv[0:1, 0:512],
                         start=True, stop=True)
            mmb = p.eng.matmul(PB[:, 1, :], h.ones_row[0:1, :], selv[0:1, 512:1024],
                               start=True, stop=True)
        mmbv = p.tag(mmb, "mm")
        p.wait("vector", "ev", e2)
        dcol = p.tag(p.eng.tensor_scalar(h.nb[:, 3 * 256 + 64:3 * 256 + 65],
                                         h.relbc2[:, 31:32], 1.0, ERF_B, ALU.mult, ALU.add)
                     if p.on("vector") else None, "dv")
        p.wait("vector", "mm", mmbv)
        p.wait("vector", "dv", dcol)
        dbc = p.tag(p.eng.tensor_scalar(h.bias_cols[:], PB[:, 0, :],
                                        h.nb[:, 3 * 256 + 64:3 * 256 + 65], None, ALU.mult)
                    if p.on("vector") else None, "dv")
        p.wait("vector", "dv", dbc)
        dbc2 = p.tag(p.eng.scalar_tensor_tensor(h.bias_cols[:], PB[:, 1, :], NEG_BIG,
                                                h.bias_cols[:], ALU.mult, ALU.add)
                     if p.on("vector") else None, "dv")
        p.wait("tensor", "dv", dbc2)
        mm128 = p.tag(p.eng.matmul(PB[:, 0, :], h.ones_row[0:1, :], selv[0:1, 1024:1536],
                                   start=True, stop=True) if p.on("tensor") else None, "mm")
        p.wait("vector", "mm", mm128)
        d128 = p.tag(p.eng.tensor_copy(h.s128_cols[:], PB[:, 0, :])
                     if p.on("vector") else None, "dv")
        p.wait("vector", "dv", d128)
        dbc3 = p.tag(p.eng.scalar_tensor_tensor(h.bias_cols[:], h.s128_cols[:], ERF_B,
                                                h.bias_cols[:], ALU.mult, ALU.add)
                     if p.on("vector") else None, "dv")
        p.barrier()

        def sim_part(qi, kt_glob, src_k, src_v, kti, nbti, with_mask, sel128_col):
            qs = slice(qi * qt, (qi + 1) * qt)
            sidx = p.state.get("sidx", 0)
            p.state["sidx"] = (sidx + 1) % 4
            sslot = (0, 2, 1, 3)[sidx]
            vslot = p.state.get("vslot", 0)
            p.state["vslot"] = (vslot + 1) % 5
            aslot = p.state.get("aslot", 0)
            p.state["aslot"] = (aslot + 1) % 4
            vsem = ("va", "vb", "ra", "rb", "wa")[vslot]
            qeng = "sync" if vslot % 2 == 0 else "gpsimd"
            p.wait(qeng, "mm", p.state.get(("vfree", vslot), 0))
            lv = p.tag(p.eng.dma_start(out=h.vb[vslot][:], in_=src_v[kti * 128:(kti + 1) * 128, :])
                       if p.on(qeng) else None, vsem, 16)
            for s_ in (sslot, sslot ^ 1):
                skind, sval = p.state.get(("simfree", s_), ("ev", 0))
                p.wait("tensor", skind, sval)
            ms = p.tag(p.eng.matmul(PSIM[:, sslot, :], src_k[:, kti * 128:(kti + 1) * 128],
                                    h.q[:, qs], start=True, stop=True)
                       if p.on("tensor") else None, "mm")
            if nbti is not None or sel128_col is not None:
                p.wait("vector", "mm", ms)
                p.wait("vector", "ev", p.state.get(("tts_ev", aslot), 0))
                tsum = h.tts[aslot]
                if nbti is not None:
                    dn = p.tag(p.eng.tensor_add(tsum[:], PSIM[:, sslot, :],
                                                h.nb[:, nbti * 256:(nbti + 1) * 256])
                               if p.on("vector") else None, "dv")
                else:
                    dn = p.tag(p.eng.scalar_tensor_tensor(tsum[:], h.nb[:, 0:256], sel128_col,
                                                          PSIM[:, sslot, :], ALU.mult, ALU.add)
                               if p.on("vector") else None, "dv")
                p.state[("simfree", sslot)] = ("dv", dn)
                p.wait("scalar", "dv", dn)
                erf_in = tsum[:]
            else:
                p.wait("scalar", "mm", ms)
                erf_in = PSIM[:, sslot, :]
            p.wait("scalar", "dv", p.state.get(("erf_dv", aslot), 0))
            bias_ap = h.bcol[:, 0:1] if (nbti is not None) else \
                h.bias_cols[:, qi * 64 + kt_glob:qi * 64 + kt_glob + 1]
            ee = p.tag(p.eng.activation(h.erf[aslot][:], erf_in, ERF_FN,
                                        scale=ERF_A / float(seq), bias=bias_ap)
                       if p.on("scalar") else None, "ev")
            if nbti is None and sel128_col is None:
                p.state[("simfree", sslot)] = ("ev", ee)
            p.state[("tts_ev", aslot)] = ee
            p.wait("vector", "ev", ee)
            p.wait("vector", "mm", p.state.get(("attnfree", aslot), 0))
            if with_mask is not None:
                da = p.tag(p.eng.scalar_tensor_tensor(h.attn[aslot][:], h.erf[aslot][:], 1.0,
                                                      with_mask, ALU.add, ALU.mult)
                           if p.on("vector") else None, "dv")
            else:
                da = p.tag(p.eng.tensor_scalar(h.attn[aslot][:], h.erf[aslot][:], 1.0, None,
                                               ALU.add)
                           if p.on("vector") else None, "dv")
            p.state[("erf_dv", aslot)] = da
            return dict(vslot=vslot, vsem=vsem, lv=lv, aslot=aslot, da=da)

        def av_part(t, first, last):
            p.wait("tensor", "dv", t["da"])
            p.wait("tensor", t["vsem"], t["lv"])
            if first:
                p.wait("tensor", "ev", p.state.get("poutfree", 0))
            mv = None
            for vc in range(8):
                if p.on("tensor"):
                    mv = p.eng.matmul(POUT[:, vc, :],
                                      h.vb[t["vslot"]][:, vc * 128:(vc + 1) * 128],
                                      h.attn[t["aslot"]][:],
                                      start=(first and vc % 2 == 0),
                                      stop=(last and vc % 2 == 1))
            mmv = p.tag(mv, "mm")
            p.state[("vfree", t["vslot"])] = mmv
            p.state[("attnfree", t["aslot"])] = mmv
            return mmv

        LAG = 2
        for qi in range(nqi):
            KT = 2 * (24 + qi) + 2
            n_diag = 2 if qi == 0 else 3
            total_tiles = KT + n_diag
            tiles = []
            for kt in range(KT):
                s128col = h.s128_cols[:, qi * 64 + kt:qi * 64 + kt + 1] if qi == 0 else None
                tiles.append((qi, kt, h.k, v_full, kt, None, None, s128col))
            diag = []
            if qi > 0:
                diag.append((2 * qi - 1, 0, None))
            diag.append((2 * qi, 1, h.masks[:, 0:256]))
            diag.append((2 * qi + 1, 2, h.masks[:, 256:512]))
            for (ktl, nbti, msk) in diag:
                tiles.append((qi, 63, h.kown, v_own, ktl, nbti, msk, None))
            pipe = []
            done = 0
            for targs in tiles:
                pipe.append(sim_part(*targs))
                if len(pipe) > LAG:
                    av_part(pipe.pop(0), first=(done == 0), last=(done == total_tiles - 1))
                    done += 1
            while pipe:
                av_part(pipe.pop(0), first=(done == 0), last=(done == total_tiles - 1))
                done += 1
            outdone = p.c["mm"]
            for vc in range(8):
                p.wait("scalar", "mm", outdone)
                p.wait("scalar", "dv", p.state.get(("attadd", vc), 0))
                evv = p.tag(p.eng.activation(h.attst[:, vc, :], POUT[:, vc, :],
                                             AF.Copy, scale=0.5)
                            if p.on("scalar") else None, "ev")
                p.wait("vector", "ev", evv)
                p.state[("attadd", vc)] = p.tag(
                    p.eng.tensor_add(h.xt[:, vc, qi * qt:(qi + 1) * qt],
                                     h.xt[:, vc, qi * qt:(qi + 1) * qt], h.attst[:, vc, :])
                    if p.on("vector") else None, "dv")
            p.state["poutfree"] = p.c["ev"]
        p.wait("sync", "dv", p.c["dv"])
        stv = p.tag(p.eng.dma_start(out=y.rearrange("(c p) t -> p c t", p=128), in_=h.xt[:])
                    if p.on("sync") else None, "stq", 16)
        p.wait("sync", "stq", stv)

    _mk_prog(nc, sems, st, base)(program)
    sem_cm.close()
    st.close()
    return nc


def emit_mx_quant(p, scr, src, out_bf, width):
    iscr, f1, f2 = scr["iscr"], scr["f1"], scr["f2"]
    b1_, b2_, b3_ = scr["b1"], scr["b2"], scr["b3"]
    mxs, mxi = scr["mxs"], scr["mxi"]
    nb = width // 32
    v3 = src.rearrange("p (a b) -> p a b", b=32)
    dmx = p.tag(p.eng.reduce_max(mxs[:, 0:nb], v3, axis=AX.X, apply_absolute_value=True)
                if p.on("vector") else None, "dv")
    p.wait("vector", "dv", dmx)
    ds = p.tag(p.eng.tensor_scalar(mxs[:, 0:nb], mxs[:, 0:nb], 1.0 / 7.5, 1e-30,
                                   ALU.mult, ALU.max) if p.on("vector") else None, "dv")
    p.wait("vector", "dv", ds)
    di = p.tag(p.eng.reciprocal(mxi[:, 0:nb], mxs[:, 0:nb])
               if p.on("vector") else None, "dv")
    p.wait("vector", "dv", di)
    dy = p.tag(p.eng.tensor_mul(f1.rearrange("p (a b) -> p a b", b=32), v3,
                                mxi[:, 0:nb].unsqueeze(2).broadcast_to([128, nb, 32]))
               if p.on("vector") else None, "dv")
    p.wait("scalar", "dv", dy)
    ea = p.tag(p.eng.activation(b1_, f1, AF.Abs) if p.on("scalar") else None, "ev")
    p.wait("vector", "ev", ea)
    dc2 = p.tag(p.eng.tensor_scalar(b2_, b1_, 4.0, None, ALU.is_lt)
                if p.on("vector") else None, "dv")
    dc1 = p.tag(p.eng.tensor_scalar(b3_, b1_, 2.0, None, ALU.is_lt)
                if p.on("vector") else None, "dv")
    p.wait("vector", "dv", dc1)
    du = p.tag(p.eng.tensor_scalar(b1_, b2_, 2.0, 2.0, ALU.mult, ALU.add)
               if p.on("vector") else None, "dv")
    p.wait("vector", "dv", du)
    dis = p.tag(p.eng.scalar_tensor_tensor(b1_, b3_, 4.0, b1_, ALU.mult, ALU.add)
                if p.on("vector") else None, "dv")
    p.wait("vector", "dv", dis)
    dz = p.tag(p.eng.tensor_mul(f2, f1, b1_) if p.on("vector") else None, "dv")
    p.wait("scalar", "dv", dz)
    ei = p.tag(p.eng.activation(iscr, f2, AF.Copy, bias=64.5) if p.on("scalar") else None, "ev")
    p.wait("scalar", "ev", ei)
    ef = p.tag(p.eng.activation(f2, iscr, AF.Copy, bias=-64.0) if p.on("scalar") else None, "ev")
    dv1 = p.tag(p.eng.tensor_scalar(b2_, b2_, -0.25, 0.5, ALU.mult, ALU.add)
                if p.on("vector") else None, "dv")
    p.wait("vector", "dv", dv1)
    dv2 = p.tag(p.eng.scalar_tensor_tensor(b2_, b3_, -0.125, b2_, ALU.mult, ALU.add)
                if p.on("vector") else None, "dv")
    p.wait("vector", "ev", ef)
    p.wait("vector", "dv", dv2)
    dq = p.tag(p.eng.tensor_mul(f2, f2, b2_) if p.on("vector") else None, "dv")
    p.wait("vector", "dv", dq)
    dd = p.tag(p.eng.tensor_mul(out_bf.rearrange("p (a b) -> p a b", b=32),
                                f2.rearrange("p (a b) -> p a b", b=32),
                                mxs[:, 0:nb].unsqueeze(2).broadcast_to([128, nb, 32]))
               if p.on("vector") else None, "dv")
    return dd


def build_ffn1(tok=2048, d_model=1024, fctx=None, wmode=None):
    nkc = d_model // 128
    if fctx is None:
        nc = bass.Bass("TRN2", target_bir_lowering=False)
        x = nc.dram_tensor("x", [d_model, tok], F32, kind="ExternalInput").ap()
        w = nc.dram_tensor("w", [d_model, d_model], F32, kind="ExternalInput").ap()
        bias = nc.dram_tensor("bias", [d_model], F32, kind="ExternalInput").ap()
        ones_in = nc.dram_tensor("ones", [128, 128], F32, kind="ExternalInput").ap()
        idn_in = nc.dram_tensor("idn", [128, 128], F32, kind="ExternalInput").ap()
        y = nc.dram_tensor("y", [d_model, tok], F32, kind="ExternalOutput").ap()
        base = None
        wdT_dram = None
        wmode = None
    else:
        nc, sems, base = fctx["nc"], fctx["sems"], fctx["base"]
        x = fctx["x"]; w = fctx["w"]; bias = fctx["bias"]
        ones_in = fctx["ones"]; idn_in = fctx["idn"]; y = fctx["y"]
        wdT_dram = fctx.get("wdT_dram")

    st = contextlib.ExitStack()

    class H:
        pass

    h = H()
    sb = _alloc_common(nc, st, h)
    _, psa = _mk_alloc(nc, st)
    h.xt = sb("xt", [128, nkc, tok], F32)
    h.xln = sb("xln", [128, nkc, tok], BF16)
    h.idn_bf = sb("idn_bf", [128, 128], BF16)
    h.wtc = [sb(f"wtc{i}", [128, d_model], F32) for i in range(2)]
    h.wqc = [sb(f"wqc{i}", [128, d_model], BF16) for i in range(2)]
    h.wdT = sb("wdT", [128, nkc, d_model], BF16)
    h.xd = sb("xd", [128, nkc, tok], BF16)
    _xf = h.xd[:].rearrange("p a b -> p (a b)")
    h.cb = _xf[:, 0:nkc * 512].rearrange("p (c w) -> p c w", w=512)
    h.sqb = _xf[:, nkc * 512:2 * nkc * 512].rearrange("p (c w) -> p c w", w=512)
    h.xtm = [sb(f"xtm{i}", [128, d_model], BF16) for i in range(2)]
    h.xdtm = [sb(f"xdtm{i}", [128, d_model], BF16) for i in range(2)]
    h.scr = []
    for i in range(2):
        _f1 = sb(f"qf1_{i}", [128, d_model], F32)
        h.scr.append({
            "f1": _f1[:], "f2": sb(f"qf2_{i}", [128, d_model], F32)[:],
            "iscr": _f1[:].bitcast(I32),
            "b1": sb(f"qb1_{i}", [128, d_model], BF16)[:],
            "b2": sb(f"qb2_{i}", [128, d_model], BF16)[:],
            "b3": sb(f"qb3_{i}", [128, d_model], BF16)[:],
            "mxs": sb(f"mxs_{i}", [128, d_model // 32], F32)[:],
            "mxi": sb(f"mxi_{i}", [128, d_model // 32], F32)[:],
        })
    h.bih = sb("bih", [128, nkc], F32)
    h.bic = sb("bic", [128, nkc], F32)
    h.h2e = [sb(f"h2e{i}", [128, 512], BF16) for i in range(2)]
    PS0 = psa("PS0", [128, 2, 512])
    PS1 = psa("PS1", [128, 2, 512])
    PSX = psa("PSX", [128, 2, 512])
    PS2 = psa("PS2", [128, 2, 512])
    sem_cm = contextlib.ExitStack()
    if fctx is None:
        sems = {}
        for s_ in SEM_NAMES:
            sems[s_] = sem_cm.enter_context(nc.semaphore(s_))

    def program(p):
        _common_boot(p, h, ones_in, idn_in)
        p.tag(p.eng.dma_start(out=h.idn_bf[:], in_=idn_in[:]) if p.on("gpsimd") else None,
              "boot", 16)
        p.tag(p.eng.dma_start(out=h.bic[:], in_=bias.rearrange("(c p) -> p c", p=128),
                              allow_slow_non_contiguous=True)
              if p.on("sync") else None, "ld", 16)
        p.tag(p.eng.dma_start(out=h.xt[:], in_=x.rearrange("(c p) t -> p c t", p=128))
              if p.on("sync") else None, "ld", 16)
        p.barrier()
        p.tag(p.eng.activation(h.bih[:], h.bic[:], AF.Copy, scale=0.5)
              if p.on("scalar") else None, "ev")
        emit_ln(p, h, h.xt, h.xln, nkc, tok, d_model, PSX, PS2)
        ln_done = p.c["dv"]

        def tp_block(src_ap, dst_ap, gate_mm=None, gate_dv=None):
            tsl = p.state.get("tpsl", 0)
            p.state["tpsl"] = tsl ^ 1
            pslot = PS2[:, tsl, 0:128].bitcast(BF16)[:, 0:128]
            p.wait("tensor", "ev", p.state.get(("tps", tsl), 0))
            if gate_dv is not None:
                p.wait("tensor", "dv", gate_dv)
            mt1 = p.tag(p.eng.transpose(pslot, src_ap, h.idn_bf[:])
                        if p.on("tensor") else None, "mm")
            p.wait("scalar", "mm", mt1)
            if gate_mm is not None:
                p.wait("scalar", "dv", gate_mm)
            cst = p.tag(p.eng.copy(dst_ap, pslot) if p.on("scalar") else None, "ev")
            p.state[("tps", tsl)] = cst
            return cst

        if wmode == "load":
            lwc = p.tag(p.eng.dma_start(out=h.wdT[:].rearrange("p a b -> p (a b)"),
                                        in_=wdT_dram[:, :])
                        if p.on("gpsimd") else None, "boot", 16)
            p.wait("tensor", "boot", lwc)
        else:
            for c in range(nkc):
                sl = c % 2
                wsem = "va" if sl == 0 else "vb"
                p.wait("sync", "dv", p.state.get(("wtc", sl), 0))
                lw = p.tag(p.eng.dma_start(out=h.wtc[sl][:],
                                           in_=w.rearrange("(c p) e -> p c e", p=128)[:, c, :])
                           if p.on("sync") else None, wsem, 16)
                p.wait("vector", wsem, lw)
                p.wait("scalar", wsem, lw)
                p.wait("vector", "dv", p.state.get(("scr_free", sl), 0))
                p.wait("vector", "mm", p.state.get(("wqc_free", sl), 0))
                dd = emit_mx_quant(p, h.scr[sl], h.wtc[sl][:], h.wqc[sl][:], d_model)
                p.state[("scr_free", sl)] = dd
                p.state[("wtc", sl)] = dd
                first = True
                for icc in range(nkc):
                    tp_block(h.wqc[sl][:, icc * 128:(icc + 1) * 128],
                             h.wdT[:, icc, c * 128:(c + 1) * 128],
                             gate_dv=dd if first else None)
                    first = False
                p.state[("wqc_free", sl)] = p.c["mm"]
            if wmode == "save":
                p.wait("gpsimd", "ev", p.c["ev"])
                swc = p.tag(p.eng.dma_start(out=wdT_dram[:, :],
                                            in_=h.wdT[:].rearrange("p a b -> p (a b)"))
                            if p.on("gpsimd") else None, "boot", 16)
                p.wait("gpsimd", "boot", swc)
        for tc in range(tok // 128):
            sl = tc % 2
            first = True
            for kc in range(nkc):
                tp_block(h.xln[:, kc, tc * 128:(tc + 1) * 128],
                         h.xtm[sl][:, kc * 128:(kc + 1) * 128],
                         gate_mm=p.state.get(("xtm_free", sl), 0),
                         gate_dv=ln_done if first else None)
                first = False
            tin_done = p.c["ev"]
            p.wait("vector", "ev", tin_done)
            p.wait("vector", "dv", p.state.get(("scr_free", sl), 0))
            p.wait("vector", "mm", p.state.get(("xdtm_free", sl), 0))
            dd = emit_mx_quant(p, h.scr[sl], h.xtm[sl][:], h.xdtm[sl][:], d_model)
            p.state[("scr_free", sl)] = dd
            p.state[("xtm_free", sl)] = dd
            first = True
            for kc in range(nkc):
                tp_block(h.xdtm[sl][:, kc * 128:(kc + 1) * 128],
                         h.xd[:, kc, tc * 128:(tc + 1) * 128],
                         gate_dv=dd if first else None)
                first = False
            p.state[("xdtm_free", sl)] = p.c["mm"]
        p.barrier()
        for oc in range(nkc):
            for tq in range(tok // 512):
                bk = (oc * (tok // 512) + tq) % 2
                pa = PS0 if bk == 0 else PS1
                p.wait("tensor", "ev", p.state.get(("ffps", bk), 0))
                mv = None
                for icc in range(nkc):
                    if p.on("tensor"):
                        mv = p.eng.matmul(pa[:, 0, :], h.wdT[:, icc, oc * 128:(oc + 1) * 128],
                                          h.xd[:, icc, tq * 512:(tq + 1) * 512],
                                          start=(icc == 0), stop=(icc == nkc - 1))
                mmc = p.tag(mv, "mm")
                sl = tq % 2
                p.wait("scalar", "mm", mmc)
                p.wait("scalar", "dv", p.state.get(("h2", sl), 0))
                evv = p.tag(p.eng.activation(h.h2e[sl][:], pa[:, 0, :], AF.Identity,
                                             scale=0.5, bias=h.bih[:, oc:oc + 1])
                            if p.on("scalar") else None, "ev")
                p.state[("ffps", bk)] = evv
                p.wait("vector", "ev", evv)
                p.state[("h2", sl)] = p.tag(
                    p.eng.tensor_add(h.xt[:, oc, tq * 512:(tq + 1) * 512],
                                     h.xt[:, oc, tq * 512:(tq + 1) * 512], h.h2e[sl][:])
                    if p.on("vector") else None, "dv")
        p.wait("sync", "dv", p.c["dv"])
        stv = p.tag(p.eng.dma_start(out=y.rearrange("(c p) t -> p c t", p=128), in_=h.xt[:])
                    if p.on("sync") else None, "stq", 16)
        p.wait("sync", "stq", stv)

    _mk_prog(nc, sems, st, base)(program)
    sem_cm.close()
    st.close()
    return nc


def build_scanprep(tok=2048, d_model=1024, R=64, N=16, fctx=None):
    nkc = d_model // 128
    if fctx is None:
        nc = bass.Bass("TRN2", target_bir_lowering=False)
        x = nc.dram_tensor("x", [d_model, tok], F32, kind="ExternalInput").ap()
        wdtwT = nc.dram_tensor("wdtwT", [d_model, R], F32, kind="ExternalInput").ap()
        wdtT = nc.dram_tensor("wdtT", [R, d_model], F32, kind="ExternalInput").ap()
        bdt = nc.dram_tensor("bdt", [d_model], F32, kind="ExternalInput").ap()
        wBT = nc.dram_tensor("wBT", [d_model, N], F32, kind="ExternalInput").ap()
        wCT = nc.dram_tensor("wCT", [d_model, N], F32, kind="ExternalInput").ap()
        ones_in = nc.dram_tensor("ones", [128, 128], F32, kind="ExternalInput").ap()
        idn_in = nc.dram_tensor("idn", [128, 128], F32, kind="ExternalInput").ap()
        delta_o = nc.dram_tensor("delta", [d_model, tok], F32, kind="ExternalOutput").ap()
        dx_o = nc.dram_tensor("dx", [d_model, tok], F32, kind="ExternalOutput").ap()
        B_o = nc.dram_tensor("B", [N, tok], F32, kind="ExternalOutput").ap()
        C_o = nc.dram_tensor("C", [N, tok], F32, kind="ExternalOutput").ap()
        base = None
        a2a = False
    else:
        nc, sems, base = fctx["nc"], fctx["sems"], fctx["base"]
        x = fctx["x"]; wdtwT = fctx["wdtwT"]; wdtT = fctx["wdtT"]; bdt = fctx["bdt"]
        wBT = fctx["wBT"]; wCT = fctx["wCT"]; ones_in = fctx["ones"]; idn_in = fctx["idn"]
        delta_o = fctx["delta_o"]; dx_o = fctx["dx_o"]; B_o = fctx["B_o"]; C_o = fctx["C_o"]
        a2a = fctx.get("a2a", False)

    st = contextlib.ExitStack()

    class H:
        pass

    h = H()
    sb = _alloc_common(nc, st, h)
    _, psa = _mk_alloc(nc, st)
    h.xt = sb("xt", [128, nkc, tok], F32)
    h.xlnf = sb("xlnf", [128, nkc, tok], F32)
    h.cb = sb("cb", [128, nkc, 512], BF16)
    h.sqb = sb("sqb", [128, nkc, 512], BF16)
    h.wdtwb = sb("wdtwb", [128, nkc, R], BF16)
    h.wdtb = sb("wdtb", [R, nkc, 128], BF16)
    h.wBb = sb("wBb", [128, nkc, N], BF16)
    h.wCb = sb("wCb", [128, nkc, N], BF16)
    h.bdtc = sb("bdtc", [128, nkc], F32)
    h.dt = sb("dt", [R, tok], BF16)
    h.zf = sb("zf", [128, 512], F32)
    h.ab = sb("ab", [128, 512], F32)
    h.e1 = sb("e1", [128, 512], F32)
    h.r1 = sb("r1", [128, 512], F32)
    h.dst = [sb(f"dst{i}", [128, 512], F32) for i in range(2)]
    h.dxs = [sb(f"dxs{i}", [128, 512], F32) for i in range(2)]
    h.onec = sb("onec", [128, 1], F32)
    PS0 = psa("PS0", [128, 2, 512])
    PS1 = psa("PS1", [128, 2, 512])
    PSX = psa("PSX", [128, 2, 512])
    PS2 = psa("PS2", [128, 2, 512])
    sem_cm = contextlib.ExitStack()
    if fctx is None:
        sems = {}
        for s_ in SEM_NAMES:
            sems[s_] = sem_cm.enter_context(nc.semaphore(s_))

    def program(p):
        _common_boot(p, h, ones_in, idn_in)
        oc0 = p.tag(p.eng.memset(h.onec[:], 1.0) if p.on("vector") else None, "dv")
        p.tag(p.eng.dma_start(out=h.wdtwb[:], in_=wdtwT.rearrange("(c p) r -> p c r", p=128))
              if p.on("gpsimd") else None, "boot", 16)
        p.tag(p.eng.dma_start(out=h.wdtb[:], in_=wdtT.rearrange("r (c p) -> r c p", p=128),
                              allow_slow_non_contiguous=True)
              if p.on("gpsimd") else None, "boot", 16)
        p.tag(p.eng.dma_start(out=h.wBb[:], in_=wBT.rearrange("(c p) n -> p c n", p=128))
              if p.on("gpsimd") else None, "boot", 16)
        p.tag(p.eng.dma_start(out=h.wCb[:], in_=wCT.rearrange("(c p) n -> p c n", p=128))
              if p.on("gpsimd") else None, "boot", 16)
        p.tag(p.eng.dma_start(out=h.bdtc[:], in_=bdt.rearrange("(c p) -> p c", p=128),
                              allow_slow_non_contiguous=True)
              if p.on("sync") else None, "ld", 16)
        p.tag(p.eng.dma_start(out=h.xt[:], in_=x.rearrange("(c p) t -> p c t", p=128))
              if p.on("sync") else None, "ld", 16)
        p.barrier()
        emit_ln(p, h, h.xt, h.xlnf, nkc, tok, d_model, PSX, PS2)
        nt = tok // 512
        for tq in range(nt):
            p.wait("vector", "dv", p.c["dv"])
            p.wait("vector", "mm", p.c["mm"])
            dcb = p.tag(p.eng.tensor_copy(h.cb[:], h.xlnf[:, :, tq * 512:(tq + 1) * 512])
                        if p.on("vector") else None, "dv")
            p.wait("tensor", "dv", dcb)
            mv = None
            for kc in range(nkc):
                if p.on("tensor"):
                    mv = p.eng.matmul(PS0[0:R, 0, :], h.wdtwb[:, kc, :], h.cb[:, kc, :],
                                      start=(kc == 0), stop=(kc == nkc - 1))
            m1 = p.tag(mv, "mm")
            p.wait("scalar", "mm", m1)
            ev1 = p.tag(p.eng.copy(h.dt[:, tq * 512:(tq + 1) * 512], PS0[0:R, 0, :])
                        if p.on("scalar") else None, "ev")
            mv = None
            for kc in range(nkc):
                if p.on("tensor"):
                    p.eng.matmul(PS1[0:N, 0, :], h.wBb[:, kc, :], h.cb[:, kc, :],
                                 start=(kc == 0), stop=(kc == nkc - 1))
                    mv = p.eng.matmul(PS1[0:N, 1, :], h.wCb[:, kc, :], h.cb[:, kc, :],
                                      start=(kc == 0), stop=(kc == nkc - 1))
            m2 = p.tag(mv, "mm")
            p.wait("scalar", "mm", m2)
            p.wait("scalar", "va", p.state.get(("bcq", 0), 0))
            p.wait("scalar", "vb", p.state.get(("bcq", 1), 0))
            ev2 = p.tag(p.eng.copy(h.zf[0:N, :], PS1[0:N, 0, :]) if p.on("scalar") else None, "ev")
            ev3 = p.tag(p.eng.copy(h.ab[0:N, :], PS1[0:N, 1, :]) if p.on("scalar") else None, "ev")
            p.wait("sync", "ev", ev3)
            bsem = "va" if tq % 2 == 0 else "vb"
            p.tag(p.eng.dma_start(out=B_o[:, tq * 512:(tq + 1) * 512], in_=h.zf[0:N, :])
                  if p.on("sync") else None, bsem, 16)
            bq = p.tag(p.eng.dma_start(out=C_o[:, tq * 512:(tq + 1) * 512], in_=h.ab[0:N, :])
                       if p.on("sync") else None, bsem, 16)
            p.state[("bcq", tq % 2)] = bq
            p.wait("tensor", "ev", ev3)
        dtdone = p.c["ev"]
        for kc in range(nkc):
            for tq in range(nt):
                pa = PS0 if (kc * nt + tq) % 2 == 0 else PS1
                p.wait("tensor", "ev", p.c["ev"])
                mv = None
                if p.on("tensor"):
                    mv = p.eng.matmul(pa[:, 0, :], h.wdtb[:, kc, :],
                                      h.dt[:, tq * 512:(tq + 1) * 512], start=True, stop=True)
                m3 = p.tag(mv, "mm")
                p.wait("scalar", "mm", max(m3, 0))
                p.wait("scalar", "ev", dtdone)
                sl = tq % 2
                p.wait("scalar", "va", p.state.get(("bcq", 0), 0))
                p.wait("scalar", "vb", p.state.get(("bcq", 1), 0))
                p.wait("scalar", "dv", p.state.get("sp_d1", 0))
                e1 = p.tag(p.eng.activation(h.zf[:], pa[:, 0, :], AF.Identity,
                                            bias=h.bdtc[:, kc:kc + 1])
                           if p.on("scalar") else None, "ev")
                p.wait("scalar", "ev", e1)
                e2 = p.tag(p.eng.activation(h.ab[:], h.zf[:], AF.Abs)
                           if p.on("scalar") else None, "ev")
                p.wait("scalar", "ev", e2)
                e3 = p.tag(p.eng.activation(h.e1[:], h.ab[:], AF.Exp, scale=-1.0)
                           if p.on("scalar") else None, "ev")
                p.wait("scalar", "ev", e3)
                e4 = p.tag(p.eng.activation(h.e1[:], h.e1[:], AF.Ln, bias=h.onec[:, 0:1])
                           if p.on("scalar") else None, "ev")
                p.wait("scalar", "ev", e4)
                e5 = p.tag(p.eng.activation(h.r1[:], h.zf[:], AF.Relu)
                           if p.on("scalar") else None, "ev")
                p.wait("vector", "ev", e5)
                p.wait("vector", "va" if sl == 0 else "vb", p.state.get(("dq", sl), 0))
                d1 = p.tag(p.eng.tensor_add(h.dst[sl][:], h.r1[:], h.e1[:])
                           if p.on("vector") else None, "dv")
                p.state["sp_d1"] = d1
                p.wait("vector", "dv", d1)
                d2 = p.tag(p.eng.tensor_mul(h.dxs[sl][:], h.dst[sl][:],
                                            h.xlnf[:, kc, tq * 512:(tq + 1) * 512])
                           if p.on("vector") else None, "dv")
                deng = "gpsimd" if a2a else "sync"
                p.wait(deng, "dv", d2)
                dsem = "va" if sl == 0 else "vb"
                if a2a:
                    rd = 512 * (kc // 2) + 128 * (kc % 2)
                    ddst = delta_o[rd:rd + 128, tq * 512:(tq + 1) * 512]
                    xdst = dx_o[rd + 256:rd + 384, tq * 512:(tq + 1) * 512]
                else:
                    ddst = delta_o.rearrange("(c p) t -> p c t", p=128)[
                        :, kc, tq * 512:(tq + 1) * 512]
                    xdst = dx_o.rearrange("(c p) t -> p c t", p=128)[
                        :, kc, tq * 512:(tq + 1) * 512]
                p.tag(p.eng.dma_start(out=ddst, in_=h.dst[sl][:])
                      if p.on(deng) else None, dsem, 16)
                dq = p.tag(p.eng.dma_start(out=xdst, in_=h.dxs[sl][:])
                           if p.on(deng) else None, dsem, 16)
                p.state[("dq", sl)] = dq
        for s_ in ("va", "vb"):
            p.wait("sync", s_, p.c[s_])

    _mk_prog(nc, sems, st, base)(program)
    sem_cm.close()
    st.close()
    return nc


def build_scan(seq=8192, n_b=2, N=16, fctx=None):
    if fctx is None:
        nc = bass.Bass("TRN2", target_bir_lowering=False)
        delta = nc.dram_tensor("delta", [n_b, 128, seq], F32, kind="ExternalInput").ap()
        dx = nc.dram_tensor("dx", [n_b, 128, seq], F32, kind="ExternalInput").ap()
        Bf = nc.dram_tensor("Bf", [n_b, N, seq], F32, kind="ExternalInput").ap()
        Cf = nc.dram_tensor("Cf", [n_b, N, seq], F32, kind="ExternalInput").ap()
        A_pairs = nc.dram_tensor("A_pairs", [128, 16], F32, kind="ExternalInput").ap()
        seld = nc.dram_tensor("seld", [128, 16, 128], F32, kind="ExternalInput").ap()
        seln = nc.dram_tensor("seln", [N, 128], F32, kind="ExternalInput").ap()
        sel8 = nc.dram_tensor("sel8", [128, 8], F32, kind="ExternalInput").ap()
        y_o = nc.dram_tensor("y", [n_b, 128, seq], F32, kind="ExternalOutput").ap()
        base = None
        fused = False
    else:
        nc, sems, base = fctx["nc"], fctx["sems"], fctx["base"]
        A_pairs = fctx["A_pairs"]; seld = fctx["seld"]; seln = fctx["seln"]
        sel8 = fctx["sel8"]
        fused = True
        a2ain = fctx["a2ain"]; ag2 = fctx["ag2"]
        bcin = fctx["bcin"]; bcout = fctx["bcout"]
        gidx = fctx["gidx2"]
        ya = fctx["ya"]

    st = contextlib.ExitStack()
    sb, psa = _mk_alloc(nc, st)

    class H:
        pass

    h = H()
    h.delt = sb("delt", [128, seq], BF16)
    h.dxt = sb("dxt", [128, seq], BF16)
    h.Bbc = sb("Bbc", [128, seq], F32)
    h.Cbc = sb("Cbc", [128, seq], F32)
    h.Bin = sb("Bin", [N, seq], F32R)
    h.selds = sb("selds", [128, 16, 128], BF16)
    h.selns = sb("selns", [N, 128], F32R)
    h.sel8s = sb("sel8s", [128, 8], BF16)
    h.Aps = sb("Aps", [128, 32 if fused else 16], F32)
    h.Acol = sb("Acol", [128, 32 if fused else 16], F32)
    h.a_sb = [sb(f"a_sb{i}", [128, 1024], F32) for i in range(2)]
    h.u_sb = [sb(f"u_sb{i}", [128, 1024], F32) for i in range(2)]
    h.h_sb = [sb(f"h_sb{i}", [128, 1024], F32) for i in range(2)]
    h.hc_sb = [sb(f"hc_sb{i}", [128, 1024], BF16) for i in range(2)]
    h.carry = sb("carry", [128, 16], F32)
    h.ystg = sb("ystg", [128, 1024], BF16)
    h.gidxs = sb("gidxs", [128, 16], I32)
    PA = psa("PA", [128, 2, 512])
    PB = psa("PB", [128, 2, 512])
    PY = psa("PY", [128, 2, 512])
    PC = PY
    sem_cm = contextlib.ExitStack()
    if fctx is None:
        sems = {}
        for s_ in SEM_NAMES:
            sems[s_] = sem_cm.enter_context(nc.semaphore(s_))
    TQ = 1024
    nch = seq // TQ

    def program(p):
        if fused:
            c1 = None
            if p.on("gpsimd"):
                c1 = p.eng.collective_compute(
                    "AllGather", ALU.bypass,
                    replica_groups=[[0, 1, 2, 3], [4, 5, 6, 7]],
                    ins=[a2ain.opt()], outs=[ag2.opt()])
            p.tag(c1, "cc")
            c2 = None
            if p.on("gpsimd"):
                c2 = p.eng.collective_compute(
                    "AllGather", ALU.bypass,
                    replica_groups=[[0, 1, 2, 3], [4, 5, 6, 7]],
                    ins=[bcin.opt()], outs=[bcout.opt()])
            cct = p.tag(c2, "cc")
            lgi = p.tag(p.eng.dma_start(out=h.gidxs[:], in_=gidx[:])
                        if p.on("sync") else None, "ld", 16)
            p.wait("gpsimd", "cc", cct)
            p.wait("gpsimd", "ld", lgi)
        p.tag(p.eng.dma_start(out=h.selds[:], in_=seld[:]) if p.on("gpsimd") else None,
              "boot", 16)
        p.tag(p.eng.dma_start(out=h.selns[:], in_=seln[:]) if p.on("gpsimd") else None,
              "boot", 16)
        p.tag(p.eng.dma_start(out=h.sel8s[:], in_=sel8[:]) if p.on("gpsimd") else None,
              "boot", 16)
        p.tag(p.eng.dma_start(out=h.Aps[:], in_=A_pairs[:]) if p.on("sync") else None, "ld", 16)
        pz = p.tag(p.eng.memset(PY[:], 0.0) if p.on("vector") else None, "dv")
        p.wait("tensor", "dv", pz)
        p.barrier()
        ea = p.tag(p.eng.activation(h.Acol[:], h.Aps[:], AF.Exp) if p.on("scalar") else None,
                   "ev")
        p.wait("vector", "ev", ea)
        dA = p.tag(p.eng.tensor_scalar(h.Acol[:], h.Acol[:], -1.0, None, ALU.mult)
                   if p.on("vector") else None, "dv")
        p.wait("scalar", "dv", dA)

        ypend = []

        def do_y(ent):
            sl_, k2_, b_, ch_ = ent["sl"], ent["k2"], ent["b"], ent["ch"]
            p.wait("tensor", "gp", ent["d4"])
            if k2_ % 4 == 0:
                p.wait("tensor", "ev", p.state.get("py_free", 0))
            strip = (k2_ % 4) * 32
            mv = None
            for half in range(2):
                if p.on("tensor"):
                    mv = p.eng.matmul(
                        PY[strip:strip + 8, half, :], h.sel8s[:],
                        h.hc_sb[sl_][:, half * 512:(half + 1) * 512],
                        start=True, stop=True,
                        tile_position=(0, strip))
            my = p.tag(mv, "mm")
            p.state[("hc_free", sl_)] = my
            if k2_ % 4 == 3:
                g = k2_ // 4
                p.wait("scalar", "mm", my)
                p.wait("scalar", "va", p.state.get(("ystq", 0), 0))
                p.wait("scalar", "vb", p.state.get(("ystq", 1), 0))
                ey = p.tag(p.eng.activation(h.ystg[:].rearrange("p (x y) -> p x y", x=2),
                                            PY[:, 0:2, :], AF.Copy)
                           if p.on("scalar") else None, "ev")
                p.state["py_free"] = ey
                p.wait("sync", "ev", ey)
                ysem = "va" if g % 2 == 0 else "vb"
                yq = None
                if fused:
                    q_ = (ch_ * TQ) // 2048
                    cw = (ch_ * TQ) % 2048
                    for j in range(4):
                        rr = q_ * 256 + b_ * 128 + 32 * g + 8 * j
                        yq = p.tag(p.eng.dma_start(
                            out=ya[rr:rr + 8, cw:cw + TQ],
                            in_=h.ystg[32 * j:32 * j + 8, :])
                            if p.on("sync") else None, ysem, 16)
                else:
                    for j in range(4):
                        yq = p.tag(p.eng.dma_start(
                            out=y_o[b_, 32 * g + 8 * j:32 * g + 8 * j + 8, ent["csl"]],
                            in_=h.ystg[32 * j:32 * j + 8, :])
                            if p.on("sync") else None, ysem, 16)
                p.state[("ystq", g % 2)] = yq

        for b in range(n_b):
            p.barrier()
            if fused:
                lds = None
                for q_ in range(4):
                    r = b * 4 + q_
                    lds = p.tag(p.eng.indirect_dma_start(
                        h.delt[:, q_ * 2048:(q_ + 1) * 2048], None, ag2[:, :],
                        bass.IndirectOffsetOnAxis(h.gidxs[:, r:r + 1], 0))
                        if p.on("gpsimd") else None, "wa", 16)
                    lds = p.tag(p.eng.indirect_dma_start(
                        h.dxt[:, q_ * 2048:(q_ + 1) * 2048], None, ag2[:, :],
                        bass.IndirectOffsetOnAxis(h.gidxs[:, 8 + r:8 + r + 1], 0))
                        if p.on("gpsimd") else None, "wa", 16)
                p.wait("tensor", "wa", lds)
                lb = None
                for q_ in range(4):
                    lb = p.tag(p.eng.dma_start(out=h.Bin[:, q_ * 2048:(q_ + 1) * 2048],
                                               in_=bcout[32 * q_:32 * q_ + N, :])
                               if p.on("gpsimd") else None, "wa", 16)
                p.wait("tensor", "wa", lb)
            else:
                p.tag(p.eng.dma_start(out=h.delt[:], in_=delta[b]) if p.on("gpsimd") else None,
                      "wa", 16)
                p.tag(p.eng.dma_start(out=h.dxt[:], in_=dx[b]) if p.on("gpsimd") else None,
                      "wa", 16)
                lb = p.tag(p.eng.dma_start(out=h.Bin[:], in_=Bf[b]) if p.on("gpsimd") else None,
                           "wa", 16)
                p.wait("tensor", "wa", lb)
            for cc in range(seq // 512):
                mv = None
                if p.on("tensor"):
                    mv = p.eng.matmul(PC[:, cc % 2, :], h.selns[:],
                                      h.Bin[:, cc * 512:(cc + 1) * 512], start=True, stop=True)
                m1 = p.tag(mv, "mm")
                p.wait("scalar", "mm", m1)
                p.tag(p.eng.copy(h.Bbc[:, cc * 512:(cc + 1) * 512], PC[:, cc % 2, :])
                      if p.on("scalar") else None, "ev")
                p.wait("tensor", "ev", p.c["ev"])
            p.wait("gpsimd", "mm", p.c["mm"])
            lc = None
            if fused:
                for q_ in range(4):
                    lc = p.tag(p.eng.dma_start(
                        out=h.Bin[:, q_ * 2048:(q_ + 1) * 2048],
                        in_=bcout[32 * q_ + N:32 * q_ + 2 * N, :])
                        if p.on("gpsimd") else None, "wb", 16)
            else:
                lc = p.tag(p.eng.dma_start(out=h.Bin[:], in_=Cf[b]) if p.on("gpsimd") else None,
                           "wb", 16)
            p.wait("tensor", "wb", lc)
            for cc in range(seq // 512):
                mv = None
                if p.on("tensor"):
                    mv = p.eng.matmul(PC[:, cc % 2, :], h.selns[:],
                                      h.Bin[:, cc * 512:(cc + 1) * 512], start=True, stop=True)
                m1 = p.tag(mv, "mm")
                p.wait("scalar", "mm", m1)
                p.tag(p.eng.copy(h.Cbc[:, cc * 512:(cc + 1) * 512], PC[:, cc % 2, :])
                      if p.on("scalar") else None, "ev")
                p.wait("tensor", "ev", p.c["ev"])
            p.barrier()
            for ch in range(nch):
                csl = slice(ch * TQ, (ch + 1) * TQ)
                for k2 in range(16):
                    it = p.state.get("scan_it", 0)
                    p.state["scan_it"] = it + 1
                    sl = it % 2
                    a_sb, u_sb = h.a_sb[sl], h.u_sb[sl]
                    h_sb, hc_sb = h.h_sb[sl], h.hc_sb[sl]
                    mh = [0, 0]
                    for half in range(2):
                        p.wait("tensor", "ev", p.state.get(("paf", half), 0))
                        p.wait("tensor", "dv", p.state.get(("pbf", half), 0))
                        mv = None
                        if p.on("tensor"):
                            p.eng.matmul(PA[:, half, :], h.selds[:, k2, :],
                                         h.delt[:, ch * TQ + half * 512:ch * TQ + (half + 1) * 512],
                                         start=True, stop=True)
                            mv = p.eng.matmul(PB[:, half, :], h.selds[:, k2, :],
                                              h.dxt[:, ch * TQ + half * 512:ch * TQ + (half + 1) * 512],
                                              start=True, stop=True)
                        mh[half] = p.tag(mv, "mm")
                    e1 = 0
                    for half in range(2):
                        p.wait("scalar", "mm", mh[half])
                        if half == 0:
                            p.wait("scalar", "dv", p.state.get(("a_free", sl), 0))
                        ak = b * 16 + k2 if fused else k2
                        e1 = p.tag(p.eng.activation(a_sb[:, half * 512:(half + 1) * 512],
                                                    PA[:, half, :], AF.Exp,
                                                    scale=h.Acol[:, ak:ak + 1])
                                   if p.on("scalar") else None, "ev")
                        p.state[("paf", half)] = e1
                    d1 = 0
                    for half in range(2):
                        p.wait("vector", "mm", mh[half])
                        if half == 0:
                            p.wait("vector", "dv", p.state.get(("u_free", sl), 0))
                        d1 = p.tag(p.eng.tensor_mul(u_sb[:, half * 512:(half + 1) * 512],
                                                    PB[:, half, :],
                                                    h.Bbc[:, ch * TQ + half * 512:ch * TQ + (half + 1) * 512])
                                   if p.on("vector") else None, "dv")
                        p.state[("pbf", half)] = d1
                    p.wait("vector", "ev", e1)
                    p.wait("vector", "dv", d1)
                    p.wait("vector", "gp", p.state.get(("h_free", sl), 0))
                    init = h.carry[:, k2:k2 + 1] if ch > 0 else 0.0
                    if ch > 0:
                        p.wait("vector", "gp", p.state.get(("carry_gp", k2), 0))
                    d2 = p.tag(p.eng.tensor_tensor_scan(h_sb[:], a_sb[:], u_sb[:],
                                                        init, ALU.mult, ALU.add)
                               if p.on("vector") else None, "dv")
                    p.state[("a_free", sl)] = d2
                    p.state[("u_free", sl)] = d2
                    p.wait("gpsimd", "dv", d2)
                    d3 = p.tag(p.eng.tensor_copy(h.carry[:, k2:k2 + 1], h_sb[:, TQ - 1:TQ])
                               if p.on("gpsimd") else None, "gp")
                    p.state[("carry_gp", k2)] = d3
                    p.wait("gpsimd", "mm", p.state.get(("hc_free", sl), 0))
                    d4 = p.tag(p.eng.tensor_mul(hc_sb[:], h_sb[:], h.Cbc[:, csl])
                               if p.on("gpsimd") else None, "gp")
                    p.state[("h_free", sl)] = d4
                    ypend.append(dict(sl=sl, k2=k2, b=b, ch=ch, csl=csl, d4=d4))
                    if len(ypend) > 1:
                        do_y(ypend.pop(0))
            while ypend:
                do_y(ypend.pop(0))
        for s_ in ("va", "vb"):
            p.wait("sync", s_, p.c[s_])

    _mk_prog(nc, sems, st, base)(program)
    sem_cm.close()
    st.close()
    return nc


def build_l1post(tok=2048, d_model=1024, fctx=None):
    nkc = d_model // 128
    if fctx is None:
        nc = bass.Bass("TRN2", target_bir_lowering=False)
        x = nc.dram_tensor("x", [d_model, tok], F32, kind="ExternalInput").ap()
        yin = nc.dram_tensor("yin", [d_model, tok], F32, kind="ExternalInput").ap()
        Dp = nc.dram_tensor("Dp", [d_model], F32, kind="ExternalInput").ap()
        lng = nc.dram_tensor("lng", [d_model], F32, kind="ExternalInput").ap()
        lnb = nc.dram_tensor("lnb", [d_model], F32, kind="ExternalInput").ap()
        woutT = nc.dram_tensor("woutT", [d_model, d_model], F32, kind="ExternalInput").ap()
        ones_in = nc.dram_tensor("ones", [128, 128], F32, kind="ExternalInput").ap()
        idn_in = nc.dram_tensor("idn", [128, 128], F32, kind="ExternalInput").ap()
        y = nc.dram_tensor("y", [d_model, tok], F32, kind="ExternalOutput").ap()
        base = None
        fused = False
    else:
        nc, sems, base = fctx["nc"], fctx["sems"], fctx["base"]
        x = fctx["x"]; y = fctx["x"]; yin = None
        Dp = fctx["Dp"]; lng = fctx["lng"]; lnb = fctx["lnb"]; woutT = fctx["woutT"]
        ones_in = fctx["ones"]; idn_in = fctx["idn"]
        ya_cc = fctx["ya"]; agy = fctx["agy"]; gidx = fctx["gidx"]
        fused = True

    st = contextlib.ExitStack()

    class H:
        pass

    h = H()
    sb = _alloc_common(nc, st, h, idn=False)
    _, psa = _mk_alloc(nc, st)
    h.xt = sb("xt", [128, nkc, tok], F32)
    h.yt = sb("yt", [128, nkc, tok], BF16)
    h.xlnb = sb("xlnb", [128, nkc, tok], BF16)
    h.y2ln = sb("y2ln", [128, nkc, tok], BF16)
    h.wob = [sb(f"wob{i}", [128, nkc, 128], BF16) for i in range(2)]
    h.dpc = sb("dpc", [128, nkc], F32)
    h.gidxs = sb("gidxs", [128, 8], I32)
    h.gc = sb("gc", [128, nkc], F32)
    h.bc = sb("bc", [128, nkc], F32)
    PS0 = psa("PS0", [128, 2, 512])
    PS1 = psa("PS1", [128, 2, 512])
    PSX = psa("PSX", [128, 2, 512])
    PS2 = psa("PS2", [128, 2, 512])
    sem_cm = contextlib.ExitStack()
    if fctx is None:
        sems = {}
        for s_ in SEM_NAMES:
            sems[s_] = sem_cm.enter_context(nc.semaphore(s_))

    def program(p):
        _common_boot(p, h, ones_in, idn_in)
        p.tag(p.eng.dma_start(out=h.dpc[:], in_=Dp.rearrange("(c p) -> p c", p=128),
                              allow_slow_non_contiguous=True)
              if p.on("sync") else None, "ld", 16)
        p.tag(p.eng.dma_start(out=h.gc[:], in_=lng.rearrange("(c p) -> p c", p=128),
                              allow_slow_non_contiguous=True)
              if p.on("sync") else None, "ld", 16)
        p.tag(p.eng.dma_start(out=h.bc[:], in_=lnb.rearrange("(c p) -> p c", p=128),
                              allow_slow_non_contiguous=True)
              if p.on("sync") else None, "ld", 16)
        p.tag(p.eng.dma_start(out=h.xt[:], in_=x.rearrange("(c p) t -> p c t", p=128))
              if p.on("sync") else None, "ld", 16)
        if not fused:
            p.tag(p.eng.dma_start(out=h.yt[:], in_=yin.rearrange("(c p) t -> p c t", p=128))
                  if p.on("sync") else None, "ld", 16)
        else:
            ccl = None
            if p.on("gpsimd"):
                ccl = p.eng.collective_compute(
                    "AllGather", ALU.bypass,
                    replica_groups=[[0, 1, 2, 3], [4, 5, 6, 7]],
                    ins=[ya_cc.opt()], outs=[agy.opt()])
            cct = p.tag(ccl, "cc")
            lgi = p.tag(p.eng.dma_start(out=h.gidxs[:], in_=gidx[:])
                        if p.on("sync") else None, "ld", 16)
            p.wait("sync", "ld", lgi)
            p.wait("gpsimd", "cc", cct)
            p.wait("gpsimd", "ld", lgi)
            gv = None
            for j in range(8):
                gv = p.tag(p.eng.indirect_dma_start(
                    h.yt[:, j, :], None, agy[:, :],
                    bass.IndirectOffsetOnAxis(h.gidxs[:, j:j + 1], 0))
                    if p.on("gpsimd") else None, "wa", 16)
        p.barrier()
        _yf = h.y2ln[:].rearrange("p a b -> p (a b)")
        h.cb = _yf[:, 0:nkc * 512].rearrange("p (c w) -> p c w", w=512)
        h.sqb = _yf[:, nkc * 512:2 * nkc * 512].rearrange("p (c w) -> p c w", w=512)
        emit_ln(p, h, h.xt, h.xlnb, nkc, tok, d_model, PSX, PS2)
        for kc in range(nkc):
            p.wait("vector", "dv", p.c["dv"])
            dd = p.tag(p.eng.scalar_tensor_tensor(h.yt[:, kc, :], h.xlnb[:, kc, :],
                                                  h.dpc[:, kc:kc + 1], h.yt[:, kc, :],
                                                  ALU.mult, ALU.add)
                       if p.on("vector") else None, "dv")
        p.barrier()
        _xf2 = h.xlnb[:].rearrange("p a b -> p (a b)")
        h.cb = _xf2[:, 0:nkc * 512].rearrange("p (c w) -> p c w", w=512)
        h.sqb = _xf2[:, nkc * 512:2 * nkc * 512].rearrange("p (c w) -> p c w", w=512)
        emit_ln(p, h, h.yt, h.y2ln, nkc, tok, d_model, PSX, PS2, gb=(h.gc, h.bc))
        nt = tok // 512
        for oc in range(nkc):
            buf = oc % 2
            p.wait("gpsimd", "mm", p.state.get(("wo", buf), 0))
            wsem = "wa" if buf == 0 else "wb"
            wv_ = p.tag(p.eng.dma_start(
                out=h.wob[buf][:],
                in_=woutT.rearrange("(c p) e -> p c e", p=128)[:, :, oc * 128:(oc + 1) * 128])
                if p.on("gpsimd") else None, wsem, 16)
            for tq in range(nt):
                bk = (oc * nt + tq) % 2
                pa = PS0 if bk == 0 else PS1
                p.wait("tensor", wsem, wv_)
                if oc == 0 and tq == 0:
                    p.wait("tensor", "dv", p.c["dv"])
                else:
                    p.wait("tensor", "dv", p.state.get(("ffps", bk), 0))
                mv = None
                for kc in range(nkc):
                    if p.on("tensor"):
                        mv = p.eng.matmul(pa[:, 0, :], h.wob[buf][:, kc, :],
                                          h.y2ln[:, kc, tq * 512:(tq + 1) * 512],
                                          start=(kc == 0), stop=(kc == nkc - 1))
                mmc = p.tag(mv, "mm")
                p.wait("vector", "mm", mmc)
                dvt = p.tag(p.eng.tensor_add(h.xt[:, oc, tq * 512:(tq + 1) * 512],
                                             h.xt[:, oc, tq * 512:(tq + 1) * 512],
                                             pa[:, 0, :])
                            if p.on("vector") else None, "dv")
                p.state[("ffps", bk)] = dvt
            p.state[("wo", buf)] = p.c["mm"]
        p.wait("sync", "dv", p.c["dv"])
        stv = p.tag(p.eng.dma_start(out=y.rearrange("(c p) t -> p c t", p=128), in_=h.xt[:])
                    if p.on("sync") else None, "stq", 16)
        p.wait("sync", "stq", stv)

    _mk_prog(nc, sems, st, base)(program)
    sem_cm.close()
    st.close()
    return nc


def _bucket_host(n):
    import math as _m
    if n < 16:
        return int(n)
    v = 16 + int(np.float32(np.log(np.float32(max(n, 1)) / 16) / _m.log(128 / 16) * 16))
    return min(v, 31)


def _attn_consts():
    f32 = np.float32
    mask_d0 = np.zeros((128, 256), BF16NP)
    mask_m128 = np.zeros((128, 256), BF16NP)
    for p_ in range(128):
        mask_d0[p_, p_:] = 1
        if p_ + 128 < 256:
            mask_m128[p_, p_ + 128:] = 1
    nbm = np.zeros((3, 32, 128, 256), BF16NP)
    for ti, d in enumerate((128, 0, -128)):
        for p_ in range(128):
            for f_ in range(256):
                dist = d + f_ - p_
                if dist >= 0:
                    nbm[ti, _bucket_host(dist), p_, f_] = 1
    return mask_d0, mask_m128, nbm


def _attn_sels(q):
    sel1 = np.zeros((1, 512), np.float32)
    sel2 = np.zeros((1, 512), np.float32)
    sel128 = np.zeros((1, 512), np.float32)
    for qi in range(8):
        KT = 2 * (24 + qi) + 2
        kd128 = 16 * q + 2 * qi - 1
        kd0 = 16 * q + 2 * qi
        kdm = 16 * q + 2 * qi + 1
        for kt in range(64):
            j = qi * 64 + kt
            if kt >= KT:
                sel2[0, j] = 1.0
                continue
            skip = (kt > kdm) or (kt in (kd0, kdm)) or (qi >= 1 and kt == kd128)
            if skip:
                sel2[0, j] = 1.0
            else:
                if qi == 0 and q > 0 and kt == 16 * q - 1:
                    sel128[0, j] = 1.0
                else:
                    sel1[0, j] = 1.0
    return sel1, sel2, sel128


_PROG_CACHE = {}


def _prog(name, builder, *args):
    key = (name,) + args
    if key not in _PROG_CACHE:
        _PROG_CACHE[key] = builder(*args)
    return _PROG_CACHE[key]


def kernel_multi(**inputs):
    f32 = np.float32
    g = {k: np.ascontiguousarray(np.asarray(v, f32)) for k, v in inputs.items()}
    x0 = g["L0_attn_x"]
    ones = np.ones((128, 128), f32)
    idn = np.eye(128, dtype=f32)
    cores = list(range(N_CORES))
    TOK = 2048

    def fm(c, arr):
        b, q = c // 4, c % 4
        return np.ascontiguousarray(arr[b, q * TOK:(q + 1) * TOK, :].T)

    x_fm = [fm(c, x0) for c in cores]

    cfg = {"D": 1024, "H": 4096, "E": 8, "T": TOK, "MT": 1024}
    nc_ffn0 = _prog("ffn0", build_ffn0, tuple(sorted(cfg.items()))) if False else None
    if "ffn0" not in _PROG_CACHE:
        _PROG_CACHE["ffn0"] = build_ffn0(cfg)
    nc_ffn0 = _PROG_CACHE["ffn0"]

    def run_ffn0(xs):
        base = {"rw1": g["L0_ffn_rw1"], "rb1": g["L0_ffn_rb1"], "rw2": g["L0_ffn_rw2"],
                "rb2": g["L0_ffn_rb2"], "w1": g["L0_ffn_w1"], "b1": g["L0_ffn_b1"],
                "w2": g["L0_ffn_w2"], "b2": g["L0_ffn_b2"], "ones": ones, "idn": idn}
        res = run_bass_kernel_spmd(nc_ffn0, [dict(base, x=xs[c]) for c in cores],
                                   core_ids=cores)
        return [res.results[c]["y"] for c in cores]

    x1 = run_ffn0(x_fm)

    nc_kv = _PROG_CACHE.setdefault("kv", build_kv())
    base = {"wqk": g["L0_attn_Wqk"], "bqk": g["L0_attn_bqk"], "wv": g["L0_attn_Wv"],
            "bv": g["L0_attn_bv"], "ones": ones, "idn": idn}
    res = run_bass_kernel_spmd(nc_kv, [dict(base, x=x1[c]) for c in cores], core_ids=cores)
    qk = [res.results[c]["qk"] for c in cores]
    vtm = [res.results[c]["v"] for c in cores]

    nc_attn = _PROG_CACHE.setdefault("attn", build_attn())
    mask_d0, mask_m128, nbm = _attn_consts()
    qk_full = [np.ascontiguousarray(np.concatenate([qk[b * 4 + q] for q in range(4)], 1))
               for b in range(2)]
    v_full = [np.ascontiguousarray(np.concatenate([vtm[b * 4 + q] for q in range(4)], 0))
              for b in range(2)]
    in_maps = []
    for c in cores:
        b, q = c // 4, c % 4
        s1, s2, s128 = _attn_sels(q)
        in_maps.append({"x": x1[c], "qk_own": qk[c], "qk_full": qk_full[b],
                        "v_full": v_full[b], "v_own": vtm[c],
                        "gamma": g["L0_attn_gamma"], "beta": g["L0_attn_beta"],
                        "rel_emb": g["L0_attn_rel_emb"], "sel1": s1, "sel2": s2,
                        "sel128": s128, "nbmask": nbm, "mask_d0": mask_d0,
                        "mask_m128": mask_m128, "ones": ones})
    res = run_bass_kernel_spmd(nc_attn, in_maps, core_ids=cores)
    x2 = [res.results[c]["y"] for c in cores]

    x3 = run_ffn0(x2)

    nc_ffn1 = _PROG_CACHE.setdefault("ffn1", build_ffn1())

    def run_ffn1(xs):
        base = {"w": g["L1_ffn_weight"], "bias": g["L1_ffn_bias"], "ones": ones, "idn": idn}
        res = run_bass_kernel_spmd(nc_ffn1, [dict(base, x=xs[c]) for c in cores],
                                   core_ids=cores)
        return [res.results[c]["y"] for c in cores]

    x4 = run_ffn1(x3)

    nc_sp = _PROG_CACHE.setdefault("scanprep", build_scanprep())
    base = {"wdtwT": np.ascontiguousarray(g["L1_attn_W_dtw"].T),
            "wdtT": np.ascontiguousarray(g["L1_attn_W_dt"].T),
            "bdt": g["L1_attn_b_dt"],
            "wBT": np.ascontiguousarray(g["L1_attn_W_B"].T),
            "wCT": np.ascontiguousarray(g["L1_attn_W_C"].T),
            "ones": ones, "idn": idn}
    res = run_bass_kernel_spmd(nc_sp, [dict(base, x=x4[c]) for c in cores], core_ids=cores)
    delta = [res.results[c]["delta"] for c in cores]
    dxs = [res.results[c]["dx"] for c in cores]
    Bs = [res.results[c]["B"] for c in cores]
    Cs = [res.results[c]["C"] for c in cores]

    nc_scan = _PROG_CACHE.setdefault("scan", build_scan())
    seld = np.zeros((128, 16, 128), f32)
    for dd in range(128):
        for p_ in range(128):
            if dd == (p_ // 16) + 0:
                pass
    for k2 in range(16):
        for p_ in range(128):
            seld[k2 * 8 + p_ // 16, k2, p_] = 1.0
    seln = np.zeros((16, 128), f32)
    for p_ in range(128):
        seln[p_ % 16, p_] = 1.0
    sel8 = np.zeros((128, 8), f32)
    for p_ in range(128):
        sel8[p_, p_ // 16] = 1.0
    Bfull = np.stack([np.concatenate([Bs[b * 4 + q] for q in range(4)], 1) for b in range(2)])
    Cfull = np.stack([np.concatenate([Cs[b * 4 + q] for q in range(4)], 1) for b in range(2)])
    A_log = g["L1_attn_A_log"]
    in_maps = []
    for j in cores:
        dj = np.stack([np.concatenate([delta[b * 4 + q][128 * j:128 * (j + 1)]
                                       for q in range(4)], 1) for b in range(2)])
        dxj = np.stack([np.concatenate([dxs[b * 4 + q][128 * j:128 * (j + 1)]
                                        for q in range(4)], 1) for b in range(2)])
        Ap = np.empty((128, 16), f32)
        for k2 in range(16):
            for p_ in range(128):
                Ap[p_, k2] = A_log[128 * j + 8 * k2 + p_ // 16, p_ % 16]
        in_maps.append({"delta": np.ascontiguousarray(dj), "dx": np.ascontiguousarray(dxj),
                        "Bf": Bfull, "Cf": Cfull, "A_pairs": Ap, "seld": seld,
                        "seln": seln, "sel8": sel8})
    res = run_bass_kernel_spmd(nc_scan, in_maps, core_ids=cores)
    yj = [res.results[j]["y"] for j in cores]
    y_fm = []
    for c in cores:
        b, q = c // 4, c % 4
        yc = np.concatenate([yj[j][b, :, q * TOK:(q + 1) * TOK] for j in cores], 0)
        y_fm.append(np.ascontiguousarray(yc))

    nc_post = _PROG_CACHE.setdefault("l1post", build_l1post())
    base = {"Dp": g["L1_attn_Dp"], "lng": g["L1_attn_ln_g"], "lnb": g["L1_attn_ln_b"],
            "woutT": np.ascontiguousarray(g["L1_attn_W_out"].T), "ones": ones, "idn": idn}
    res = run_bass_kernel_spmd(nc_post, [dict(base, x=x4[c], yin=y_fm[c]) for c in cores],
                               core_ids=cores)
    x5 = [res.results[c]["y"] for c in cores]

    x6 = run_ffn1(x5)

    out = np.empty((2, 8192, 1024), f32)
    for c in cores:
        b, q = c // 4, c % 4
        out[b, q * TOK:(q + 1) * TOK, :] = x6[c].T
    return out


def build_fused(cfg):
    d_model, tok = cfg["D"], cfg["T"]
    n_e, hid = cfg["E"], cfg["H"]
    nc = bass.Bass("TRN2", target_bir_lowering=False)
    EI = lambda n, sh, dt=F32: nc.dram_tensor(n, sh, dt, kind="ExternalInput").ap()
    IN = lambda n, sh, dt=F32: nc.dram_tensor(n, sh, dt, kind="Internal").ap()
    x_in = EI("x", [d_model, tok])
    out = nc.dram_tensor("out", [d_model, tok], F32, kind="ExternalOutput").ap()
    ones = EI("ones", [128, 128])
    idn = EI("idn", [128, 128])
    io = {
        "rw1": EI("rw1", [d_model, hid]), "rb1": EI("rb1", [hid]),
        "rw2": EI("rw2", [hid, n_e]), "rb2": EI("rb2", [n_e]),
        "w1": EI("w1", [n_e, d_model, hid]), "b1": EI("b1", [n_e, hid]),
        "w2": EI("w2", [n_e, hid, d_model]), "b2": EI("b2", [n_e, d_model]),
        "wqk": EI("wqk", [d_model, 128]), "bqk": EI("bqk", [128]),
        "wv": EI("wv", [d_model, d_model]), "bv": EI("bv", [d_model]),
        "gamma": EI("gamma", [2, 128]), "beta": EI("beta", [2, 128]),
        "rel_emb": EI("rel_emb", [32, 1]),
        "sel1": EI("sel1", [1, 512]), "sel2": EI("sel2", [1, 512]),
        "sel128": EI("sel128", [1, 512]),
        "nbmask": EI("nbmask", [3, 32, 128, 256], BF16),
        "mask_d0": EI("mask_d0", [128, 256], BF16),
        "mask_m128": EI("mask_m128", [128, 256], BF16),
        "w": EI("w", [d_model, d_model]), "bias": EI("bias", [d_model]),
        "wdtwT": EI("wdtwT", [d_model, 64]), "wdtT": EI("wdtT", [64, d_model]),
        "bdt": EI("bdt", [d_model]),
        "wBT": EI("wBT", [d_model, 16]), "wCT": EI("wCT", [d_model, 16]),
        "A_pairs": EI("A_pairs", [128, 32]),
        "seld": EI("seld", [128, 16, 128]), "seln": EI("seln", [16, 128]),
        "sel8": EI("sel8", [128, 8]), "gidx": EI("gidx", [128, 8], I32),
        "gidx2": EI("gidx2", [128, 16], I32),
        "Dp": EI("Dp", [d_model]), "lng": EI("lng", [d_model]),
        "lnb": EI("lnb", [d_model]), "woutT": EI("woutT", [d_model, d_model]),
        "ones": ones, "idn": idn,
    }
    xs = IN("xs", [d_model, tok])
    qk = IN("qk_i", [128, tok])
    v = IN("v_i", [tok, d_model], BF16)
    qkg = IN("qkg", [4 * 128, tok])
    vg = IN("vg", [4 * tok, d_model], BF16)
    a2ain = IN("a2ain", [2048, tok], BF16)
    ag2 = IN("ag2", [4 * 2048, tok], BF16)
    bcin = IN("bcin", [32, tok])
    bcout = IN("bcout", [128, tok])
    ya = IN("ya", [8 * 128, tok], BF16)
    agy = IN("agy", [4 * 1024, tok], BF16)
    wdTs = IN("wdTs", [128, 8 * 1024], BF16)

    sem_cm = contextlib.ExitStack()
    sems = {s_: sem_cm.enter_context(nc.semaphore(s_)) for s_ in SEM_NAMES}
    base = {s_: 0 for s_ in SEM_NAMES}
    fc = dict(io)
    fc.update(nc=nc, sems=sems, base=base)

    build_ffn0(cfg, fctx=dict(fc, x=x_in, y=xs))
    build_kv(tok, d_model, fctx=dict(fc, x=xs, qk_o=qk, v_o=v))
    build_attn(tok, d_model, fctx=dict(fc, x=xs, qk_own=qk, v_own=v, qkg=qkg, vg=vg))
    build_ffn0(cfg, fctx=dict(fc, x=xs, y=xs))
    build_ffn1(tok, d_model, fctx=dict(fc, x=xs, y=xs, wdT_dram=wdTs), wmode="save")
    build_scanprep(tok, d_model, fctx=dict(fc, x=xs, delta_o=a2ain, dx_o=a2ain,
                                           B_o=bcin[0:16, :], C_o=bcin[16:32, :],
                                           a2a=True))
    build_scan(fctx=dict(fc, a2ain=a2ain, ag2=ag2, bcin=bcin,
                         bcout=bcout, ya=ya))
    build_l1post(tok, d_model, fctx=dict(fc, x=xs, ya=ya, agy=agy))
    build_ffn1(tok, d_model, fctx=dict(fc, x=xs, y=out, wdT_dram=wdTs), wmode="load")
    sem_cm.close()
    return nc


def kernel(**inputs):
    f32 = np.float32
    g = {k: np.ascontiguousarray(np.asarray(v, f32)) for k, v in inputs.items()}
    x0 = g["L0_attn_x"]
    ones = np.ones((128, 128), f32)
    idn = np.eye(128, dtype=f32)
    cores = list(range(N_CORES))
    TOK = 2048
    cfg = {"D": 1024, "H": 4096, "E": 8, "T": TOK, "MT": 1024}
    if "fused" not in _PROG_CACHE:
        _PROG_CACHE["fused"] = build_fused(cfg)
    nc = _PROG_CACHE["fused"]
    mask_d0, mask_m128, nbm = _attn_consts()
    seld = np.zeros((128, 16, 128), f32)
    for k2 in range(16):
        for p_ in range(128):
            seld[k2 * 8 + p_ // 16, k2, p_] = 1.0
    seln = np.zeros((16, 128), f32)
    sel8 = np.zeros((128, 8), f32)
    for p_ in range(128):
        seln[p_ % 16, p_] = 1.0
        sel8[p_, p_ // 16] = 1.0
    A_log = g["L1_attn_A_log"]
    base = {
        "rw1": g["L0_ffn_rw1"], "rb1": g["L0_ffn_rb1"], "rw2": g["L0_ffn_rw2"],
        "rb2": g["L0_ffn_rb2"], "w1": g["L0_ffn_w1"], "b1": g["L0_ffn_b1"],
        "w2": g["L0_ffn_w2"], "b2": g["L0_ffn_b2"],
        "wqk": g["L0_attn_Wqk"], "bqk": g["L0_attn_bqk"], "wv": g["L0_attn_Wv"],
        "bv": g["L0_attn_bv"], "gamma": g["L0_attn_gamma"], "beta": g["L0_attn_beta"],
        "rel_emb": g["L0_attn_rel_emb"],
        "nbmask": nbm, "mask_d0": mask_d0, "mask_m128": mask_m128,
        "w": g["L1_ffn_weight"], "bias": g["L1_ffn_bias"],
        "wdtwT": np.ascontiguousarray(g["L1_attn_W_dtw"].T),
        "wdtT": np.ascontiguousarray(g["L1_attn_W_dt"].T),
        "bdt": g["L1_attn_b_dt"],
        "wBT": np.ascontiguousarray(g["L1_attn_W_B"].T),
        "wCT": np.ascontiguousarray(g["L1_attn_W_C"].T),
        "seld": seld, "seln": seln, "sel8": sel8,
        "Dp": g["L1_attn_Dp"], "lng": g["L1_attn_ln_g"], "lnb": g["L1_attn_ln_b"],
        "woutT": np.ascontiguousarray(g["L1_attn_W_out"].T),
        "ones": ones, "idn": idn,
    }
    in_maps = []
    for c in cores:
        b, q = c // 4, c % 4
        s1, s2, s128 = _attn_sels(q)
        Ap = np.empty((128, 32), f32)
        for dblk in range(2):
            dwin = 256 * q + 128 * dblk
            for k2 in range(16):
                for p_ in range(128):
                    Ap[p_, dblk * 16 + k2] = A_log[dwin + 8 * k2 + p_ // 16, p_ % 16]
        dd_ = np.arange(1024, dtype=np.int32)
        rows = (dd_ // 256) * 1024 + q * 256 + (dd_ % 256)
        gidx = rows.reshape(8, 128).T.astype(np.int32)
        g2a = np.empty((128, 8), np.int32)
        for dblk in range(2):
            for q_ in range(4):
                g2a[:, dblk * 4 + q_] = (2048 * q_ + 512 * q + 128 * dblk
                                         + np.arange(128))
        gidx2 = np.concatenate([g2a, g2a + 256], axis=1).astype(np.int32)
        in_maps.append(dict(base, x=np.ascontiguousarray(x0[b, q * TOK:(q + 1) * TOK, :].T),
                            sel1=s1, sel2=s2, sel128=s128, A_pairs=Ap, gidx=gidx,
                            gidx2=gidx2))
    res = run_bass_kernel_spmd(nc, in_maps, core_ids=cores)
    outp = np.empty((2, 8192, 1024), f32)
    for c in cores:
        b, q = c // 4, c % 4
        outp[b, q * TOK:(q + 1) * TOK, :] = res.results[c]["out"].T
    return outp
